# version 05
# baseline (speedup 1.0000x reference)
import numpy as np
import ml_dtypes
import concourse.bass as bass
import concourse.mybir as mybir
from concourse.bass_utils import run_bass_kernel_spmd
from contextlib import ExitStack

F32 = mybir.dt.float32
BF16 = mybir.dt.bfloat16
AF = mybir.ActivationFunctionType
ALU = mybir.AluOpType
AX = mybir.AxisListType

NCORE = 8
TOK = 8192
NCH = 64
DD = 1024
DFF = 2048
NE = 8
NSLICE = 16
LAM = 31.5

BUILD_PHASES = 3
DEBUG = 0
NCH_L = NCH
SLICES = None


def gegen_monomials():
    R = [np.zeros(9, np.float64) for _ in range(9)]
    R[0][0] = 1.0
    R[1][1] = 1.0
    for k in range(2, 9):
        c1 = 2.0 * (k + LAM - 1.0) / (k + 2.0 * LAM - 1.0)
        c2 = (k - 1.0) / (k + 2.0 * LAM - 1.0)
        R[k] = c1 * np.concatenate([[0.0], R[k - 1][:-1]]) - c2 * R[k - 2]
    return np.stack(R).astype(np.float32)


MAGIC = 0x5F3759DF


def emit_rsqrt(E, out_f32, in_f32, t2_f32, newton=2):
    I32 = mybir.dt.int32
    oi = out_f32.bitcast(I32)
    E.tensor_scalar(oi, in_f32.bitcast(I32), 1, None, ALU.arith_shift_right)
    E.tensor_scalar(oi, oi, -1, None, ALU.bitwise_xor)
    E.tensor_scalar(oi, oi, MAGIC + 1, None, ALU.add)
    for _ in range(newton):
        E.tensor_tensor(t2_f32, out_f32, out_f32, ALU.mult)
        E.tensor_tensor(t2_f32, t2_f32, in_f32, ALU.mult)
        E.tensor_scalar(t2_f32, t2_f32, -0.5, 1.5, ALU.mult, ALU.add)
        E.tensor_tensor(out_f32, out_f32, t2_f32, ALU.mult)


class Sem:
    __slots__ = ("h", "n", "name")

    def __init__(self, nc, ctx, name):
        self.h = ctx.enter_context(nc.semaphore(name))
        self.n = 0
        self.name = name


class EB:

    def __init__(self):
        self.q = {"tensor": [], "vector": [], "scalar": [], "gpsimd": [], "sync": []}

    def op(self, eng, build, waits=(), incs=()):
        for s, v in waits:
            assert 0 <= v <= s.n, f"wait {s.name}>={v} pledged {s.n}"
        for s, n in incs:
            s.n += n

        def f(E, build=build, waits=tuple(waits), incs=tuple(incs)):
            for s, v in waits:
                if v > 0:
                    E.wait_ge(s.h, v)
            inst = build(E)
            for s, n in incs:
                inst.then_inc(s.h, n)
        self.q[eng].append(f)

    def emit(self, nc):
        with nc.Block() as blk:
            for name, fns in self.q.items():
                if fns:
                    getattr(blk, name)(lambda E, fns=fns: [f(E) for f in fns])


def small_chain(E, sem, n_start):
    state = {"n": n_start}

    def go(make, inc=True):
        E.wait_ge(sem.h, state["n"])
        inst = make()
        if inc:
            inst.then_inc(sem.h, 1)
            state["n"] += 1
        return inst
    return go


def build_nc(debug=0, phases=3):
    nc = bass.Bass("TRN2", target_bir_lowering=False)

    def din(name, shape, dt=F32):
        return nc.dram_tensor(name, shape, dt, kind="ExternalInput").ap()

    def dout(name, shape, dt=F32):
        return nc.dram_tensor(name, shape, dt, kind="ExternalOutput").ap()

    dbg_names = ("xnf", "qf", "kf", "vt", "sS") if debug else ()

    def dint(name, shape, dt):
        kind = "ExternalOutput" if name in dbg_names else "Internal"
        return nc.dram_tensor(name, shape, dt, kind=kind).ap()

    x = din("x", [TOK, DD])
    wqT = din("wqT", [DD, DD]); wkT = din("wkT", [DD, DD])
    wvT = din("wvT", [DD, DD]); woT = din("woT", [DD, DD])
    rwT = din("rwT", [DD, NE])
    polyT = din("polyT", [9, 16]); hdrow = din("hdrow", [1, 16])
    nw2 = din("nw2", [128, 1])
    gwi = din("gw", [NE, DD, DFF]); uwi = din("uw", [NE, DD, DFF])
    dwi = din("dw", [NE, DFF, DD])
    gbl = din("gbl", [128, 128]); ubl = din("ubl", [128, 128])
    dbi = din("db", [NE, DD])
    identF = din("identF", [128, 128]); identB = din("identB", [128, 128], BF16)
    rmat = din("rmat", [9, 9]); kcol = din("kcol", [9, 1])
    dvl = din("dvl", [128, 16]); dmc = din("dmc", [128, 2048], BF16)

    out = dout("out", [TOK, DD])
    dbgt = {}
    if debug:
        dbgt["d_probs"] = dout("d_probs", [128, NCH, NE])
        dbgt["d_lns"] = dout("d_lns", [NCH, 128, 12])

    xnf = dint("xnf", [8, 128, TOK], BF16)
    qf = dint("qf", [8, 128, TOK], BF16)
    kf = dint("kf", [8, 128, TOK], BF16)
    vt = dint("vt", [TOK, DD], BF16)
    sS = dint("sS", [TOK, DD], F32)
    gwb = dint("gwb", [NE, DD, DFF], BF16)
    uwb = dint("uwb", [NE, DD, DFF], BF16)
    dwb = dint("dwb", [NE, DFF, DD], BF16)
    aTd = dint("aTd", [9, 16], F32)
    pfm = dint("pfm", [NE, NCH, 128], BF16)
    tens = dict(x=x, out=out, xnf=xnf, qf=qf, kf=kf, vt=vt, sS=sS,
                gwb=gwb, uwb=uwb, dwb=dwb, pfm=pfm)

    with ExitStack() as pctx:
        def psb(name, shape, dt=F32):
            return pctx.enter_context(nc.sbuf_tensor(name, shape, dt))

        P = {}
        for name, shape, dt in [
                ("ones_r", [1, 144], F32), ("sidF", [128, 128], F32),
                ("sidB", [128, 128], BF16), ("A_bc", [128, 144], F32),
                ("A_bcb", [128, 144], BF16),
                ("dvls", [128, 16], F32), ("dms", [128, 2048], BF16),
                ("WoTb", [128, 8, DD], BF16), ("probs_res", [128, NCH, NE], F32),
                ("pfm_st", [8, 128], BF16), ("gbs", [128, 128], F32),
                ("ubs", [128, 128], F32), ("dbs", [8, DD], BF16),
                ("nws", [128, 1], F32)]:
            P[name] = psb(name, shape, dt)

        S = {}
        for name in ["sC0", "sFW", "sAW", "gPE", "gV", "sEPC", "sVAR", "sVC",
                     "sX0", "sX1", "sST", "sSQ", "sNRM", "sXP", "sXE", "sXC",
                     "sRT", "sPR", "sPTR", "sPFM", "sWST", "sGD",
                     "sQM", "sQA", "sQV", "sQEq", "sQEv", "sQTP", "sQTE",
                     "sSPV0", "sSPV1", "sSPx", "sSPq", "sSPk", "sPFC", "fPF",
                     "sEC0", "sEC1", "sEC2",
                     "aQ", "aK0", "aK1", "aV0", "aV1", "aSIM", "aEV", "aPQ",
                     "aPHI", "aAV", "aEPI", "aEPV", "aEPA", "aTP", "aTE",
                     "aWO", "aWOE", "aSP0", "aSP1", "aSPD",
                     "fXN0", "fXN1", "fG0", "fG1", "fU0", "fU1", "fD", "fPG", "fPU", "fSIL",
                     "fH", "fPY", "fYE", "fBI", "fBIC", "fXSx0", "fXSx1", "fXSs0", "fXSs1", "fGT", "cACT", "cPOOL", "cDVE",
                     "fCT", "fOUT0", "fOUT1", "dC"]:
            S[name] = Sem(nc, pctx, name)

        consts = dict(identF=identF, identB=identB, rmat=rmat, kcol=kcol,
                      dvl=dvl, dmc=dmc, gbl=gbl, ubl=ubl, db=dbi, nw2=nw2,
                      polyT=polyT, hdrow=hdrow, rwT=rwT,
                      wqT=wqT, wkT=wkT, wvT=wvT, woT=woT,
                      gwi=gwi, uwi=uwi, dwi=dwi, aTd=aTd)
        meta = build_phase01(nc, S, P, tens, consts, dbgt, debug, phases)
        if phases >= 2:
            build_phase23(nc, S, P, tens, meta, dbgt, debug, phases)

    return nc, dbgt


def build_phase01(nc, S, P, tens, C, dbgt, debug, phases):
    x, xnf, qf, kf, vt = tens["x"], tens["xnf"], tens["qf"], tens["kf"], tens["vt"]
    with ExitStack() as c1:
        def sb1(name, shape, dt=F32):
            return c1.enter_context(nc.sbuf_tensor(name, shape, dt))

        def ps1(name, shape, dt=F32):
            return c1.enter_context(nc.psum_tensor(name, shape, dt))

        WqTb = sb1("WqTb", [128, 8, DD], BF16)
        WkTb = sb1("WkTb", [128, 8, DD], BF16)
        WvTb = sb1("WvTb", [128, 8, DD], BF16)
        wstage = sb1("wstage", [128, DD])
        rwTs = sb1("rwTs", [128, 8, NE])
        ptys = sb1("ptys", [9, 16]); hds = sb1("hds", [1, 16])
        rms = sb1("rms", [9, 9]); kcs = sb1("kcs", [9, 1])
        gats = sb1("gats", [9, 16]); wgs = sb1("wgs", [9, 16])
        aTs = sb1("aTs", [9, 16]); arow = sb1("arow", [1, 144])
        sx = [sb1(f"sx{i}", [128, DD]) for i in range(2)]
        lns = sb1("lns", [128, 12])
        sqh = [sb1(f"sqh{i}", [128, 512]) for i in range(2)]
        sxn = [sb1(f"sxn{i}", [128, DD]) for i in range(2)]
        xnfm = [sb1(f"xnfm{i}", [128, 8, 128]) for i in range(2)]
        xnstage = sb1("xnstage", [128, 8, 1024], BF16)
        qstage = sb1("qstage", [128, 8, 1024], BF16)
        kstage = sb1("kstage", [128, 8, 1024], BF16)
        qtb = sb1("qtb", [128, DD], BF16)
        ktb = sb1("ktb", [128, DD], BF16)
        vtb = [sb1(f"vtb{i}", [128, DD], BF16) for i in range(2)]
        qks = sb1("qks", [128, 32])
        prw = sb1("prw", [128, 24])

        ptr = ps1("ptr", [128, 8, 128])
        pqk = [ps1(f"pqk{i}", [128, 512]) for i in range(2)]
        psm = ps1("psm", [128, 160])
        pqb = ps1("pqb", [128, 16, 128], BF16)

        eb = EB()

        for dst, src in [(P["sidF"][:], C["identF"]), (P["sidB"][:], C["identB"]),
                         (P["dvls"][:], C["dvl"]), (P["dms"][:], C["dmc"]),
                         (P["gbs"][:], C["gbl"]), (P["ubs"][:], C["ubl"]),
                         (P["nws"][:], C["nw2"]),
                         (ptys[:], C["polyT"]), (hds[:], C["hdrow"]),
                         (rms[:], C["rmat"]), (kcs[:], C["kcol"]),
                         (rwTs[:],
                          C["rwT"].rearrange("(c p) e -> p c e", p=128))]:
            eb.op("sync", lambda E, d=dst, s=src: E.dma_start(d, s),
                  waits=[(S["sC0"], S["sC0"].n)], incs=[(S["sC0"], 16)])
        C0 = S["sC0"].n

        for e in range(NE):
            for dsrc, ddst in ((C["gwi"], tens["gwb"]), (C["uwi"], tens["uwb"]),
                               (C["dwi"], tens["dwb"])):
                eb.op("gpsimd", lambda E, a=ddst[e], b=dsrc[e]: E.dma_start(a, b),
                      waits=[(S["sFW"], S["sFW"].n)], incs=[(S["sFW"], 16)])

        eb.op("gpsimd", lambda E: E.dma_start(P["dbs"][:], C["db"]),
              waits=[(S["sAW"], S["sAW"].n)], incs=[(S["sAW"], 16)])
        for dst, src in ((WqTb, C["wqT"]), (WkTb, C["wkT"]), (WvTb, C["wvT"])):
            eb.op("gpsimd", lambda E, d=dst, s=src: E.dma_start(
                d[:], s.rearrange("(c p) f -> p c f", p=128)),
                waits=[(S["sAW"], S["sAW"].n)], incs=[(S["sAW"], 16)])
        CASTW = S["sAW"].n

        for jc in range(8):
            wl = [] if jc == 0 else [(S["gV"], jc)]
            eb.op("sync", lambda E, jc=jc: E.dma_start(
                wstage[:], C["woT"][jc * 128:(jc + 1) * 128, :]),
                waits=wl, incs=[(S["sWST"], 16)])
            eb.op("vector", lambda E, jc=jc: E.tensor_scalar_mul(
                P["WoTb"][:, jc, :], wstage[:], P["nws"][:, 0:1]),
                waits=[(S["sWST"], (jc + 1) * 16), (S["sC0"], C0)],
                incs=[(S["gV"], 1)])
        C1 = S["sC0"].n

        eb.op("vector", lambda E: E.memset(P["ones_r"][:], 1.0), incs=[(S["gV"], 1)])
        gv0 = S["gV"].n
        eb.op("tensor", lambda E: E.matmul(psm[0:9, 0:16], P["ones_r"][0:1, 0:9],
                                           hds[:], start=True, stop=True),
              waits=[(S["sC0"], C0), (S["gV"], gv0)], incs=[(S["gPE"], 1)])
        vc0 = S["sVC"].n

        def gchain(E, vc0=vc0):
            E.wait_ge(S["gPE"].h, 1)
            go = small_chain(E, S["sVC"], vc0)
            go(lambda: E.tensor_scalar(gats[:], psm[0:9, 0:16], kcs[:, 0:1], 1.0,
                                       ALU.subtract, ALU.add))
            go(lambda: E.tensor_scalar(gats[:], gats[:], 0.0, 1.0, ALU.max, ALU.min))
            i = go(lambda: E.tensor_tensor(wgs[:], ptys[:], gats[:], ALU.mult),
                   inc=False)
            i.then_inc(S["gV"].h, 3)
        eb.q["vector"].append(gchain)
        S["sVC"].n += 2
        S["gV"].n += 3
        eb.op("tensor", lambda E: E.matmul(psm[0:9, 16:32], rms[:], wgs[:],
                                           start=True, stop=True),
              waits=[(S["gV"], gv0 + 3)], incs=[(S["gPE"], 1)])
        eb.op("vector", lambda E: E.tensor_copy(aTs[:], psm[0:9, 16:32]),
              waits=[(S["gPE"], 2)], incs=[(S["gV"], 1)])
        eb.op("sync", lambda E: E.dma_start(C["aTd"], aTs[:]),
              waits=[(S["gV"], gv0 + 4)], incs=[(S["sGD"], 16)])
        eb.op("sync", lambda E: E.dma_start(
            arow[:].rearrange("a (h j) -> a h j", j=9),
            C["aTd"].transpose([1, 0]).unsqueeze(0),
            allow_slow_non_contiguous=True),
            waits=[(S["sGD"], 16)], incs=[(S["sGD"], 16)])
        eb.op("tensor", lambda E: E.matmul(psm[:, 16:160], P["ones_r"][0:1, 0:128],
                                           arow[:], start=True, stop=True),
              waits=[(S["sGD"], 32), (S["gV"], gv0)], incs=[(S["gPE"], 1)])
        eb.op("vector", lambda E: E.tensor_copy(P["A_bc"][:], psm[:, 16:160]),
              waits=[(S["gPE"], 3)], incs=[(S["gV"], 1)])
        eb.op("vector", lambda E: E.tensor_copy(P["A_bcb"][:], psm[:, 16:160]),
              incs=[(S["gV"], 1)])

        deltas0 = None
        spill_after = {}
        for ct in range(NCH_L):
            b = ct % 2
            tile, sl = ct // 8, ct % 8
            base = {k: S[k].n for k in S}
            epw = [(S["sEC0"], max(0, ct - 1)), (S["sEC1"], max(0, ct - 1)),
                   (S["sEC2"], max(0, ct - 1))]

            sxp = S["sX0"] if b == 0 else S["sX1"]
            eb.op("sync", lambda E, ct=ct, b=b: E.dma_start(
                sx[b][:], x[ct * 128:(ct + 1) * 128, :]),
                waits=[(S["sST"], max(0, ct - 1))], incs=[(sxp, 16)])

            eb.op("scalar", lambda E, b=b: E.activation(sqh[0][:], sx[b][:, 0:512],
                                                        AF.Square),
                  waits=epw + [(sxp, 16 * (ct // 2 + 1)),
                               (S["sQV"], max(0, 5 * ct))],
                  incs=[(S["sSQ"], 1)])
            eb.op("scalar", lambda E, b=b: E.activation(sqh[1][:], sx[b][:, 512:1024],
                                                        AF.Square),
                  incs=[(S["sSQ"], 1)])

            vcs = S["sVC"].n

            def ln_stats(E, ct=ct, b=b, epv=epw[0][1], vcs=vcs, sxp=sxp):
                E.wait_ge(S["sEC0"].h, epv)
                E.wait_ge(S["sEC1"].h, epv)
                E.wait_ge(S["sEC2"].h, epv)
                E.wait_ge(sxp.h, 16 * (ct // 2 + 1))
                E.wait_ge(S["sSQ"].h, 2 * (ct + 1))
                go = small_chain(E, S["sVC"], vcs)
                E.reduce_sum(lns[:, 0:1], sx[b][:], AX.X)
                E.reduce_sum(lns[:, 1:2], sqh[0][:], AX.X)
                go(lambda: E.reduce_sum(lns[:, 2:3], sqh[1][:], AX.X))
                go(lambda: E.tensor_scalar_mul(lns[:, 3:4], lns[:, 0:1], 1.0 / DD))
                go(lambda: E.tensor_add(lns[:, 4:5], lns[:, 1:2], lns[:, 2:3]))
                go(lambda: E.tensor_scalar_mul(lns[:, 5:6], lns[:, 4:5], 1.0 / DD))
                go(lambda: E.tensor_mul(lns[:, 6:7], lns[:, 3:4], lns[:, 3:4]))
                go(lambda: E.scalar_tensor_tensor(lns[:, 7:8], lns[:, 6:7], -1.0,
                                                  lns[:, 5:6], ALU.mult, ALU.add))
                i = go(lambda: E.tensor_scalar_add(lns[:, 8:9], lns[:, 7:8], 1e-5),
                       inc=False)
                i.then_inc(S["sVAR"].h, 1)
            eb.q["vector"].append(ln_stats)
            S["sVC"].n += 6
            S["sVAR"].n += 1
            eb.op("scalar", lambda E: E.activation(lns[:, 9:10], lns[:, 8:9], AF.Ln),
                  waits=[(S["sVAR"], ct + 1)], incs=[(S["cACT"], 1)])
            eb.op("scalar", lambda E: E.activation(lns[:, 10:11], lns[:, 9:10],
                                                   AF.Exp, scale=-0.5),
                  waits=[(S["cACT"], S["cACT"].n)], incs=[(S["sQA"], 1)])
            qa_rstd = S["sQA"].n

            vcn = S["sVC"].n

            def ln_nbias(E, b=b, qa_rstd=qa_rstd):
                E.wait_ge(S["sQA"].h, qa_rstd)
                i = E.scalar_tensor_tensor(lns[:, 11:12], lns[:, 3:4], -1.0,
                                           lns[:, 10:11], ALU.mult, ALU.mult)
                i.then_inc(S["sQV"].h, 1)
            eb.q["vector"].append(ln_nbias)
            S["sQV"].n += 1
            nb_now = S["sQV"].n
            eb.op("scalar", lambda E, b=b: E.activation(
                sxn[b][:], sx[b][:], AF.Identity, bias=lns[:, 11:12],
                scale=lns[:, 10:11]),
                waits=[(S["sQV"], nb_now), (S["cACT"], S["cACT"].n)],
                incs=[(S["sST"], 1)])
            S["sNRM"].n += 1
            if debug:
                eb.op("scalar", lambda E, ct=ct: E.dma_start(
                    dbgt["d_lns"][ct], lns[:]),
                    waits=[(S["sST"], ct + 1)], incs=[(S["dC"], 16)])

            rmap = lambda j: (j % 2) * 4 + j // 2
            for dc in range(8):
                wl = []
                if dc == 0:
                    wl = epw + [(S["sST"], ct + 1), (S["sXE"], max(0, ct * 8))]
                elif dc >= 2:
                    wl = [(S["sXE"], ct * 8 + dc - 1)]
                eb.op("tensor", lambda E, b=b, dc=dc, r=rmap(dc): E.transpose(
                    ptr[:, r, :], sxn[b][:, dc * 128:(dc + 1) * 128], P["sidF"][:]),
                    waits=wl, incs=[(S["sXP"], 1)])
                eb.op("scalar", lambda E, b=b, dc=dc, r=rmap(dc): E.activation(
                    xnfm[b][:, dc, :], ptr[:, r, :], AF.Identity),
                    waits=[(S["sXP"], ct * 8 + dc + 1)] +
                          ([(S["sXC"], max(0, ct - 1))] if dc == 0 else []),
                    incs=[(S["sXE"], 1)])
            eb.op("vector", lambda E, b=b, sl=sl: E.tensor_copy(
                xnstage[:, :, sl * 128:(sl + 1) * 128], xnfm[b][:]),
                waits=[(S["sXE"], (ct + 1) * 8),
                       (S["sSPx"], spill_after.get(tile - 1, 0) if sl == 0 else 0),
                       (S["sQM"], max(0, (ct - 7) * 6))],
                incs=[(S["sXC"], 1)])

            for dc in range(8):
                eb.op("tensor", lambda E, b=b, dc=dc: E.matmul(
                    psm[:, 0:8], xnfm[b][:, dc, :], rwTs[:, dc, :],
                    start=(dc == 0), stop=(dc == 7)),
                    waits=([(S["sXE"], ct * 8 + 8), (S["sPFC"], max(0, ct))]
                           if dc == 0 else []),
                    incs=([(S["sRT"], 1)] if dc == 7 else []))

            vpa = S["sVC"].n

            def probs_a(E, ct=ct, vpa=vpa):
                E.wait_ge(S["sRT"].h, ct + 1)
                go = small_chain(E, S["sVC"], vpa)
                E.tensor_copy(prw[:, 16:24], psm[:, 0:8])
                go(lambda: E.reduce_max(prw[:, 8:9], psm[:, 0:8], AX.X))
                i = go(lambda: E.tensor_scalar_mul(prw[:, 9:10], prw[:, 8:9], -1.0),
                       inc=False)
                i.then_inc(S["sPR"].h, 1)
            eb.q["vector"].append(probs_a)
            S["sVC"].n += 1
            S["sPR"].n += 1
            eb.op("scalar", lambda E: E.activation(prw[:, 0:8], prw[:, 16:24],
                                                   AF.Exp, bias=prw[:, 9:10]),
                  waits=[(S["sPR"], base["sPR"] + 1)], incs=[(S["sQA"], 1)])
            qa_pe = S["sQA"].n

            vpb = S["sVC"].n

            def probs_b(E, ct=ct, qa=qa_pe, vpb=vpb):
                E.wait_ge(S["sQA"].h, qa)
                go = small_chain(E, S["sVC"], vpb)
                go(lambda: E.reduce_sum(prw[:, 8:9], prw[:, 0:8], AX.X))
                go(lambda: E.reciprocal(prw[:, 9:10], prw[:, 8:9]))
                go(lambda: E.tensor_scalar_mul(prw[:, 0:8], prw[:, 0:8],
                                               prw[:, 9:10]))
                go(lambda: E.reduce_max(prw[:, 8:9], prw[:, 0:8], AX.X))
                go(lambda: E.tensor_scalar(prw[:, 10:18], prw[:, 0:8], prw[:, 8:9],
                                           None, ALU.is_ge))
                go(lambda: E.scalar_tensor_tensor(prw[:, 10:18], prw[:, 10:18],
                                                  -1e30, prw[:, 0:8],
                                                  ALU.mult, ALU.add))
                go(lambda: E.reduce_max(prw[:, 8:9], prw[:, 10:18], AX.X))
                go(lambda: E.tensor_scalar(prw[:, 10:18], prw[:, 0:8], prw[:, 8:9],
                                           None, ALU.is_ge))
                go(lambda: E.tensor_tensor(prw[:, 0:8], prw[:, 0:8], prw[:, 10:18],
                                           ALU.mult))
                go(lambda: E.reduce_sum(prw[:, 8:9], prw[:, 0:8], AX.X))
                go(lambda: E.tensor_scalar_add(prw[:, 8:9], prw[:, 8:9], 1e-9))
                go(lambda: E.reciprocal(prw[:, 9:10], prw[:, 8:9]))
                i = go(lambda: E.tensor_scalar_mul(P["probs_res"][:, ct, :],
                                                   prw[:, 0:8], prw[:, 9:10]),
                       inc=False)
                i.then_inc(S["sPR"].h, 1)
            eb.q["vector"].append(probs_b)
            S["sVC"].n += 12
            S["sPR"].n += 1
            eb.op("tensor", lambda E, ct=ct: E.transpose(
                psm[0:8, 16:144], P["probs_res"][:, ct, :], P["sidF"][:]),
                waits=[(S["sPR"], base["sPR"] + 2), (S["sPFC"], max(0, ct))],
                incs=[(S["sPTR"], 1)])
            eb.op("vector", lambda E, ct=ct: E.tensor_copy(
                P["pfm_st"][:], psm[0:8, 16:144]),
                waits=[(S["sPTR"], ct + 1), (S["sPFM"], max(0, 16 * ct))],
                incs=[(S["sPFC"], 1)])
            eb.op("scalar", lambda E, ct=ct: E.dma_start(
                tens["pfm"][:, ct, :], P["pfm_st"][:]),
                waits=[(S["sPFC"], ct + 1)], incs=[(S["sPFM"], 16)])

            qv_marks = []
            for mi, (Wb, kind) in enumerate(((WqTb, "q"), (WkTb, "k"), (WvTb, "v"))):
                for hf in range(2):
                    idx = 6 * ct + 2 * mi + hf
                    lidx = 2 * mi + hf
                    bank = idx % 2
                    for dc in range(8):
                        wl = []
                        if dc == 0:
                            wl = [(S["sXC"], ct + 1), (S["sAW"], CASTW)]
                            li2 = lidx - 2
                            if li2 >= -2 and idx >= 2:
                                if li2 >= 0:
                                    if li2 < 4:
                                        wl.append((S["sQEq"], 4 * ct + li2 + 1))
                                    else:
                                        wl.append((S["sQEv"], 2 * ct + li2 - 3))
                                else:
                                    wl.append((S["sQEv"], 2 * (ct - 1) + li2 + 3))
                        eb.op("tensor", lambda E, sl=sl, dc=dc, Wb=Wb, hf=hf,
                              bank=bank: E.matmul(
                                  pqk[bank][:],
                                  xnstage[:, dc, sl * 128:(sl + 1) * 128],
                                  Wb[:, dc, hf * 512:(hf + 1) * 512],
                                  start=(dc == 0), stop=(dc == 7)),
                              waits=wl, incs=([(S["sQM"], 1)] if dc == 7 else []))
                    mmn = S["sQM"].n
                    if kind == "v":
                        spv = S["sSPV0"] if b == 0 else S["sSPV1"]
                        eb.op("scalar", lambda E, bank=bank, hf=hf, b=b:
                              E.activation(
                                  vtb[b][:, hf * 512:(hf + 1) * 512],
                                  pqk[bank][:], AF.Identity),
                              waits=[(S["sQM"], mmn), (spv, 16 * (ct // 2))],
                              incs=[(S["sQEv"], 1)])
                    else:
                        dsttb = qtb if kind == "q" else ktb
                        sqb = sqh[lidx % 2]
                        sqw = ([(S["sST"], ct + 1)] if lidx < 2
                               else [(S["sQV"], qv_marks[lidx - 2])])
                        eb.op("scalar", lambda E, bank=bank, sqb=sqb: E.activation(
                            sqb[:], pqk[bank][:], AF.Square),
                            waits=[(S["sQM"], mmn)] + sqw, incs=[(S["sQA"], 1)])
                        qa1 = S["sQA"].n

                        vqs = S["sVC"].n

                        def qk_ss(E, sqb=sqb, qa1=qa1, vqs=vqs):
                            E.wait_ge(S["sQA"].h, qa1)
                            go = small_chain(E, S["sVC"], vqs)
                            go(lambda: E.reduce_sum(
                                qks[:, 0:8].unsqueeze(2),
                                sqb[:].rearrange("p (h d) -> p h d", h=8), AX.X))
                            i = go(lambda: E.tensor_scalar(qks[:, 8:16], qks[:, 0:8],
                                                           1e-24, None, ALU.max),
                                   inc=False)
                            i.then_inc(S["sQV"].h, 1)
                        eb.q["vector"].append(qk_ss)
                        S["sVC"].n += 1
                        S["sQV"].n += 1
                        qv_marks.append(S["sQV"].n)
                        eb.op("scalar", lambda E: E.activation(
                            qks[:, 16:24], qks[:, 8:16], AF.Ln),
                            waits=[(S["sQV"], S["sQV"].n)], incs=[(S["cACT"], 1)])
                        eb.op("scalar", lambda E: E.activation(
                            qks[:, 16:24], qks[:, 16:24], AF.Exp, scale=-0.5),
                            waits=[(S["cACT"], S["cACT"].n)], incs=[(S["sQA"], 1)])
                        qa2 = S["sQA"].n
                        eb.op("vector", lambda E, bank=bank, hf=hf, dsttb=dsttb:
                              E.tensor_tensor(
                                  dsttb[:, hf * 512:(hf + 1) * 512].rearrange(
                                      "p (h d) -> p h d", h=8),
                                  pqk[bank][:].rearrange("p (h d) -> p h d", h=8),
                                  qks[:, 16:24].unsqueeze(2).broadcast_to(
                                      [128, 8, 64]),
                                  ALU.mult),
                              waits=[(S["sQA"], qa2)], incs=[(S["sQEq"], 1)])
                if kind == "v":
                    pass

            qe_now = S["sQEq"].n
            rmap16 = lambda j: (j % 2) * 8 + j // 2
            for ti, (srctb, stage) in enumerate(((qtb, qstage), (ktb, kstage))):
                for dc in range(8):
                    j = ti * 8 + dc
                    wl = [(S["sQEq"], qe_now)] if j == 0 else []
                    if j >= 2:
                        wl.append((S["sQTE"], ct * 16 + j - 1))
                    eb.op("tensor", lambda E, srctb=srctb, dc=dc, r=rmap16(ti * 8 + dc):
                          E.transpose(pqb[:, r, :],
                                      srctb[:, dc * 128:(dc + 1) * 128], P["sidB"][:]),
                          waits=wl, incs=[(S["sQTP"], 1)])
                    wl2 = [(S["sQTP"], ct * 16 + j + 1)]
                    if dc == 0 and sl == 0:
                        wl2.append(((S["sSPq"] if ti == 0 else S["sSPk"]),
                                    spill_after.get(tile - 1, 0)))
                    eb.op("scalar", lambda E, stage=stage, dc=dc, sl=sl,
                          r=rmap16(ti * 8 + dc):
                          E.activation(stage[:, dc, sl * 128:(sl + 1) * 128],
                                       pqb[:, r, :], AF.Identity),
                          waits=wl2, incs=[(S["sQTE"], 1)])
            spv = S["sSPV0"] if b == 0 else S["sSPV1"]
            eb.op("scalar", lambda E, ct=ct, b=b: E.dma_start(
                vt[ct * 128:(ct + 1) * 128, :], vtb[b][:]),
                waits=[(S["sQEv"], 2 * (ct + 1))], incs=[(spv, 16)])

            if sl == 7:
                te_now = S["sQTE"].n
                xc_now = S["sXC"].n
                for stage, dst, ssem in ((xnstage, xnf, S["sSPx"]),
                                         (qstage, qf, S["sSPq"]),
                                         (kstage, kf, S["sSPk"])):
                    wl = ([(S["sXC"], xc_now)] if stage is xnstage
                          else [(S["sQTE"], te_now)])
                    eb.op("scalar", lambda E, stage=stage, dst=dst, tile=tile:
                          E.dma_start(
                              dst.rearrange("c p t -> p c t")[
                                  :, :, tile * 1024:(tile + 1) * 1024],
                              stage[:]),
                          waits=wl, incs=[(ssem, 16)])
                spill_after[tile] = 16 * (tile + 1)

            eb.op("vector", lambda E: E.tensor_copy(qks[:, 24:25], qks[:, 0:1]),
                  incs=[(S["sEC0"], 1)])
            eb.op("scalar", lambda E: E.copy(qks[:, 25:26], qks[:, 0:1]),
                  incs=[(S["sEC1"], 1)])
            eb.op("tensor", lambda E: E.transpose(
                pqb[:, 15, :], ktb[:, 896:1024], P["sidB"][:]),
                waits=[(S["sQTE"], ct * 16 + 16)], incs=[(S["sEC2"], 1)])

            d = {}
            for k2 in S:
                if S[k2].n != base[k2]:
                    key = k2.rstrip("01") if k2[-1] in "01" else k2
                    d[key] = d.get(key, 0) + S[k2].n - base[k2]
            if deltas0 is None and sl not in (0, 7):
                deltas0 = d
            elif sl not in (0, 7) and deltas0 is not None:
                assert d == deltas0, f"chunk {ct} deltas {d} != {deltas0}"

        meta = dict(SPx=S["sSPx"].n, SPq=S["sSPq"].n, SPk=S["sSPk"].n,
                    SPV0=S["sSPV0"].n, SPV1=S["sSPV1"].n, CASTW=CASTW)

        if debug:
            eb.op("scalar", lambda E: E.dma_start(dbgt["d_probs"], P["probs_res"][:]),
                  waits=[(S["sPR"], S["sPR"].n)], incs=[(S["dC"], 16)])
        if phases == 1:
            dcw = S["dC"].n
            fins = [(S["sSPx"], S["sSPx"].n), (S["sSPq"], S["sSPq"].n),
                    (S["sSPk"], S["sSPk"].n), (S["sSPV0"], S["sSPV0"].n),
                    (S["sSPV1"], S["sSPV1"].n), (S["sFW"], S["sFW"].n),
                    (S["dC"], dcw), (S["sPFM"], S["sPFM"].n)]

            def fin2(E, fins=fins):
                for s, v in fins:
                    if v:
                        E.wait_ge(s.h, v)
            eb.q["sync"].append(fin2)
        eb.emit(nc)
    return meta


def build_phase23(nc, S, P, tens, meta, dbgt, debug, phases):
    x, out, sS = tens["x"], tens["out"], tens["sS"]
    qf, kf, vt = tens["qf"], tens["kf"], tens["vt"]
    xnf, gwb, uwb, dwb = tens["xnf"], tens["gwb"], tens["uwb"], tens["dwb"]
    P1B = [("sSPx", meta["SPx"]), ("sSPq", meta["SPq"]), ("sSPk", meta["SPk"]),
           ("sSPV0", meta["SPV0"]), ("sSPV1", meta["SPV1"])]
    with ExitStack() as c2:
        def sb2(name, shape, dt=F32):
            return c2.enter_context(nc.sbuf_tensor(name, shape, dt))

        def ps2(name, shape, dt=F32):
            return c2.enter_context(nc.psum_tensor(name, shape, dt))

        qtl = [sb2("qtl0", [128, 8, 512], BF16)] * 2
        ksp = [sb2(f"ksp{i}", [128, 8, 256], BF16) for i in range(2)]
        vsp = [sb2(f"vsp{i}", [128, 2, 1024], BF16) for i in range(2)]
        st_s = [sb2(f"st_s{i}", [128, 2, 512], BF16) for i in range(2)]
        st_z = [sb2(f"st_z{i}", [128, 2, 512], BF16) for i in range(2)]
        stp2 = [sb2(f"st_p{i}", [128, 1024], BF16) for i in range(2)]
        st_q = [sb2(f"st_q{i}", [128, 1024], BF16) for i in range(2)]
        st_f = [sb2(f"st_f{i}", [128, 2, 512], BF16) for i in range(4)]
        ao_hq = sb2("ao_hq", [128, 4, 4, 64])
        ao_fm = sb2("ao_fm", [128, 8, 512], BF16)
        scr1 = sb2("scr1", [128, 1024])
        est = sb2("est", [128, 64])
        xtile = [sb2(f"xtile{i}", [128, 8, 512], BF16) for i in range(2)]
        gch = [sb2(f"gch{i}", [128, 8, 256], BF16) for i in range(2)]
        uch = [sb2(f"uch{i}", [128, 8, 256], BF16) for i in range(2)]
        Dsb = sb2("Dsb", [128, 16, 1024], BF16)
        hsb = sb2("hsb", [128, 16, 512], BF16)
        y_acc = sb2("y_acc", [128, 4, 1024])
        hs_scr = [sb2(f"hs_scr{i}", [128, 512]) for i in range(2)]
        pfm_sb = sb2("pfm_sb", [8, 512], BF16)
        xts = [sb2(f"xts{i}", [128, 512]) for i in range(2)]
        sts = [sb2(f"sts{i}", [128, 512]) for i in range(2)]
        osg = [sb2(f"osg{i}", [128, 512]) for i in range(2)]

        psim = [ps2(f"psim{i}", [128, 512]) for i in range(2)]
        pav = ps2("pav", [128, 1024])
        pg = ps2("pg", [128, 512])
        pu = ps2("pu", [128, 512])
        py = [ps2(f"py{i}", [128, 512]) for i in range(2)]

        eb = EB()
        A = S

        mk = dict(sim_slice={}, woe_slice={}, sp_slice={}, kv={}, te=0, tp=0,
                  epa=0, epv=0, epi=0, wo_slice={}, ffn_mm_tt={}, ye_tt={},
                  ct_tt={}, gt_tt={}, fout_tt={})
        phi_ctr = [0]
        kv_ctr = [0]
        pend_av = []

        def co(h, j):
            return P["A_bc"][:, h * 9 + j:h * 9 + j + 1]

        def coB(h, j):
            return P["A_bcb"][:, h * 9 + j:h * 9 + j + 1].broadcast_to([128, 1024])

        def emit_av(pidx, k, qt, sp, hh, h, sb_, kvb, first_q, nsp):
            sb_ = pidx % 4
            wl = [(A["aPHI"], pidx + 1)]
            if first_q:
                wl.append((A["aTE"], mk["te"]))
                wl.append((A["aEPI"], mk["epi"]))
            valid = [(kcw, ts) for kcw in range(2) for ts in range(4)
                     if 2 * sp + kcw <= 4 * qt + ts]
            for n_i, (kcw, ts) in enumerate(valid):
                st = (sp == 0 and kcw == 0 and ts == 0 and hh in (0, 2))
                stp = (2 * sp + kcw == 4 * qt + ts)
                eb.op("tensor", lambda E, kcw=kcw, ts=ts, hh=hh, h=h, sb_=sb_,
                      kvb=kvb, st=st, stp=stp: E.matmul(
                          pav[:, hh * 256 + ts * 64:hh * 256 + (ts + 1) * 64],
                          st_f[sb_][:, kcw, ts * 128:(ts + 1) * 128],
                          vsp[kvb][:, kcw, h * 64:(h + 1) * 64],
                          start=st, stop=stp, skip_group_check=True),
                      waits=(wl if n_i == 0 else []),
                      incs=([(A["aAV"], 1)] if n_i == len(valid) - 1 else []))

        def attn_slice(k):
            seq, qt = divmod(k, 4)
            nsp = 2 * (qt + 1)
            t0 = k * 512
            qb = k % 2
            wl = [(S[n], v) for n, v in P1B]
            if k >= 1:
                wl.append((A["aSIM"], mk["sim_slice"][k - 1]))
            eb.op("sync", lambda E, qb=qb, t0=t0: E.dma_start(
                qtl[qb][:], qf.rearrange("c p t -> p c t")[:, :, t0:t0 + 512]),
                waits=wl, incs=[(A["aQ"], 16)])
            aQ_k = A["aQ"].n
            yield 1.0
            def kv_load(sp):
                kvi = kv_ctr[0]
                kv_ctr[0] += 1
                kvb = kvi % 2
                tk0 = seq * 2048 + sp * 256
                sK = A["aK0"] if kvb == 0 else A["aK1"]
                sV = A["aV0"] if kvb == 0 else A["aV1"]
                wlk = [(S[n], v) for n, v in P1B]
                wlv = []
                if kvi >= 2:
                    simm, avm = mk["kv"][kvi - 2]
                    wlk.append((A["aSIM"], simm))
                    wlv.append((A["aAV"], avm))
                eb.op("sync", lambda E, kvb=kvb, tk0=tk0: E.dma_start(
                    ksp[kvb][:], kf.rearrange("c p t -> p c t")[:, :, tk0:tk0 + 256]),
                    waits=wlk, incs=[(sK, 16)])
                eb.op("sync", lambda E, kvb=kvb, tk0=tk0: E.dma_start(
                    vsp[kvb][:],
                    vt.rearrange("(a w p) d -> a p w d", p=128, w=2)[tk0 // 256]),
                    waits=wlv, incs=[(sV, 16)])
                return kvi, kvb, (sK.n, sV.n, sK, sV)

            for hq in range(4):
                first_of_quarter = True
                pend_kv = kv_load(0)
                for sp in range(nsp):
                    kvi, kvb, kvn = pend_kv
                    diag = sp >= nsp - 2
                    for hpair in (0, 2):
                        CH = []
                        for cc in range(2):
                            hh = hpair + cc
                            h = hq * 4 + hh
                            pidx = phi_ctr[0]
                            phi_ctr[0] += 1
                            CH.append(dict(hh=hh, h=h, pidx=pidx, sb=pidx % 2,
                                           c=h // 2, par=(h % 2) * 64))
                        for ch in CH:
                            pidx, sb_ = ch["pidx"], ch["sb"]
                            for kcw in range(2):
                                ei = 2 * pidx + kcw
                                bank = ei % 2
                                wl = []
                                if kcw == 0:
                                    wl = [(kvn[2], kvn[0]), (kvn[3], kvn[1]),
                                          (A["aQ"], aQ_k)]
                                if ei >= 2:
                                    wl.append((A["aEV"], 2 * (ei - 2) + 2))
                                eb.op("tensor", lambda E, c=ch["c"], par=ch["par"],
                                      kcw=kcw, bank=bank, kvb=kvb, qb=qb: E.matmul(
                                          psim[bank][:],
                                          ksp[kvb][par:par + 64, c,
                                                   kcw * 128:(kcw + 1) * 128],
                                          qtl[qb][par:par + 64, c, :],
                                          start=True, stop=True),
                                      waits=wl, incs=[(A["aSIM"], 1)])
                                smn = A["aSIM"].n
                                wl2 = [(A["aSIM"], smn)]
                                if pidx >= 2:
                                    wl2.append((A["aPHI"], pidx - 1))
                                eb.op("scalar", lambda E, bank=bank, sb_=sb_,
                                      kcw=kcw: E.activation(
                                          st_s[sb_][:, kcw, :], psim[bank][:],
                                          AF.Identity),
                                      waits=wl2, incs=[(A["aEV"], 2)])
                            ch["evn"] = A["aEV"].n
                        CP, CV, CA = S["cPOOL"], S["cDVE"], S["cACT"]
                        for ch in CH:
                            sb_, pidx = ch["sb"], ch["pidx"]
                            zc = st_z[sb_][:].rearrange("p w t -> p (w t)")
                            sc = st_s[sb_][:].rearrange("p w t -> p (w t)")
                            ch["zc"], ch["sc"] = zc, sc
                            ch["qs"] = st_q[pidx % 2][:]
                            eb.op("gpsimd", lambda E, zc=zc, sc=sc: E.tensor_tensor(
                                zc, sc, sc, ALU.mult),
                                waits=[(A["aEV"], ch["evn"]),
                                       (A["aPHI"], max(0, pidx - 1)),
                                       (CP, CP.n)],
                                incs=[(CP, 1)])
                            ch["zn"] = CP.n
                        for ch in CH:
                            h, pidx = ch["h"], ch["pidx"]
                            pst = stp2[pidx % 2][:]
                            ch["pst"] = pst
                            eb.op("scalar", lambda E, zc=ch["zc"], h=h, pst=pst:
                                  E.activation(pst, zc, AF.Identity,
                                               bias=co(h, 6), scale=co(h, 8)),
                                  waits=[(CP, ch["zn"]),
                                         (A["aPHI"], max(0, pidx - 1)),
                                         (CA, CA.n)],
                                  incs=[(CA, 1)])
                            eb.op("scalar", lambda E, zc=ch["zc"], h=h,
                                  qs=ch["qs"]: E.activation(
                                      qs, zc, AF.Identity, bias=co(h, 5),
                                      scale=co(h, 7)),
                                  waits=[(A["aPHI"], max(0, pidx - 1)),
                                         (CA, CA.n)],
                                  incs=[(CA, 1)])
                            ch["initn"] = CA.n
                        while len(pend_av) > 2:
                            pend_av.pop(0)()
                        def dve(ch, f, waits=()):
                            eb.op("vector", f, waits=list(waits) + [(CV, CV.n)],
                                  incs=[(CV, 1)])
                            return CV.n

                        def act(ch, f, waits=()):
                            eb.op("scalar", f, waits=list(waits) + [(CA, CA.n)],
                                  incs=[(CA, 1)])
                            return CA.n
                        for ch in CH:
                            ch["m"] = dve(ch, lambda E, p=ch["pst"], zc=ch["zc"]:
                                          E.tensor_tensor(p, p, zc, ALU.mult),
                                          [(CA, ch["initn"])])
                        for ch in CH:
                            bc4 = coB(ch["h"], 4)
                            ch["m"] = dve(ch, lambda E, p=ch["pst"], b=bc4:
                                          E.tensor_tensor(p, p, b, ALU.add))
                        for ch in CH:
                            ch["mq"] = dve(ch, lambda E, q=ch["qs"], zc=ch["zc"]:
                                           E.tensor_tensor(q, q, zc, ALU.mult))
                        for ch in CH:
                            ch["m"] = dve(ch, lambda E, p=ch["pst"], zc=ch["zc"]:
                                          E.tensor_tensor(p, p, zc, ALU.mult))
                        for ch in CH:
                            ch["aq"] = act(ch, lambda E, q=ch["qs"], h=ch["h"]:
                                           E.activation(q, q, AF.Identity,
                                                        bias=co(h, 3)),
                                           [(CV, ch["mq"])])
                        for ch in CH:
                            ch["ap"] = act(ch, lambda E, p=ch["pst"], h=ch["h"]:
                                           E.activation(p, p, AF.Identity,
                                                        bias=co(h, 2)),
                                           [(CV, ch["m"])])
                        for ch in CH:
                            ch["mq"] = dve(ch, lambda E, q=ch["qs"], zc=ch["zc"]:
                                           E.tensor_tensor(q, q, zc, ALU.mult),
                                           [(CA, ch["aq"])])
                        for ch in CH:
                            ch["m"] = dve(ch, lambda E, p=ch["pst"], zc=ch["zc"]:
                                          E.tensor_tensor(p, p, zc, ALU.mult),
                                          [(CA, ch["ap"])])
                        for ch in CH:
                            bc1 = coB(ch["h"], 1)
                            ch["mq"] = dve(ch, lambda E, q=ch["qs"], b=bc1:
                                           E.tensor_tensor(q, q, b, ALU.add))
                        for ch in CH:
                            bc0 = coB(ch["h"], 0)
                            ch["m"] = dve(ch, lambda E, p=ch["pst"], b=bc0:
                                          E.tensor_tensor(p, p, b, ALU.add))
                        for ch in CH:
                            ch["mq"] = dve(ch, lambda E, q=ch["qs"], sc=ch["sc"]:
                                           E.tensor_tensor(q, q, sc, ALU.mult))
                        for ch in CH:
                            pidx = ch["pidx"]
                            fcs = st_f[pidx % 4][:].rearrange("p w t -> p (w t)")
                            wl = [(CV, CV.n)]
                            if pidx >= 4:
                                wl.append((A["aAV"], pidx - 3))
                            if diag:
                                eb.op("vector", lambda E, fcs=fcs, p=ch["pst"],
                                      q=ch["qs"]: E.tensor_tensor(
                                          fcs, p, q, ALU.add),
                                      waits=wl, incs=[(CV, 1)])
                                dmo = 0 if sp == nsp - 2 else 1024
                                eb.op("vector", lambda E, fcs=fcs, dmo=dmo:
                                      E.tensor_tensor(
                                          fcs, fcs, P["dms"][:, dmo:dmo + 1024],
                                          ALU.mult),
                                      waits=[(CV, CV.n)],
                                      incs=[(A["aPHI"], 1)])
                            else:
                                eb.op("vector", lambda E, fcs=fcs, p=ch["pst"],
                                      q=ch["qs"]: E.tensor_tensor(
                                          fcs, p, q, ALU.add),
                                      waits=wl, incs=[(A["aPHI"], 1)])
                        for ch in CH:
                            pend_av.append(
                                lambda pidx=ch["pidx"], k=k, qt=qt, sp=sp,
                                hh=ch["hh"], h=ch["h"], sb_=ch["sb"], kvb=kvb,
                                fq=first_of_quarter, nsp=nsp:
                                emit_av(pidx, k, qt, sp, hh, h, sb_, kvb, fq,
                                        nsp))
                            first_of_quarter = False
                        mk["kv"][kvi] = (A["aSIM"].n,
                                         A["aAV"].n + len(pend_av))
                    if sp + 1 < nsp:
                        pend_kv = kv_load(sp + 1)
                    yield 22.0
                while pend_av:
                    pend_av.pop(0)()
                avq = A["aAV"].n
                epi_wl = [(A["aAV"], avq), (A["aEPA"], mk["epa"]),
                          (A["aTP"], mk["tp"])]
                for hi in range(4):
                    pv = pav[:, hi * 256:(hi + 1) * 256].rearrange(
                        "p (ts d) -> p ts d", ts=4)
                    dvb = P["dvls"][:, 4 * qt:4 * qt + 4].unsqueeze(2).broadcast_to(
                        [128, 4, 64])
                    eb.op("vector", lambda E, hi=hi, pv=pv, dvb=dvb: E.tensor_tensor(
                        ao_hq[:, :, hi, :], pv, dvb, ALU.mult),
                        waits=(epi_wl if hi == 0 else []),
                        incs=([(A["aEPI"], 1)] if hi == 3 else []))
                epin = A["aEPI"].n
                mk["epi"] = epin
                eb.op("scalar", lambda E: E.activation(
                    scr1[:], ao_hq[:].rearrange("p a b c -> p (a b c)"), AF.Square),
                    waits=[(A["aEPI"], epin), (A["aEPV"], mk["epv"]),
                           (A["aSPD"], mk["sp_slice"].get(k - 1, 0))],
                    incs=[(A["aEPA"], 1)])
                mk["epa"] = A["aEPA"].n
                vcm = S["sVC"].n

                def epi_dve(E, epan=mk["epa"], vcm=vcm):
                    E.wait_ge(A["aEPA"].h, epan)
                    go = small_chain(E, S["sVC"], vcm)
                    go(lambda: E.reduce_sum(
                        est[:, 0:16].unsqueeze(2),
                        scr1[:].rearrange("p (g d) -> p g d", d=64), AX.X))
                    go(lambda: E.tensor_scalar(est[:, 16:32], est[:, 0:16],
                                               1.0 / 64, 1e-6, ALU.mult, ALU.add))
                    I32 = mybir.dt.int32
                    t1 = est[:, 32:48]
                    t2 = est[:, 48:64]
                    oi = t1.bitcast(I32)
                    go(lambda: E.tensor_scalar(oi, est[:, 16:32].bitcast(I32), 23,
                                               None, ALU.arith_shift_right))
                    go(lambda: E.tensor_scalar(oi, oi, -1, None, ALU.bitwise_xor))
                    go(lambda: E.tensor_scalar(oi, oi, 382, None, ALU.add))
                    go(lambda: E.tensor_scalar(oi, oi, 1, None, ALU.arith_shift_right))
                    go(lambda: E.tensor_scalar(oi, oi, 23, None, ALU.arith_shift_left))
                    for _ in range(5):
                        go(lambda: E.tensor_tensor(t2, t1, t1, ALU.mult))
                        go(lambda: E.tensor_tensor(t2, t2, est[:, 16:32], ALU.mult))
                        go(lambda: E.tensor_scalar(t2, t2, -0.5, 1.5, ALU.mult,
                                                   ALU.add))
                        go(lambda: E.tensor_tensor(t1, t1, t2, ALU.mult))
                    i = go(lambda: E.tensor_tensor(
                        ao_hq[:].rearrange("p ts h d -> p (ts h) d"),
                        ao_hq[:].rearrange("p ts h d -> p (ts h) d"),
                        est[:, 32:48].rearrange("p (ts h) -> p ts h", ts=4)
                        .rearrange("p ts h -> p (ts h)").unsqueeze(2)
                        .broadcast_to([128, 16, 64]), ALU.mult), inc=False)
                    i.then_inc(A["aEPV"].h, 1)
                eb.q["vector"].append(epi_dve)
                S["sVC"].n += 27
                A["aEPV"].n += 1
                mk["epv"] = A["aEPV"].n
                rmap8 = lambda j: (j % 2) * 4 + j // 2
                tpw0 = [(A["aEPV"], mk["epv"]), (A["aWO"], mk["wo_slice"].get(k - 1, 0))]
                for j in range(8):
                    ts, jj = divmod(j, 2)
                    jc = hq * 2 + jj
                    r = rmap8(j)
                    wl = list(tpw0) if j == 0 else []
                    if j >= 2:
                        wl.append((A["aTE"], mk["te"] + j - 1))
                    eb.op("tensor", lambda E, ts=ts, jj=jj, r=r: E.transpose(
                        pav[:, r * 128:(r + 1) * 128],
                        ao_hq[:, ts, jj * 2:jj * 2 + 2, :].rearrange(
                            "p h d -> p (h d)"),
                        P["sidF"][:]),
                        waits=wl, incs=[(A["aTP"], 1)])
                    eb.op("scalar", lambda E, ts=ts, jc=jc, r=r: E.activation(
                        ao_fm[:, jc, ts * 128:(ts + 1) * 128],
                        pav[:, r * 128:(r + 1) * 128], AF.Identity),
                        waits=[(A["aTP"], A["aTP"].n)], incs=[(A["aTE"], 1)])
                mk["tp"] = A["aTP"].n
                mk["te"] = A["aTE"].n
                yield 8.0
            evall = A["aEV"].n
            wo0 = A["aWO"].n
            for g in range(8):
                ts, ih = divmod(g, 2)
                bank = g % 2
                wl = []
                if g == 0:
                    wl = [(A["aEV"], evall), (A["aTE"], mk["te"]),
                          (A["aEPV"], mk["epv"])]
                if g >= 2:
                    wl.append((A["aWOE"], wo0 // 1 + g - 1 if False else
                               A["aWOE"].n - 1))
                for jc in range(8):
                    eb.op("tensor", lambda E, jc=jc, ts=ts, ih=ih, bank=bank:
                          E.matmul(psim[bank][:],
                                   ao_fm[:, jc, ts * 128:(ts + 1) * 128],
                                   P["WoTb"][:, jc, ih * 512:(ih + 1) * 512],
                                   start=(jc == 0), stop=(jc == 7)),
                          waits=(wl if jc == 0 else []),
                          incs=([(A["aWO"], 1)] if jc == 7 else []))
                won = A["aWO"].n
                spp = A["aSP0"] if g % 2 == 0 else A["aSP1"]
                wle = [(A["aWO"], won)]
                if g >= 2:
                    wle.append((spp, spp.n))
                eb.op("scalar", lambda E, bank=bank, g=g: E.activation(
                    scr1[:, (g % 2) * 512:(g % 2) * 512 + 512], psim[bank][:],
                    AF.Identity),
                    waits=wle, incs=[(A["aWOE"], 1)])
                woen = A["aWOE"].n
                r0 = t0 + ts * 128
                eb.op("scalar", lambda E, r0=r0, ih=ih, g=g: E.dma_start(
                    sS[r0:r0 + 128, ih * 512:(ih + 1) * 512],
                    scr1[:, (g % 2) * 512:(g % 2) * 512 + 512]),
                    waits=[(A["aWOE"], woen)], incs=[(spp, 16)])
            sp0n, sp1n = A["aSP0"].n, A["aSP1"].n

            def s_echo(E, sp0n=sp0n, sp1n=sp1n):
                E.wait_ge(A["aSP0"].h, sp0n)
                E.wait_ge(A["aSP1"].h, sp1n)
                i = E.copy(est[:, 0:1], est[:, 0:1])
                i.then_inc(A["aSPD"].h, 1)
            eb.q["scalar"].append(s_echo)
            A["aSPD"].n += 1
            mk["wo_slice"][k] = A["aWO"].n
            mk["woe_slice"][k] = A["aWOE"].n
            mk["sim_slice"][k] = A["aSIM"].n
            mk["sp_slice"][k] = A["aSPD"].n
            yield 6.0

        FWE = 48

        def ffn_tt(tt):
            xb = tt % 2
            fxp = A["fXN0"] if xb == 0 else A["fXN1"]
            wl = [(S[n], v) for n, v in P1B]
            if tt >= 2:
                wl.append((A["fPG"], mk["ffn_mm_tt"][tt - 2][0]))
                wl.append((A["fPU"], mk["ffn_mm_tt"][tt - 2][1]))
            eb.op("sync", lambda E, xb=xb, tt=tt: E.dma_start(
                xtile[xb][:],
                xnf.rearrange("c p t -> p c t")[:, :, tt * 512:(tt + 1) * 512]),
                waits=wl, incs=[(fxp, 16)])
            wlp = [(S["sPFM"], 16 * NCH_L)]
            if tt >= 1:
                wlp.append((A["fBI"], A["fBI"].n))
            eb.op("sync", lambda E, tt=tt: E.dma_start(
                pfm_sb[:].rearrange("e (a t) -> e a t", a=4),
                tens["pfm"][:, tt * 4:tt * 4 + 4, :]),
                waits=wlp, incs=[(A["fPF"], 16)])
            yield 2.0
            for e in range(NE):
                wl = [(S["sFW"], S["sFW"].n)]
                if not (tt == 0 and e == 0):
                    wl.append((A["fPY"], A["fPY"].n))
                eb.op("sync", lambda E, e=e: E.dma_start(
                    Dsb[:], dwb[e].rearrange("(f p) d -> p f d", p=128)),
                    waits=wl, incs=[(A["fD"], 16)])
                fdn = A["fD"].n
                gu_marks = {}
                FWALL = S["sFW"].n
                for fcp in range(2):
                    gs = A["fG0"] if fcp % 2 == 0 else A["fG1"]
                    us = A["fU0"] if fcp % 2 == 0 else A["fU1"]
                    wl = [(S["sFW"], FWALL)]
                    wlu = []
                    if not (tt == 0 and e == 0 and fcp < 2):
                        wl.append((A["fPG"], A["fPG"].n))
                        wlu.append((A["fPU"], A["fPU"].n))
                    eb.op("sync", lambda E, e=e, fcp=fcp: E.dma_start(
                        gch[fcp % 2][:],
                        gwb[e].rearrange("(c p) f -> p c f", p=128)[
                            :, :, fcp * 256:(fcp + 1) * 256]),
                        waits=wl, incs=[(gs, 16)])
                    eb.op("sync", lambda E, e=e, fcp=fcp: E.dma_start(
                        uch[fcp % 2][:],
                        uwb[e].rearrange("(c p) f -> p c f", p=128)[
                            :, :, fcp * 256:(fcp + 1) * 256]),
                        waits=wlu, incs=[(us, 16)])
                    gu_marks[fcp] = (gs, gs.n, us, us.n)
                for fc in range(16):
                    fcp = fc // 2
                    if fc % 2 == 0 and fc >= 2 and fcp + 1 < 8:
                        gs = A["fG0"] if (fcp + 1) % 2 == 0 else A["fG1"]
                        us = A["fU0"] if (fcp + 1) % 2 == 0 else A["fU1"]
                        eb.op("sync", lambda E, e=e, fcp=fcp: E.dma_start(
                            gch[(fcp + 1) % 2][:],
                            gwb[e].rearrange("(c p) f -> p c f", p=128)[
                                :, :, (fcp + 1) * 256:(fcp + 2) * 256]),
                            waits=[(A["fPG"], A["fPG"].n)], incs=[(gs, 16)])
                        eb.op("sync", lambda E, e=e, fcp=fcp: E.dma_start(
                            uch[(fcp + 1) % 2][:],
                            uwb[e].rearrange("(c p) f -> p c f", p=128)[
                                :, :, (fcp + 1) * 256:(fcp + 2) * 256]),
                            waits=[(A["fPU"], A["fPU"].n)], incs=[(us, 16)])
                        gu_marks[fcp + 1] = (gs, gs.n, us, us.n)
                    for dc in range(8):
                        wl = []
                        if dc == 0:
                            wl = [(gu_marks[fcp][0], gu_marks[fcp][1]),
                                  (fxp, 16 * (tt // 2 + 1)),
                                  (A["fSIL"], A["fSIL"].n)]
                        eb.op("tensor", lambda E, dc=dc, fc=fc, fcp=fcp, xb=xb:
                              E.matmul(pg[:],
                                       gch[fcp % 2][:, dc, (fc % 2) * 128:(fc % 2) * 128 + 128],
                                       xtile[xb][:, dc, :],
                                       start=(dc == 0), stop=(dc == 7)),
                              waits=wl, incs=([(A["fPG"], 1)] if dc == 7 else []))
                    pgn = A["fPG"].n
                    wl = [(A["fPG"], pgn), (A["fH"], max(0, A["fH"].n - 1))]
                    eb.op("scalar", lambda E, e=e, fc=fc: E.activation(
                        hs_scr[fc % 2][:], pg[:], AF.Silu,
                        bias=P["gbs"][:, e * 16 + fc:e * 16 + fc + 1]),
                        waits=wl, incs=[(A["fSIL"], 1)])
                    siln = A["fSIL"].n
                    for dc in range(8):
                        wl = []
                        if dc == 0:
                            wl = [(gu_marks[fcp][2], gu_marks[fcp][3]),
                                  (A["fH"], A["fH"].n)]
                        eb.op("tensor", lambda E, dc=dc, fc=fc, fcp=fcp, xb=xb:
                              E.matmul(pu[:],
                                       uch[fcp % 2][:, dc, (fc % 2) * 128:(fc % 2) * 128 + 128],
                                       xtile[xb][:, dc, :],
                                       start=(dc == 0), stop=(dc == 7)),
                              waits=wl, incs=([(A["fPU"], 1)] if dc == 7 else []))
                    pun = A["fPU"].n
                    wl = [(A["fPU"], pun), (A["fSIL"], siln)]
                    if tt > 0 or e > 0:
                        wl.append((A["fPY"], A["fPY"].n))
                    eb.op("vector", lambda E, e=e, fc=fc: E.scalar_tensor_tensor(
                        hsb[:, fc, :], pu[:],
                        P["ubs"][:, e * 16 + fc:e * 16 + fc + 1],
                        hs_scr[fc % 2][:], ALU.add, ALU.mult),
                        waits=wl, incs=[(A["fH"], 1)])
                    if fc % 4 == 3:
                        yield 7.0
                fhn = A["fH"].n
                if e == 0:
                    for g8 in range(8):
                        t128, dh = divmod(g8, 2)
                        ct = tt * 4 + t128
                        bank = g8 % 2
                        wl = []
                        if g8 == 0:
                            wl = [(A["fYE"], mk["ye_tt"].get(tt - 1, 0)),
                                  (A["fPF"], A["fPF"].n)]
                        if g8 >= 2:
                            wl.append((A["fBIC"], A["fBIC"].n - 1))
                        eb.op("tensor", lambda E, t128=t128, dh=dh, bank=bank: E.matmul(
                            py[bank][:], pfm_sb[0:8, t128 * 128:(t128 + 1) * 128],
                            P["dbs"][0:8, dh * 512:(dh + 1) * 512],
                            start=True, stop=True),
                            waits=wl, incs=[(A["fBI"], 1)])
                        bin_ = A["fBI"].n
                        wl2 = [(A["fBI"], bin_)]
                        if g8 == 0:
                            wl2.append((A["fGT"], mk["gt_tt"].get(tt - 1, 0)))
                        eb.op("vector", lambda E, t128=t128, dh=dh, bank=bank: E.tensor_copy(
                            y_acc[:, t128, dh * 512:(dh + 1) * 512], py[bank][:]),
                            waits=wl2, incs=[(A["fBIC"], 1)])
                    bic_all = A["fBIC"].n
                for g8 in range(8):
                    t128, dh = divmod(g8, 2)
                    ct = tt * 4 + t128
                    bank = g8 % 2
                    for fc in range(16):
                        wl = []
                        if fc == 0:
                            wl = [(A["fH"], fhn), (A["fD"], fdn),
                                  (A["fYE"], max(0, A["fYE"].n - 1))]
                            if e == 0:
                                wl.append((A["fBIC"], bic_all))
                        eb.op("tensor", lambda E, fc=fc, t128=t128, dh=dh,
                              bank=bank: E.matmul(
                                  py[bank][:],
                                  hsb[:, fc, t128 * 128:(t128 + 1) * 128],
                                  Dsb[:, fc, dh * 512:(dh + 1) * 512],
                                  start=(fc == 0), stop=(fc == 15)),
                              waits=wl, incs=([(A["fPY"], 1)] if fc == 15 else []))
                    pyn = A["fPY"].n
                    eb.op("vector", lambda E, t128=t128, dh=dh, bank=bank, ct=ct,
                          e=e: E.scalar_tensor_tensor(
                              y_acc[:, t128, dh * 512:(dh + 1) * 512],
                              py[bank][:], P["probs_res"][:, ct, e:e + 1],
                              y_acc[:, t128, dh * 512:(dh + 1) * 512],
                              ALU.mult, ALU.add),
                          waits=[(A["fPY"], pyn)], incs=[(A["fYE"], 1)])
                    if g8 % 2 == 1:
                        yield 7.0
            mk["ffn_mm_tt"][tt] = (A["fPG"].n, A["fPU"].n)
            mk["ye_tt"][tt] = A["fYE"].n

        def ffn_tail(tt):
            yeall = mk["ye_tt"][tt]
            fhn = A["fH"].n
            assert tt in mk["sp_slice"], f"attn slice {tt} must precede ffn tail"
            spm = mk["sp_slice"][tt]
            for i in range(8):
                t128, i2 = divmod(i, 2)
                r0 = tt * 512 + t128 * 128
                cs = i2 * 512
                wl = [(A["fYE"], yeall), (A["fCT"], max(0, A["fCT"].n - 1))]
                if i == 0:
                    wl.append((A["fH"], fhn))
                eb.op("scalar", lambda E, t128=t128, cs=cs, i=i: E.activation(
                    hs_scr[i % 2][:], y_acc[:, t128, cs:cs + 512], AF.Silu),
                    waits=wl, incs=[(A["fGT"], 1)])
                gtn = A["fGT"].n
                fxx = A["fXSx0"] if i % 2 == 0 else A["fXSx1"]
                fxss = A["fXSs0"] if i % 2 == 0 else A["fXSs1"]
                wl = [(A["fCT"], max(0, A["fCT"].n - 1))]
                eb.op("sync", lambda E, r0=r0, cs=cs, i=i: E.dma_start(
                    xts[i % 2][:], x[r0:r0 + 128, cs:cs + 512]),
                    waits=wl, incs=[(fxx, 16)])
                eb.op("sync", lambda E, r0=r0, cs=cs, i=i: E.dma_start(
                    sts[i % 2][:], sS[r0:r0 + 128, cs:cs + 512]),
                    waits=[(A["aSPD"], spm), (A["fCT"], max(0, A["fCT"].n - 1))],
                    incs=[(fxss, 16)])
                xsn = (fxx, fxx.n, fxss, fxss.n)
                vcc = S["sVC"].n

                fop = A["fOUT0"] if i % 2 == 0 else A["fOUT1"]

                def comb(E, i=i, gtn=gtn, xsn=xsn, vcc=vcc,
                         fo=fop.n, fop=fop):
                    E.wait_ge(A["fGT"].h, gtn)
                    E.wait_ge(xsn[0].h, xsn[1])
                    E.wait_ge(xsn[2].h, xsn[3])
                    if fo:
                        E.wait_ge(fop.h, fo)
                    go = small_chain(E, S["sVC"], vcc)
                    go(lambda: E.tensor_tensor(osg[i % 2][:], sts[i % 2][:],
                                               hs_scr[i % 2][:], ALU.mult))
                    ii = go(lambda: E.tensor_tensor(osg[i % 2][:], osg[i % 2][:],
                                                    xts[i % 2][:], ALU.add),
                            inc=False)
                    ii.then_inc(A["fCT"].h, 1)
                eb.q["vector"].append(comb)
                S["sVC"].n += 1
                A["fCT"].n += 1
                ctn = A["fCT"].n
                eb.op("scalar", lambda E, r0=r0, cs=cs, i=i: E.dma_start(
                    out[r0:r0 + 128, cs:cs + 512], osg[i % 2][:]),
                    waits=[(A["fCT"], ctn)], incs=[(fop, 16)])
                yield 2.0
            mk["ct_tt"][tt] = A["fCT"].n
            mk["gt_tt"][tt] = A["fGT"].n

        only_attn = (phases == 2)
        for k in (SLICES if SLICES is not None else range(NSLICE)):
            a = attn_slice(k)
            f = None if only_attn else ffn_tt(k)
            a_done = False
            f_done = only_attn
            ta = tf = 0.0
            while not (a_done and f_done):
                if not a_done and (f_done or ta <= tf):
                    try:
                        ta += next(a)
                    except StopIteration:
                        a_done = True
                else:
                    try:
                        tf += next(f)
                    except StopIteration:
                        f_done = True
            if not only_attn:
                for _ in ffn_tail(k):
                    pass

        fins = [(A["fOUT0"], A["fOUT0"].n), (A["fOUT1"], A["fOUT1"].n),
                (A["aSP0"], A["aSP0"].n), (A["aSP1"], A["aSP1"].n)]

        def fin(E, fins=fins):
            for s, v in fins:
                if v:
                    E.wait_ge(s.h, v)
        eb.q["sync"].append(fin)
        eb.emit(nc)


_BF = ml_dtypes.bfloat16


def host_inputs(inputs):
    f32 = np.float32
    W = {}
    W["wqT"] = np.ascontiguousarray(inputs["L0_attn_Wq"].T.astype(f32, copy=False))
    W["wkT"] = np.ascontiguousarray(inputs["L0_attn_Wk"].T.astype(f32, copy=False))
    W["wvT"] = np.ascontiguousarray(inputs["L0_attn_Wv"].T.astype(f32, copy=False))
    W["woT"] = np.ascontiguousarray(inputs["L0_attn_Wo"].T.astype(f32, copy=False))
    W["rwT"] = np.ascontiguousarray(inputs["L0_ffn_router_w"].T.astype(f32, copy=False))
    W["polyT"] = np.ascontiguousarray(
        inputs["L0_attn_poly_weights"].T.astype(f32, copy=False))
    W["hdrow"] = np.ascontiguousarray(
        inputs["L0_attn_head_degrees"].reshape(1, 16).astype(f32, copy=False))
    nw = inputs["L0_attn_norm_w"].astype(f32, copy=False)
    W["nw2"] = np.ascontiguousarray(np.concatenate([nw, nw]).reshape(128, 1))
    W["gw"] = np.ascontiguousarray(inputs["L0_ffn_gate_w"].astype(f32, copy=False))
    W["uw"] = np.ascontiguousarray(inputs["L0_ffn_up_w"].astype(f32, copy=False))
    W["dw"] = np.ascontiguousarray(inputs["L0_ffn_down_w"].astype(f32, copy=False))
    W["gbl"] = np.ascontiguousarray(
        inputs["L0_ffn_gate_b"].reshape(NE, 16, 128).transpose(2, 0, 1)
        .reshape(128, 128).astype(f32, copy=False))
    W["ubl"] = np.ascontiguousarray(
        inputs["L0_ffn_up_b"].reshape(NE, 16, 128).transpose(2, 0, 1)
        .reshape(128, 128).astype(f32, copy=False))
    W["db"] = np.ascontiguousarray(inputs["L0_ffn_down_b"].astype(f32, copy=False))
    W["identF"] = np.eye(128, dtype=f32)
    W["identB"] = np.eye(128, dtype=_BF)
    W["rmat"] = gegen_monomials()
    W["kcol"] = np.arange(9, dtype=f32).reshape(9, 1)
    p = np.arange(128, dtype=np.float64)
    j = np.arange(16, dtype=np.float64)
    W["dvl"] = (1.0 / (j[None, :] * 128 + p[:, None] + 1.0)).astype(f32)
    dm = np.zeros((128, 2, 2, 512), np.float32)
    for v in range(2):
        for kcw in range(2):
            thr = (2 * v + kcw) * 128 + p[:, None]
            dm[:, v, kcw, :] = (np.arange(512)[None, :] >= thr)
    W["dmc"] = np.ascontiguousarray(dm.reshape(128, 2048).astype(_BF))
    return W


def kernel(**inputs):
    nc, _ = build_nc(debug=DEBUG, phases=BUILD_PHASES)
    W = host_inputs(inputs)
    xf = inputs["L0_attn_x"].astype(np.float32, copy=False)
    in_maps = []
    for c in range(NCORE):
        m = dict(W)
        m["x"] = np.ascontiguousarray(xf[c * 4:(c + 1) * 4].reshape(TOK, DD))
        in_maps.append(m)
    res = run_bass_kernel_spmd(nc, in_maps, core_ids=list(range(NCORE)))
    outs = [np.asarray(r["out"]) for r in res.results]
    return np.concatenate(outs, axis=0).reshape(32, 2048, 1024)


# version 07
# speedup vs baseline: 1.0475x; 1.0475x over previous
import numpy as np
import ml_dtypes
import concourse.bass as bass
import concourse.mybir as mybir
from concourse.bass_utils import run_bass_kernel_spmd
from contextlib import ExitStack

F32 = mybir.dt.float32
BF16 = mybir.dt.bfloat16
AF = mybir.ActivationFunctionType
ALU = mybir.AluOpType
AX = mybir.AxisListType

NCORE = 8
TOK = 8192
NCH = 64
DD = 1024
DFF = 2048
NE = 8
NSLICE = 16
LAM = 31.5

BUILD_PHASES = 3
DEBUG = 0
NCH_L = NCH
SLICES = None


def gegen_monomials():
    R = [np.zeros(9, np.float64) for _ in range(9)]
    R[0][0] = 1.0
    R[1][1] = 1.0
    for k in range(2, 9):
        c1 = 2.0 * (k + LAM - 1.0) / (k + 2.0 * LAM - 1.0)
        c2 = (k - 1.0) / (k + 2.0 * LAM - 1.0)
        R[k] = c1 * np.concatenate([[0.0], R[k - 1][:-1]]) - c2 * R[k - 2]
    return np.stack(R).astype(np.float32)


MAGIC = 0x5F3759DF


def emit_rsqrt(E, out_f32, in_f32, t2_f32, newton=2):
    I32 = mybir.dt.int32
    oi = out_f32.bitcast(I32)
    E.tensor_scalar(oi, in_f32.bitcast(I32), 1, None, ALU.arith_shift_right)
    E.tensor_scalar(oi, oi, -1, None, ALU.bitwise_xor)
    E.tensor_scalar(oi, oi, MAGIC + 1, None, ALU.add)
    for _ in range(newton):
        E.tensor_tensor(t2_f32, out_f32, out_f32, ALU.mult)
        E.tensor_tensor(t2_f32, t2_f32, in_f32, ALU.mult)
        E.tensor_scalar(t2_f32, t2_f32, -0.5, 1.5, ALU.mult, ALU.add)
        E.tensor_tensor(out_f32, out_f32, t2_f32, ALU.mult)


class Sem:
    __slots__ = ("h", "n", "name")

    def __init__(self, nc, ctx, name):
        self.h = ctx.enter_context(nc.semaphore(name))
        self.n = 0
        self.name = name


class EB:

    def __init__(self):
        self.q = {"tensor": [], "vector": [], "scalar": [], "gpsimd": [], "sync": []}

    def op(self, eng, build, waits=(), incs=()):
        for s, v in waits:
            assert 0 <= v <= s.n, f"wait {s.name}>={v} pledged {s.n}"
        for s, n in incs:
            s.n += n

        def f(E, build=build, waits=tuple(waits), incs=tuple(incs)):
            for s, v in waits:
                if v > 0:
                    E.wait_ge(s.h, v)
            inst = build(E)
            for s, n in incs:
                inst.then_inc(s.h, n)
        self.q[eng].append(f)

    def emit(self, nc):
        with nc.Block() as blk:
            for name, fns in self.q.items():
                if fns:
                    getattr(blk, name)(lambda E, fns=fns: [f(E) for f in fns])


def small_chain(E, sem, n_start):
    state = {"n": n_start}

    def go(make, inc=True):
        E.wait_ge(sem.h, state["n"])
        inst = make()
        if inc:
            inst.then_inc(sem.h, 1)
            state["n"] += 1
        return inst
    return go


def build_nc(debug=0, phases=3):
    nc = bass.Bass("TRN2", target_bir_lowering=False)

    def din(name, shape, dt=F32):
        return nc.dram_tensor(name, shape, dt, kind="ExternalInput").ap()

    def dout(name, shape, dt=F32):
        return nc.dram_tensor(name, shape, dt, kind="ExternalOutput").ap()

    dbg_names = ("xnf", "qf", "kf", "vt", "sS") if debug else ()

    def dint(name, shape, dt):
        kind = "ExternalOutput" if name in dbg_names else "Internal"
        return nc.dram_tensor(name, shape, dt, kind=kind).ap()

    x = din("x", [TOK, DD])
    wqT = din("wqT", [DD, DD]); wkT = din("wkT", [DD, DD])
    wvT = din("wvT", [DD, DD]); woT = din("woT", [DD, DD])
    rwT = din("rwT", [DD, NE])
    polyT = din("polyT", [9, 16]); hdrow = din("hdrow", [1, 16])
    nw2 = din("nw2", [128, 1])
    gwi = din("gw", [NE, DD, DFF]); uwi = din("uw", [NE, DD, DFF])
    dwi = din("dw", [NE, DFF, DD])
    gbl = din("gbl", [128, 128]); ubl = din("ubl", [128, 128])
    dbi = din("db", [NE, DD])
    identF = din("identF", [128, 128]); identB = din("identB", [128, 128], BF16)
    rmat = din("rmat", [9, 9]); kcol = din("kcol", [9, 1])
    dvl = din("dvl", [128, 16]); dmc = din("dmc", [128, 2048], BF16)

    out = dout("out", [TOK, DD])
    dbgt = {}
    if debug:
        dbgt["d_probs"] = dout("d_probs", [128, NCH, NE])
        dbgt["d_lns"] = dout("d_lns", [NCH, 128, 12])

    xnf = dint("xnf", [8, 128, TOK], BF16)
    qf = dint("qf", [8, 128, TOK], BF16)
    kf = dint("kf", [8, 128, TOK], BF16)
    vt = dint("vt", [TOK, DD], BF16)
    sS = dint("sS", [TOK, DD], F32)
    gwb = dint("gwb", [NE, DD, DFF], BF16)
    uwb = dint("uwb", [NE, DD, DFF], BF16)
    dwb = dint("dwb", [NE, DFF, DD], BF16)
    aTd = dint("aTd", [9, 16], F32)
    pfm = dint("pfm", [NE, NCH, 128], BF16)
    tens = dict(x=x, out=out, xnf=xnf, qf=qf, kf=kf, vt=vt, sS=sS,
                gwb=gwb, uwb=uwb, dwb=dwb, pfm=pfm)

    with ExitStack() as pctx:
        def psb(name, shape, dt=F32):
            return pctx.enter_context(nc.sbuf_tensor(name, shape, dt))

        P = {}
        for name, shape, dt in [
                ("ones_r", [1, 144], F32), ("sidF", [128, 128], F32),
                ("sidB", [128, 128], BF16), ("A_bc", [128, 144], F32),
                ("A_bcb", [128, 144], BF16),
                ("dvls", [128, 16], F32), ("dms", [128, 2048], BF16),
                ("WoTb", [128, 8, DD], BF16), ("probs_res", [128, NCH, NE], F32),
                ("pfm_st", [8, 128], BF16), ("gbs", [128, 128], F32),
                ("ubs", [128, 128], F32), ("dbs", [8, DD], BF16),
                ("nws", [128, 1], F32)]:
            P[name] = psb(name, shape, dt)

        S = {}
        for name in ["sC0", "sFW", "sAW", "gPE", "gV", "sEPC", "sVAR", "sVC",
                     "sX0", "sX1", "sST", "sSQ", "sNRM", "sXP", "sXE", "sXC",
                     "sRT", "sPR", "sPTR", "sPFM", "sWST", "sGD",
                     "sQM", "sQA", "sQV", "sQEq", "sQEv", "sQTP", "sQTE",
                     "sSPV0", "sSPV1", "sSPx", "sSPq", "sSPk", "sPFC", "fPF",
                     "sEC0", "sEC1", "sEC2",
                     "aQ", "aK0", "aK1", "aV0", "aV1", "aSIM", "aEV", "aPQ",
                     "aPHI", "aAV", "aEPI", "aEPV", "aEPA", "aTP", "aTE",
                     "aWO", "aWOE", "aSP0", "aSP1", "aSPD",
                     "fXN0", "fXN1", "fG0", "fG1", "fU0", "fU1", "fD", "fPG", "fPU", "fSIL",
                     "fH", "fPY", "fYE", "fBI", "fBIC", "fXSx0", "fXSx1", "fXSs0", "fXSs1", "fGT", "cACT", "cPOOL", "cDVE",
                     "fCT", "fOUT0", "fOUT1", "dC"]:
            S[name] = Sem(nc, pctx, name)

        consts = dict(identF=identF, identB=identB, rmat=rmat, kcol=kcol,
                      dvl=dvl, dmc=dmc, gbl=gbl, ubl=ubl, db=dbi, nw2=nw2,
                      polyT=polyT, hdrow=hdrow, rwT=rwT,
                      wqT=wqT, wkT=wkT, wvT=wvT, woT=woT,
                      gwi=gwi, uwi=uwi, dwi=dwi, aTd=aTd)
        meta = build_phase01(nc, S, P, tens, consts, dbgt, debug, phases)
        if phases >= 2:
            build_phase23(nc, S, P, tens, meta, dbgt, debug, phases)

    return nc, dbgt


def build_phase01(nc, S, P, tens, C, dbgt, debug, phases):
    x, xnf, qf, kf, vt = tens["x"], tens["xnf"], tens["qf"], tens["kf"], tens["vt"]
    with ExitStack() as c1:
        def sb1(name, shape, dt=F32):
            return c1.enter_context(nc.sbuf_tensor(name, shape, dt))

        def ps1(name, shape, dt=F32):
            return c1.enter_context(nc.psum_tensor(name, shape, dt))

        WqTb = sb1("WqTb", [128, 8, DD], BF16)
        WkTb = sb1("WkTb", [128, 8, DD], BF16)
        WvTb = sb1("WvTb", [128, 8, DD], BF16)
        wstage = sb1("wstage", [128, DD])
        rwTs = sb1("rwTs", [128, 8, NE])
        ptys = sb1("ptys", [9, 16]); hds = sb1("hds", [1, 16])
        rms = sb1("rms", [9, 9]); kcs = sb1("kcs", [9, 1])
        gats = sb1("gats", [9, 16]); wgs = sb1("wgs", [9, 16])
        aTs = sb1("aTs", [9, 16]); arow = sb1("arow", [1, 144])
        sx = [sb1(f"sx{i}", [128, DD]) for i in range(2)]
        lns = sb1("lns", [128, 12])
        sqh = [sb1(f"sqh{i}", [128, 512]) for i in range(2)]
        sxn = [sb1(f"sxn{i}", [128, DD]) for i in range(2)]
        xnfm = [sb1(f"xnfm{i}", [128, 8, 128]) for i in range(2)]
        xnstage = sb1("xnstage", [128, 8, 1024], BF16)
        qstage = sb1("qstage", [128, 8, 1024], BF16)
        kstage = sb1("kstage", [128, 8, 1024], BF16)
        qtb = sb1("qtb", [128, DD], BF16)
        ktb = sb1("ktb", [128, DD], BF16)
        vtb = [sb1(f"vtb{i}", [128, DD], BF16) for i in range(2)]
        qks = sb1("qks", [128, 32])
        prw = sb1("prw", [128, 24])

        ptr = ps1("ptr", [128, 8, 128])
        pqk = [ps1(f"pqk{i}", [128, 512]) for i in range(2)]
        psm = ps1("psm", [128, 160])
        pqb = ps1("pqb", [128, 16, 128], BF16)

        eb = EB()

        for dst, src in [(P["sidF"][:], C["identF"]), (P["sidB"][:], C["identB"]),
                         (P["dvls"][:], C["dvl"]), (P["dms"][:], C["dmc"]),
                         (P["gbs"][:], C["gbl"]), (P["ubs"][:], C["ubl"]),
                         (P["nws"][:], C["nw2"]),
                         (ptys[:], C["polyT"]), (hds[:], C["hdrow"]),
                         (rms[:], C["rmat"]), (kcs[:], C["kcol"]),
                         (rwTs[:],
                          C["rwT"].rearrange("(c p) e -> p c e", p=128))]:
            eb.op("sync", lambda E, d=dst, s=src: E.dma_start(d, s),
                  waits=[(S["sC0"], S["sC0"].n)], incs=[(S["sC0"], 16)])
        C0 = S["sC0"].n

        for e in range(NE):
            for dsrc, ddst in ((C["gwi"], tens["gwb"]), (C["uwi"], tens["uwb"]),
                               (C["dwi"], tens["dwb"])):
                eb.op("gpsimd", lambda E, a=ddst[e], b=dsrc[e]: E.dma_start(a, b),
                      waits=[(S["sFW"], S["sFW"].n)], incs=[(S["sFW"], 16)])

        eb.op("gpsimd", lambda E: E.dma_start(P["dbs"][:], C["db"]),
              waits=[(S["sAW"], S["sAW"].n)], incs=[(S["sAW"], 16)])
        for dst, src in ((WqTb, C["wqT"]), (WkTb, C["wkT"]), (WvTb, C["wvT"])):
            eb.op("gpsimd", lambda E, d=dst, s=src: E.dma_start(
                d[:], s.rearrange("(c p) f -> p c f", p=128)),
                waits=[(S["sAW"], S["sAW"].n)], incs=[(S["sAW"], 16)])
        CASTW = S["sAW"].n

        for jc in range(8):
            wl = [] if jc == 0 else [(S["gV"], jc)]
            eb.op("sync", lambda E, jc=jc: E.dma_start(
                wstage[:], C["woT"][jc * 128:(jc + 1) * 128, :]),
                waits=wl, incs=[(S["sWST"], 16)])
            eb.op("vector", lambda E, jc=jc: E.tensor_scalar_mul(
                P["WoTb"][:, jc, :], wstage[:], P["nws"][:, 0:1]),
                waits=[(S["sWST"], (jc + 1) * 16), (S["sC0"], C0)],
                incs=[(S["gV"], 1)])
        C1 = S["sC0"].n

        eb.op("vector", lambda E: E.memset(P["ones_r"][:], 1.0), incs=[(S["gV"], 1)])
        gv0 = S["gV"].n
        eb.op("tensor", lambda E: E.matmul(psm[0:9, 0:16], P["ones_r"][0:1, 0:9],
                                           hds[:], start=True, stop=True),
              waits=[(S["sC0"], C0), (S["gV"], gv0)], incs=[(S["gPE"], 1)])
        vc0 = S["sVC"].n

        def gchain(E, vc0=vc0):
            E.wait_ge(S["gPE"].h, 1)
            go = small_chain(E, S["sVC"], vc0)
            go(lambda: E.tensor_scalar(gats[:], psm[0:9, 0:16], kcs[:, 0:1], 1.0,
                                       ALU.subtract, ALU.add))
            go(lambda: E.tensor_scalar(gats[:], gats[:], 0.0, 1.0, ALU.max, ALU.min))
            i = go(lambda: E.tensor_tensor(wgs[:], ptys[:], gats[:], ALU.mult),
                   inc=False)
            i.then_inc(S["gV"].h, 3)
        eb.q["vector"].append(gchain)
        S["sVC"].n += 2
        S["gV"].n += 3
        eb.op("tensor", lambda E: E.matmul(psm[0:9, 16:32], rms[:], wgs[:],
                                           start=True, stop=True),
              waits=[(S["gV"], gv0 + 3)], incs=[(S["gPE"], 1)])
        eb.op("vector", lambda E: E.tensor_copy(aTs[:], psm[0:9, 16:32]),
              waits=[(S["gPE"], 2)], incs=[(S["gV"], 1)])
        eb.op("sync", lambda E: E.dma_start(C["aTd"], aTs[:]),
              waits=[(S["gV"], gv0 + 4)], incs=[(S["sGD"], 16)])
        eb.op("sync", lambda E: E.dma_start(
            arow[:].rearrange("a (h j) -> a h j", j=9),
            C["aTd"].transpose([1, 0]).unsqueeze(0),
            allow_slow_non_contiguous=True),
            waits=[(S["sGD"], 16)], incs=[(S["sGD"], 16)])
        eb.op("tensor", lambda E: E.matmul(psm[:, 16:160], P["ones_r"][0:1, 0:128],
                                           arow[:], start=True, stop=True),
              waits=[(S["sGD"], 32), (S["gV"], gv0)], incs=[(S["gPE"], 1)])
        eb.op("vector", lambda E: E.tensor_copy(P["A_bc"][:], psm[:, 16:160]),
              waits=[(S["gPE"], 3)], incs=[(S["gV"], 1)])
        eb.op("vector", lambda E: E.tensor_copy(P["A_bcb"][:], psm[:, 16:160]),
              incs=[(S["gV"], 1)])

        deltas0 = None
        spill_after = {}
        for ct in range(NCH_L):
            b = ct % 2
            tile, sl = ct // 8, ct % 8
            base = {k: S[k].n for k in S}
            epw = [(S["sEC0"], max(0, ct - 1)), (S["sEC1"], max(0, ct - 1)),
                   (S["sEC2"], max(0, ct - 1))]

            sxp = S["sX0"] if b == 0 else S["sX1"]
            eb.op("sync", lambda E, ct=ct, b=b: E.dma_start(
                sx[b][:], x[ct * 128:(ct + 1) * 128, :]),
                waits=[(S["sST"], max(0, ct - 1))], incs=[(sxp, 16)])

            eb.op("scalar", lambda E, b=b: E.activation(sqh[0][:], sx[b][:, 0:512],
                                                        AF.Square),
                  waits=epw + [(sxp, 16 * (ct // 2 + 1)),
                               (S["sQV"], max(0, 5 * ct))],
                  incs=[(S["sSQ"], 1)])
            eb.op("scalar", lambda E, b=b: E.activation(sqh[1][:], sx[b][:, 512:1024],
                                                        AF.Square),
                  incs=[(S["sSQ"], 1)])

            vcs = S["sVC"].n

            def ln_stats(E, ct=ct, b=b, epv=epw[0][1], vcs=vcs, sxp=sxp):
                E.wait_ge(S["sEC0"].h, epv)
                E.wait_ge(S["sEC1"].h, epv)
                E.wait_ge(S["sEC2"].h, epv)
                E.wait_ge(sxp.h, 16 * (ct // 2 + 1))
                E.wait_ge(S["sSQ"].h, 2 * (ct + 1))
                go = small_chain(E, S["sVC"], vcs)
                E.reduce_sum(lns[:, 0:1], sx[b][:], AX.X)
                E.reduce_sum(lns[:, 1:2], sqh[0][:], AX.X)
                go(lambda: E.reduce_sum(lns[:, 2:3], sqh[1][:], AX.X))
                go(lambda: E.tensor_scalar_mul(lns[:, 3:4], lns[:, 0:1], 1.0 / DD))
                go(lambda: E.tensor_add(lns[:, 4:5], lns[:, 1:2], lns[:, 2:3]))
                go(lambda: E.tensor_scalar_mul(lns[:, 5:6], lns[:, 4:5], 1.0 / DD))
                go(lambda: E.tensor_mul(lns[:, 6:7], lns[:, 3:4], lns[:, 3:4]))
                go(lambda: E.scalar_tensor_tensor(lns[:, 7:8], lns[:, 6:7], -1.0,
                                                  lns[:, 5:6], ALU.mult, ALU.add))
                i = go(lambda: E.tensor_scalar_add(lns[:, 8:9], lns[:, 7:8], 1e-5),
                       inc=False)
                i.then_inc(S["sVAR"].h, 1)
            eb.q["vector"].append(ln_stats)
            S["sVC"].n += 6
            S["sVAR"].n += 1
            eb.op("scalar", lambda E: E.activation(lns[:, 9:10], lns[:, 8:9], AF.Ln),
                  waits=[(S["sVAR"], ct + 1)], incs=[(S["cACT"], 1)])
            eb.op("scalar", lambda E: E.activation(lns[:, 10:11], lns[:, 9:10],
                                                   AF.Exp, scale=-0.5),
                  waits=[(S["cACT"], S["cACT"].n)], incs=[(S["sQA"], 1)])
            qa_rstd = S["sQA"].n

            vcn = S["sVC"].n

            def ln_nbias(E, b=b, qa_rstd=qa_rstd):
                E.wait_ge(S["sQA"].h, qa_rstd)
                i = E.scalar_tensor_tensor(lns[:, 11:12], lns[:, 3:4], -1.0,
                                           lns[:, 10:11], ALU.mult, ALU.mult)
                i.then_inc(S["sQV"].h, 1)
            eb.q["vector"].append(ln_nbias)
            S["sQV"].n += 1
            nb_now = S["sQV"].n
            eb.op("scalar", lambda E, b=b: E.activation(
                sxn[b][:], sx[b][:], AF.Identity, bias=lns[:, 11:12],
                scale=lns[:, 10:11]),
                waits=[(S["sQV"], nb_now), (S["cACT"], S["cACT"].n)],
                incs=[(S["sST"], 1)])
            S["sNRM"].n += 1
            if debug:
                eb.op("scalar", lambda E, ct=ct: E.dma_start(
                    dbgt["d_lns"][ct], lns[:]),
                    waits=[(S["sST"], ct + 1)], incs=[(S["dC"], 16)])

            rmap = lambda j: (j % 2) * 4 + j // 2
            for dc in range(8):
                wl = []
                if dc == 0:
                    wl = epw + [(S["sST"], ct + 1), (S["sXE"], max(0, ct * 8))]
                elif dc >= 2:
                    wl = [(S["sXE"], ct * 8 + dc - 1)]
                eb.op("tensor", lambda E, b=b, dc=dc, r=rmap(dc): E.transpose(
                    ptr[:, r, :], sxn[b][:, dc * 128:(dc + 1) * 128], P["sidF"][:]),
                    waits=wl, incs=[(S["sXP"], 1)])
                eb.op("scalar", lambda E, b=b, dc=dc, r=rmap(dc): E.activation(
                    xnfm[b][:, dc, :], ptr[:, r, :], AF.Identity),
                    waits=[(S["sXP"], ct * 8 + dc + 1)] +
                          ([(S["sXC"], max(0, ct - 1))] if dc == 0 else []),
                    incs=[(S["sXE"], 1)])
            eb.op("vector", lambda E, b=b, sl=sl: E.tensor_copy(
                xnstage[:, :, sl * 128:(sl + 1) * 128], xnfm[b][:]),
                waits=[(S["sXE"], (ct + 1) * 8),
                       (S["sSPx"], spill_after.get(tile - 1, 0) if sl == 0 else 0),
                       (S["sQM"], max(0, (ct - 7) * 6))],
                incs=[(S["sXC"], 1)])

            for dc in range(8):
                eb.op("tensor", lambda E, b=b, dc=dc: E.matmul(
                    psm[:, 0:8], xnfm[b][:, dc, :], rwTs[:, dc, :],
                    start=(dc == 0), stop=(dc == 7)),
                    waits=([(S["sXE"], ct * 8 + 8), (S["sPFC"], max(0, ct))]
                           if dc == 0 else []),
                    incs=([(S["sRT"], 1)] if dc == 7 else []))

            vpa = S["sVC"].n

            def probs_a(E, ct=ct, vpa=vpa):
                E.wait_ge(S["sRT"].h, ct + 1)
                go = small_chain(E, S["sVC"], vpa)
                E.tensor_copy(prw[:, 16:24], psm[:, 0:8])
                go(lambda: E.reduce_max(prw[:, 8:9], psm[:, 0:8], AX.X))
                i = go(lambda: E.tensor_scalar_mul(prw[:, 9:10], prw[:, 8:9], -1.0),
                       inc=False)
                i.then_inc(S["sPR"].h, 1)
            eb.q["vector"].append(probs_a)
            S["sVC"].n += 1
            S["sPR"].n += 1
            eb.op("scalar", lambda E: E.activation(prw[:, 0:8], prw[:, 16:24],
                                                   AF.Exp, bias=prw[:, 9:10]),
                  waits=[(S["sPR"], base["sPR"] + 1)], incs=[(S["sQA"], 1)])
            qa_pe = S["sQA"].n

            vpb = S["sVC"].n

            def probs_b(E, ct=ct, qa=qa_pe, vpb=vpb):
                E.wait_ge(S["sQA"].h, qa)
                go = small_chain(E, S["sVC"], vpb)
                go(lambda: E.reduce_sum(prw[:, 8:9], prw[:, 0:8], AX.X))
                go(lambda: E.reciprocal(prw[:, 9:10], prw[:, 8:9]))
                go(lambda: E.tensor_scalar_mul(prw[:, 0:8], prw[:, 0:8],
                                               prw[:, 9:10]))
                go(lambda: E.reduce_max(prw[:, 8:9], prw[:, 0:8], AX.X))
                go(lambda: E.tensor_scalar(prw[:, 10:18], prw[:, 0:8], prw[:, 8:9],
                                           None, ALU.is_ge))
                go(lambda: E.scalar_tensor_tensor(prw[:, 10:18], prw[:, 10:18],
                                                  -1e30, prw[:, 0:8],
                                                  ALU.mult, ALU.add))
                go(lambda: E.reduce_max(prw[:, 8:9], prw[:, 10:18], AX.X))
                go(lambda: E.tensor_scalar(prw[:, 10:18], prw[:, 0:8], prw[:, 8:9],
                                           None, ALU.is_ge))
                go(lambda: E.tensor_tensor(prw[:, 0:8], prw[:, 0:8], prw[:, 10:18],
                                           ALU.mult))
                go(lambda: E.reduce_sum(prw[:, 8:9], prw[:, 0:8], AX.X))
                go(lambda: E.tensor_scalar_add(prw[:, 8:9], prw[:, 8:9], 1e-9))
                go(lambda: E.reciprocal(prw[:, 9:10], prw[:, 8:9]))
                i = go(lambda: E.tensor_scalar_mul(P["probs_res"][:, ct, :],
                                                   prw[:, 0:8], prw[:, 9:10]),
                       inc=False)
                i.then_inc(S["sPR"].h, 1)
            eb.q["vector"].append(probs_b)
            S["sVC"].n += 12
            S["sPR"].n += 1
            eb.op("tensor", lambda E, ct=ct: E.transpose(
                psm[0:8, 16:144], P["probs_res"][:, ct, :], P["sidF"][:]),
                waits=[(S["sPR"], base["sPR"] + 2), (S["sPFC"], max(0, ct))],
                incs=[(S["sPTR"], 1)])
            eb.op("vector", lambda E, ct=ct: E.tensor_copy(
                P["pfm_st"][:], psm[0:8, 16:144]),
                waits=[(S["sPTR"], ct + 1), (S["sPFM"], max(0, 16 * ct))],
                incs=[(S["sPFC"], 1)])
            eb.op("scalar", lambda E, ct=ct: E.dma_start(
                tens["pfm"][:, ct, :], P["pfm_st"][:]),
                waits=[(S["sPFC"], ct + 1)], incs=[(S["sPFM"], 16)])

            qv_marks = []
            for mi, (Wb, kind) in enumerate(((WqTb, "q"), (WkTb, "k"), (WvTb, "v"))):
                for hf in range(2):
                    idx = 6 * ct + 2 * mi + hf
                    lidx = 2 * mi + hf
                    bank = idx % 2
                    for dc in range(8):
                        wl = []
                        if dc == 0:
                            wl = [(S["sXC"], ct + 1), (S["sAW"], CASTW)]
                            li2 = lidx - 2
                            if li2 >= -2 and idx >= 2:
                                if li2 >= 0:
                                    if li2 < 4:
                                        wl.append((S["sQEq"], 4 * ct + li2 + 1))
                                    else:
                                        wl.append((S["sQEv"], 2 * ct + li2 - 3))
                                else:
                                    wl.append((S["sQEv"], 2 * (ct - 1) + li2 + 3))
                        eb.op("tensor", lambda E, sl=sl, dc=dc, Wb=Wb, hf=hf,
                              bank=bank: E.matmul(
                                  pqk[bank][:],
                                  xnstage[:, dc, sl * 128:(sl + 1) * 128],
                                  Wb[:, dc, hf * 512:(hf + 1) * 512],
                                  start=(dc == 0), stop=(dc == 7)),
                              waits=wl, incs=([(S["sQM"], 1)] if dc == 7 else []))
                    mmn = S["sQM"].n
                    if kind == "v":
                        spv = S["sSPV0"] if b == 0 else S["sSPV1"]
                        eb.op("scalar", lambda E, bank=bank, hf=hf, b=b:
                              E.activation(
                                  vtb[b][:, hf * 512:(hf + 1) * 512],
                                  pqk[bank][:], AF.Identity),
                              waits=[(S["sQM"], mmn), (spv, 16 * (ct // 2))],
                              incs=[(S["sQEv"], 1)])
                    else:
                        dsttb = qtb if kind == "q" else ktb
                        sqb = sqh[lidx % 2]
                        sqw = ([(S["sST"], ct + 1)] if lidx < 2
                               else [(S["sQV"], qv_marks[lidx - 2])])
                        eb.op("scalar", lambda E, bank=bank, sqb=sqb: E.activation(
                            sqb[:], pqk[bank][:], AF.Square),
                            waits=[(S["sQM"], mmn)] + sqw, incs=[(S["sQA"], 1)])
                        qa1 = S["sQA"].n

                        vqs = S["sVC"].n

                        def qk_ss(E, sqb=sqb, qa1=qa1, vqs=vqs):
                            E.wait_ge(S["sQA"].h, qa1)
                            go = small_chain(E, S["sVC"], vqs)
                            go(lambda: E.reduce_sum(
                                qks[:, 0:8].unsqueeze(2),
                                sqb[:].rearrange("p (h d) -> p h d", h=8), AX.X))
                            i = go(lambda: E.tensor_scalar(qks[:, 8:16], qks[:, 0:8],
                                                           1e-24, None, ALU.max),
                                   inc=False)
                            i.then_inc(S["sQV"].h, 1)
                        eb.q["vector"].append(qk_ss)
                        S["sVC"].n += 1
                        S["sQV"].n += 1
                        qv_marks.append(S["sQV"].n)
                        eb.op("scalar", lambda E: E.activation(
                            qks[:, 16:24], qks[:, 8:16], AF.Ln),
                            waits=[(S["sQV"], S["sQV"].n)], incs=[(S["cACT"], 1)])
                        eb.op("scalar", lambda E: E.activation(
                            qks[:, 16:24], qks[:, 16:24], AF.Exp, scale=-0.5),
                            waits=[(S["cACT"], S["cACT"].n)], incs=[(S["sQA"], 1)])
                        qa2 = S["sQA"].n
                        eb.op("vector", lambda E, bank=bank, hf=hf, dsttb=dsttb:
                              E.tensor_tensor(
                                  dsttb[:, hf * 512:(hf + 1) * 512].rearrange(
                                      "p (h d) -> p h d", h=8),
                                  pqk[bank][:].rearrange("p (h d) -> p h d", h=8),
                                  qks[:, 16:24].unsqueeze(2).broadcast_to(
                                      [128, 8, 64]),
                                  ALU.mult),
                              waits=[(S["sQA"], qa2)], incs=[(S["sQEq"], 1)])
                if kind == "v":
                    pass

            qe_now = S["sQEq"].n
            rmap16 = lambda j: (j % 2) * 8 + j // 2
            for ti, (srctb, stage) in enumerate(((qtb, qstage), (ktb, kstage))):
                for dc in range(8):
                    j = ti * 8 + dc
                    wl = [(S["sQEq"], qe_now)] if j == 0 else []
                    if j >= 2:
                        wl.append((S["sQTE"], ct * 16 + j - 1))
                    eb.op("tensor", lambda E, srctb=srctb, dc=dc, r=rmap16(ti * 8 + dc):
                          E.transpose(pqb[:, r, :],
                                      srctb[:, dc * 128:(dc + 1) * 128], P["sidB"][:]),
                          waits=wl, incs=[(S["sQTP"], 1)])
                    wl2 = [(S["sQTP"], ct * 16 + j + 1)]
                    if dc == 0 and sl == 0:
                        wl2.append(((S["sSPq"] if ti == 0 else S["sSPk"]),
                                    spill_after.get(tile - 1, 0)))
                    eb.op("scalar", lambda E, stage=stage, dc=dc, sl=sl,
                          r=rmap16(ti * 8 + dc):
                          E.activation(stage[:, dc, sl * 128:(sl + 1) * 128],
                                       pqb[:, r, :], AF.Identity),
                          waits=wl2, incs=[(S["sQTE"], 1)])
            spv = S["sSPV0"] if b == 0 else S["sSPV1"]
            eb.op("scalar", lambda E, ct=ct, b=b: E.dma_start(
                vt[ct * 128:(ct + 1) * 128, :], vtb[b][:]),
                waits=[(S["sQEv"], 2 * (ct + 1))], incs=[(spv, 16)])

            if sl == 7:
                te_now = S["sQTE"].n
                xc_now = S["sXC"].n
                for stage, dst, ssem in ((xnstage, xnf, S["sSPx"]),
                                         (qstage, qf, S["sSPq"]),
                                         (kstage, kf, S["sSPk"])):
                    wl = ([(S["sXC"], xc_now)] if stage is xnstage
                          else [(S["sQTE"], te_now)])
                    eb.op("scalar", lambda E, stage=stage, dst=dst, tile=tile:
                          E.dma_start(
                              dst.rearrange("c p t -> p c t")[
                                  :, :, tile * 1024:(tile + 1) * 1024],
                              stage[:]),
                          waits=wl, incs=[(ssem, 16)])
                spill_after[tile] = 16 * (tile + 1)

            eb.op("vector", lambda E: E.tensor_copy(qks[:, 24:25], qks[:, 0:1]),
                  incs=[(S["sEC0"], 1)])
            eb.op("scalar", lambda E: E.copy(qks[:, 25:26], qks[:, 0:1]),
                  incs=[(S["sEC1"], 1)])
            eb.op("tensor", lambda E: E.transpose(
                pqb[:, 15, :], ktb[:, 896:1024], P["sidB"][:]),
                waits=[(S["sQTE"], ct * 16 + 16)], incs=[(S["sEC2"], 1)])

            d = {}
            for k2 in S:
                if S[k2].n != base[k2]:
                    key = k2.rstrip("01") if k2[-1] in "01" else k2
                    d[key] = d.get(key, 0) + S[k2].n - base[k2]
            if deltas0 is None and sl not in (0, 7):
                deltas0 = d
            elif sl not in (0, 7) and deltas0 is not None:
                assert d == deltas0, f"chunk {ct} deltas {d} != {deltas0}"

        meta = dict(SPx=S["sSPx"].n, SPq=S["sSPq"].n, SPk=S["sSPk"].n,
                    SPV0=S["sSPV0"].n, SPV1=S["sSPV1"].n, CASTW=CASTW)

        if debug:
            eb.op("scalar", lambda E: E.dma_start(dbgt["d_probs"], P["probs_res"][:]),
                  waits=[(S["sPR"], S["sPR"].n)], incs=[(S["dC"], 16)])
        if phases == 1:
            dcw = S["dC"].n
            fins = [(S["sSPx"], S["sSPx"].n), (S["sSPq"], S["sSPq"].n),
                    (S["sSPk"], S["sSPk"].n), (S["sSPV0"], S["sSPV0"].n),
                    (S["sSPV1"], S["sSPV1"].n), (S["sFW"], S["sFW"].n),
                    (S["dC"], dcw), (S["sPFM"], S["sPFM"].n)]

            def fin2(E, fins=fins):
                for s, v in fins:
                    if v:
                        E.wait_ge(s.h, v)
            eb.q["sync"].append(fin2)
        eb.emit(nc)
    return meta


def build_phase23(nc, S, P, tens, meta, dbgt, debug, phases):
    x, out, sS = tens["x"], tens["out"], tens["sS"]
    qf, kf, vt = tens["qf"], tens["kf"], tens["vt"]
    xnf, gwb, uwb, dwb = tens["xnf"], tens["gwb"], tens["uwb"], tens["dwb"]
    P1B = [("sSPx", meta["SPx"]), ("sSPq", meta["SPq"]), ("sSPk", meta["SPk"]),
           ("sSPV0", meta["SPV0"]), ("sSPV1", meta["SPV1"])]
    with ExitStack() as c2:
        def sb2(name, shape, dt=F32):
            return c2.enter_context(nc.sbuf_tensor(name, shape, dt))

        def ps2(name, shape, dt=F32):
            return c2.enter_context(nc.psum_tensor(name, shape, dt))

        qtl = [sb2("qtl0", [128, 8, 512], BF16)] * 2
        ksp = [sb2(f"ksp{i}", [128, 8, 256], BF16) for i in range(2)]
        vsp = [sb2(f"vsp{i}", [128, 2, 1024], BF16) for i in range(2)]
        st_s = [sb2(f"st_s{i}", [128, 2, 512], BF16) for i in range(4)]
        st_z = [sb2(f"st_z{i}", [128, 2, 512], BF16) for i in range(4)]
        stp2 = [sb2(f"st_p{i}", [128, 1024], BF16) for i in range(2)]
        st_q = [sb2(f"st_q{i}", [128, 1024], BF16) for i in range(2)]
        st_f = [sb2(f"st_f{i}", [128, 2, 512], BF16) for i in range(3)]
        ao_hq = sb2("ao_hq", [128, 4, 4, 64])
        ao_fm = sb2("ao_fm", [128, 8, 512], BF16)
        scr1 = sb2("scr1", [128, 1024])
        est = sb2("est", [128, 64])
        xtile = [sb2(f"xtile{i}", [128, 8, 512], BF16) for i in range(2)]
        gch = [sb2(f"gch{i}", [128, 8, 256], BF16) for i in range(2)]
        uch = [sb2(f"uch{i}", [128, 8, 256], BF16) for i in range(2)]
        Dsb = sb2("Dsb", [128, 16, 1024], BF16)
        hsb = sb2("hsb", [128, 16, 512], BF16)
        y_acc = sb2("y_acc", [128, 4, 1024])
        hs_scr = [sb2(f"hs_scr{i}", [128, 512]) for i in range(2)]
        pfm_sb = sb2("pfm_sb", [8, 512], BF16)
        xts = [sb2(f"xts{i}", [128, 512]) for i in range(2)]
        sts = [sb2(f"sts{i}", [128, 512]) for i in range(2)]

        psim = [ps2(f"psim{i}", [128, 512]) for i in range(2)]
        pav = ps2("pav", [128, 1024])
        pg = ps2("pg", [128, 512])
        pu = ps2("pu", [128, 512])
        py = [ps2(f"py{i}", [128, 512]) for i in range(2)]

        eb = EB()
        A = S

        mk = dict(sim_slice={}, woe_slice={}, sp_slice={}, kv={}, te=0, tp=0,
                  epa=0, epv=0, epi=0, wo_slice={}, ffn_mm_tt={}, ye_tt={},
                  ct_tt={}, gt_tt={}, fout_tt={})
        phi_ctr = [0]
        kv_ctr = [0]
        pend_av = []

        def co(h, j):
            return P["A_bc"][:, h * 9 + j:h * 9 + j + 1]

        def coB(h, j):
            return P["A_bcb"][:, h * 9 + j:h * 9 + j + 1].broadcast_to([128, 1024])

        def emit_av(pidx, k, qt, sp, hh, h, sb_, kvb, first_q, nsp):
            sb_ = pidx % 3
            wl = [(A["aPHI"], pidx + 1)]
            if first_q:
                wl.append((A["aTE"], mk["te"]))
                wl.append((A["aEPI"], mk["epi"]))
            valid = [(kcw, ts) for kcw in range(2) for ts in range(4)
                     if 2 * sp + kcw <= 4 * qt + ts]
            for n_i, (kcw, ts) in enumerate(valid):
                st = (sp == 0 and kcw == 0 and ts == 0 and hh in (0, 2))
                stp = (2 * sp + kcw == 4 * qt + ts)
                eb.op("tensor", lambda E, kcw=kcw, ts=ts, hh=hh, h=h, sb_=sb_,
                      kvb=kvb, st=st, stp=stp: E.matmul(
                          pav[:, hh * 256 + ts * 64:hh * 256 + (ts + 1) * 64],
                          st_f[sb_][:, kcw, ts * 128:(ts + 1) * 128],
                          vsp[kvb][:, kcw, h * 64:(h + 1) * 64],
                          start=st, stop=stp, skip_group_check=True),
                      waits=(wl if n_i == 0 else []),
                      incs=([(A["aAV"], 1)] if n_i == len(valid) - 1 else []))

        def attn_slice(k):
            seq, qt = divmod(k, 4)
            nsp = 2 * (qt + 1)
            t0 = k * 512
            qb = k % 2
            wl = [(S[n], v) for n, v in P1B]
            if k >= 1:
                wl.append((A["aSIM"], mk["sim_slice"][k - 1]))
            eb.op("sync", lambda E, qb=qb, t0=t0: E.dma_start(
                qtl[qb][:], qf.rearrange("c p t -> p c t")[:, :, t0:t0 + 512]),
                waits=wl, incs=[(A["aQ"], 16)])
            aQ_k = A["aQ"].n
            yield 1.0
            def kv_load(sp):
                kvi = kv_ctr[0]
                kv_ctr[0] += 1
                kvb = kvi % 2
                tk0 = seq * 2048 + sp * 256
                sK = A["aK0"] if kvb == 0 else A["aK1"]
                sV = A["aV0"] if kvb == 0 else A["aV1"]
                wlk = [(S[n], v) for n, v in P1B]
                wlv = []
                if kvi >= 2:
                    simm, avm = mk["kv"][kvi - 2]
                    wlk.append((A["aSIM"], simm))
                    wlv.append((A["aAV"], avm))
                    while A["aAV"].n < avm and pend_av:
                        pend_av.pop(0)()
                eb.op("sync", lambda E, kvb=kvb, tk0=tk0: E.dma_start(
                    ksp[kvb][:], kf.rearrange("c p t -> p c t")[:, :, tk0:tk0 + 256]),
                    waits=wlk, incs=[(sK, 16)])
                eb.op("sync", lambda E, kvb=kvb, tk0=tk0: E.dma_start(
                    vsp[kvb][:],
                    vt.rearrange("(a w p) d -> a p w d", p=128, w=2)[tk0 // 256]),
                    waits=wlv, incs=[(sV, 16)])
                return kvi, kvb, (sK.n, sV.n, sK, sV)

            CP, CV, CA = S["cPOOL"], S["cDVE"], S["cACT"]

            def emit_prep(hq, sp, hpair, kvi, kvb, kvn, diag):
                CH = []
                for cc in range(2):
                    hh = hpair + cc
                    h = hq * 4 + hh
                    pidx = phi_ctr[0]
                    phi_ctr[0] += 1
                    CH.append(dict(hh=hh, h=h, pidx=pidx, sb=pidx % 4,
                                   c=h // 2, par=(h % 2) * 64))
                for ch in CH:
                    pidx, sb_ = ch["pidx"], ch["sb"]
                    for kcw in range(2):
                        ei = 2 * pidx + kcw
                        bank = ei % 2
                        wl = []
                        if kcw == 0:
                            wl = [(kvn[2], kvn[0]), (kvn[3], kvn[1]),
                                  (A["aQ"], aQ_k)]
                        if ei >= 2:
                            wl.append((A["aEV"], 2 * (ei - 2) + 2))
                        eb.op("tensor", lambda E, c=ch["c"], par=ch["par"],
                              kcw=kcw, bank=bank, kvb=kvb, qb=qb: E.matmul(
                                  psim[bank][:],
                                  ksp[kvb][par:par + 64, c,
                                           kcw * 128:(kcw + 1) * 128],
                                  qtl[qb][par:par + 64, c, :],
                                  start=True, stop=True),
                              waits=wl, incs=[(A["aSIM"], 1)])
                        smn = A["aSIM"].n
                        wl2 = [(A["aSIM"], smn)]
                        if pidx >= 4:
                            wl2.append((A["aPHI"], 2 * (pidx // 2) - 2))
                        eb.op("scalar", lambda E, bank=bank, sb_=sb_,
                              kcw=kcw: E.activation(
                                  st_s[sb_][:, kcw, :], psim[bank][:],
                                  AF.Identity),
                              waits=wl2, incs=[(A["aEV"], 2)])
                    ch["evn"] = A["aEV"].n
                for ch in CH:
                    sb_, pidx = ch["sb"], ch["pidx"]
                    zc = st_z[sb_][:].rearrange("p w t -> p (w t)")
                    sc = st_s[sb_][:].rearrange("p w t -> p (w t)")
                    ch["zc"], ch["sc"] = zc, sc
                    ch["qs"] = st_q[pidx % 2][:]
                    eb.op("gpsimd", lambda E, zc=zc, sc=sc: E.tensor_tensor(
                        zc, sc, sc, ALU.mult),
                        waits=[(A["aEV"], ch["evn"]),
                               (A["aPHI"], max(0, 2 * (pidx // 2) - 2)),
                               (CP, CP.n)],
                        incs=[(CP, 1)])
                    ch["zn"] = CP.n
                return dict(CH=CH, kvi=kvi, kvb=kvb, diag=diag, sp=sp,
                            simn=A["aSIM"].n)

            def emit_chain(g, foq, k, qt, nsp):
                CH, kvb, diag, sp = g["CH"], g["kvb"], g["diag"], g["sp"]
                while len(pend_av) > 1:
                    pend_av.pop(0)()
                for ch in CH:
                    h, pidx = ch["h"], ch["pidx"]
                    pst = stp2[pidx % 2][:]
                    ch["pst"] = pst
                    eb.op("scalar", lambda E, zc=ch["zc"], h=h, pst=pst:
                          E.activation(pst, zc, AF.Identity,
                                       bias=co(h, 6), scale=co(h, 8)),
                          waits=[(CP, ch["zn"]),
                                 (A["aPHI"], max(0, pidx - 1)),
                                 (CA, max(0, CA.n - 1))],
                          incs=[(CA, 1)])
                    eb.op("scalar", lambda E, zc=ch["zc"], h=h,
                          qs=ch["qs"]: E.activation(
                              qs, zc, AF.Identity, bias=co(h, 5),
                              scale=co(h, 7)),
                          waits=[(CP, ch["zn"]),
                                 (A["aPHI"], max(0, pidx - 1)),
                                 (CA, max(0, CA.n - 1))],
                          incs=[(CA, 1)])
                    ch["initn"] = CA.n

                def dve(ch, f, waits=()):
                    eb.op("vector", f,
                          waits=list(waits) + [(CV, max(0, CV.n - 1))],
                          incs=[(CV, 1)])
                    return CV.n

                def act(ch, f, waits=()):
                    eb.op("scalar", f,
                          waits=list(waits) + [(CA, max(0, CA.n - 1))],
                          incs=[(CA, 1)])
                    return CA.n
                for ch in CH:
                    ch["m"] = dve(ch, lambda E, p=ch["pst"], zc=ch["zc"]:
                                  E.tensor_tensor(p, p, zc, ALU.mult),
                                  [(CA, ch["initn"])])
                for ch in CH:
                    bc4 = coB(ch["h"], 4)
                    ch["m"] = dve(ch, lambda E, p=ch["pst"], b=bc4:
                                  E.tensor_tensor(p, p, b, ALU.add))
                for ch in CH:
                    ch["mq"] = dve(ch, lambda E, q=ch["qs"], zc=ch["zc"]:
                                   E.tensor_tensor(q, q, zc, ALU.mult))
                for ch in CH:
                    ch["m"] = dve(ch, lambda E, p=ch["pst"], zc=ch["zc"]:
                                  E.tensor_tensor(p, p, zc, ALU.mult))
                for ch in CH:
                    ch["aq"] = act(ch, lambda E, q=ch["qs"], h=ch["h"]:
                                   E.activation(q, q, AF.Identity,
                                                bias=co(h, 3)),
                                   [(CV, ch["mq"])])
                for ch in CH:
                    ch["ap"] = act(ch, lambda E, p=ch["pst"], h=ch["h"]:
                                   E.activation(p, p, AF.Identity,
                                                bias=co(h, 2)),
                                   [(CV, ch["m"])])
                for ch in CH:
                    ch["mq"] = dve(ch, lambda E, q=ch["qs"], zc=ch["zc"]:
                                   E.tensor_tensor(q, q, zc, ALU.mult),
                                   [(CA, ch["aq"])])
                for ch in CH:
                    ch["m"] = dve(ch, lambda E, p=ch["pst"], zc=ch["zc"]:
                                  E.tensor_tensor(p, p, zc, ALU.mult),
                                  [(CA, ch["ap"])])
                for ch in CH:
                    bc1 = coB(ch["h"], 1)
                    ch["mq"] = dve(ch, lambda E, q=ch["qs"], b=bc1:
                                   E.tensor_tensor(q, q, b, ALU.add))
                for ch in CH:
                    bc0 = coB(ch["h"], 0)
                    ch["m"] = dve(ch, lambda E, p=ch["pst"], b=bc0:
                                  E.tensor_tensor(p, p, b, ALU.add))
                for ch in CH:
                    ch["mq"] = dve(ch, lambda E, q=ch["qs"], sc=ch["sc"]:
                                   E.tensor_tensor(q, q, sc, ALU.mult))
                for ch in CH:
                    pidx = ch["pidx"]
                    fcs = st_f[pidx % 3][:].rearrange("p w t -> p (w t)")
                    wl = [(CV, max(0, CV.n - 1))]
                    if pidx >= 3:
                        wl.append((A["aAV"], pidx - 2))
                    if diag:
                        eb.op("vector", lambda E, fcs=fcs, p=ch["pst"],
                              q=ch["qs"]: E.tensor_tensor(
                                  fcs, p, q, ALU.add),
                              waits=wl, incs=[(CV, 1)])
                        dmo = 0 if sp == nsp - 2 else 1024
                        eb.op("vector", lambda E, fcs=fcs, dmo=dmo:
                              E.tensor_tensor(
                                  fcs, fcs, P["dms"][:, dmo:dmo + 1024],
                                  ALU.mult),
                              waits=[(CV, CV.n)],
                              incs=[(A["aPHI"], 1)])
                    else:
                        eb.op("vector", lambda E, fcs=fcs, p=ch["pst"],
                              q=ch["qs"]: E.tensor_tensor(
                                  fcs, p, q, ALU.add),
                              waits=wl, incs=[(A["aPHI"], 1)])
                for ch in CH:
                    pend_av.append(
                        lambda pidx=ch["pidx"], k=k, qt=qt, sp=sp,
                        hh=ch["hh"], h=ch["h"], sb_=ch["sb"], kvb=kvb,
                        fq=foq[0], nsp=nsp:
                        emit_av(pidx, k, qt, sp, hh, h, sb_, kvb, fq,
                                nsp))
                    foq[0] = False
                mk["kv"][g["kvi"]] = (g["simn"],
                                      A["aAV"].n + len(pend_av))

            for hq in range(4):
                foq = [True]
                pend_kv = kv_load(0)
                pend_chain = None
                for sp in range(nsp):
                    kvi, kvb, kvn = pend_kv
                    diag = sp >= nsp - 2
                    for hpair in (0, 2):
                        g = emit_prep(hq, sp, hpair, kvi, kvb, kvn, diag)
                        if pend_chain is not None:
                            pend_chain()
                        pend_chain = (lambda g=g, foq=foq, k=k, qt=qt,
                                      nsp=nsp: emit_chain(g, foq, k, qt, nsp))
                    if sp + 1 < nsp:
                        pend_kv = kv_load(sp + 1)
                    yield 22.0
                if pend_chain is not None:
                    pend_chain()
                while pend_av:
                    pend_av.pop(0)()
                avq = A["aAV"].n
                epi_wl = [(A["aAV"], avq), (A["aEPA"], mk["epa"]),
                          (A["aTP"], mk["tp"])]
                for hi in range(4):
                    pv = pav[:, hi * 256:(hi + 1) * 256].rearrange(
                        "p (ts d) -> p ts d", ts=4)
                    dvb = P["dvls"][:, 4 * qt:4 * qt + 4].unsqueeze(2).broadcast_to(
                        [128, 4, 64])
                    eb.op("vector", lambda E, hi=hi, pv=pv, dvb=dvb: E.tensor_tensor(
                        ao_hq[:, :, hi, :], pv, dvb, ALU.mult),
                        waits=(epi_wl if hi == 0 else []),
                        incs=([(A["aEPI"], 1)] if hi == 3 else []))
                epin = A["aEPI"].n
                mk["epi"] = epin
                eb.op("scalar", lambda E: E.activation(
                    scr1[:], ao_hq[:].rearrange("p a b c -> p (a b c)"), AF.Square),
                    waits=[(A["aEPI"], epin), (A["aEPV"], mk["epv"]),
                           (A["aSPD"], mk["sp_slice"].get(k - 1, 0))],
                    incs=[(A["aEPA"], 1)])
                mk["epa"] = A["aEPA"].n
                vcm = S["sVC"].n

                def epi_dve(E, epan=mk["epa"], vcm=vcm):
                    E.wait_ge(A["aEPA"].h, epan)
                    go = small_chain(E, S["sVC"], vcm)
                    go(lambda: E.reduce_sum(
                        est[:, 0:16].unsqueeze(2),
                        scr1[:].rearrange("p (g d) -> p g d", d=64), AX.X))
                    go(lambda: E.tensor_scalar(est[:, 16:32], est[:, 0:16],
                                               1.0 / 64, 1e-6, ALU.mult, ALU.add))
                    I32 = mybir.dt.int32
                    t1 = est[:, 32:48]
                    t2 = est[:, 48:64]
                    oi = t1.bitcast(I32)
                    go(lambda: E.tensor_scalar(oi, est[:, 16:32].bitcast(I32), 23,
                                               None, ALU.arith_shift_right))
                    go(lambda: E.tensor_scalar(oi, oi, -1, None, ALU.bitwise_xor))
                    go(lambda: E.tensor_scalar(oi, oi, 382, None, ALU.add))
                    go(lambda: E.tensor_scalar(oi, oi, 1, None, ALU.arith_shift_right))
                    go(lambda: E.tensor_scalar(oi, oi, 23, None, ALU.arith_shift_left))
                    for _ in range(5):
                        go(lambda: E.tensor_tensor(t2, t1, t1, ALU.mult))
                        go(lambda: E.tensor_tensor(t2, t2, est[:, 16:32], ALU.mult))
                        go(lambda: E.tensor_scalar(t2, t2, -0.5, 1.5, ALU.mult,
                                                   ALU.add))
                        go(lambda: E.tensor_tensor(t1, t1, t2, ALU.mult))
                    i = go(lambda: E.tensor_tensor(
                        ao_hq[:].rearrange("p ts h d -> p (ts h) d"),
                        ao_hq[:].rearrange("p ts h d -> p (ts h) d"),
                        est[:, 32:48].rearrange("p (ts h) -> p ts h", ts=4)
                        .rearrange("p ts h -> p (ts h)").unsqueeze(2)
                        .broadcast_to([128, 16, 64]), ALU.mult), inc=False)
                    i.then_inc(A["aEPV"].h, 1)
                eb.q["vector"].append(epi_dve)
                S["sVC"].n += 27
                A["aEPV"].n += 1
                mk["epv"] = A["aEPV"].n
                rmap8 = lambda j: (j % 2) * 4 + j // 2
                tpw0 = [(A["aEPV"], mk["epv"]), (A["aWO"], mk["wo_slice"].get(k - 1, 0))]
                for j in range(8):
                    ts, jj = divmod(j, 2)
                    jc = hq * 2 + jj
                    r = rmap8(j)
                    wl = list(tpw0) if j == 0 else []
                    if j >= 2:
                        wl.append((A["aTE"], mk["te"] + j - 1))
                    eb.op("tensor", lambda E, ts=ts, jj=jj, r=r: E.transpose(
                        pav[:, r * 128:(r + 1) * 128],
                        ao_hq[:, ts, jj * 2:jj * 2 + 2, :].rearrange(
                            "p h d -> p (h d)"),
                        P["sidF"][:]),
                        waits=wl, incs=[(A["aTP"], 1)])
                    eb.op("scalar", lambda E, ts=ts, jc=jc, r=r: E.activation(
                        ao_fm[:, jc, ts * 128:(ts + 1) * 128],
                        pav[:, r * 128:(r + 1) * 128], AF.Identity),
                        waits=[(A["aTP"], A["aTP"].n)], incs=[(A["aTE"], 1)])
                mk["tp"] = A["aTP"].n
                mk["te"] = A["aTE"].n
                yield 8.0
            evall = A["aEV"].n
            wo0 = A["aWO"].n
            for g in range(8):
                ts, ih = divmod(g, 2)
                bank = g % 2
                wl = []
                if g == 0:
                    wl = [(A["aEV"], evall), (A["aTE"], mk["te"]),
                          (A["aEPV"], mk["epv"])]
                if g >= 2:
                    wl.append((A["aWOE"], wo0 // 1 + g - 1 if False else
                               A["aWOE"].n - 1))
                for jc in range(8):
                    eb.op("tensor", lambda E, jc=jc, ts=ts, ih=ih, bank=bank:
                          E.matmul(psim[bank][:],
                                   ao_fm[:, jc, ts * 128:(ts + 1) * 128],
                                   P["WoTb"][:, jc, ih * 512:(ih + 1) * 512],
                                   start=(jc == 0), stop=(jc == 7)),
                          waits=(wl if jc == 0 else []),
                          incs=([(A["aWO"], 1)] if jc == 7 else []))
                won = A["aWO"].n
                spp = A["aSP0"] if g % 2 == 0 else A["aSP1"]
                wle = [(A["aWO"], won)]
                if g >= 2:
                    wle.append((spp, spp.n))
                eb.op("scalar", lambda E, bank=bank, g=g: E.activation(
                    scr1[:, (g % 2) * 512:(g % 2) * 512 + 512], psim[bank][:],
                    AF.Identity),
                    waits=wle, incs=[(A["aWOE"], 1)])
                woen = A["aWOE"].n
                r0 = t0 + ts * 128
                eb.op("scalar", lambda E, r0=r0, ih=ih, g=g: E.dma_start(
                    sS[r0:r0 + 128, ih * 512:(ih + 1) * 512],
                    scr1[:, (g % 2) * 512:(g % 2) * 512 + 512]),
                    waits=[(A["aWOE"], woen)], incs=[(spp, 16)])
            sp0n, sp1n = A["aSP0"].n, A["aSP1"].n

            def s_echo(E, sp0n=sp0n, sp1n=sp1n):
                E.wait_ge(A["aSP0"].h, sp0n)
                E.wait_ge(A["aSP1"].h, sp1n)
                i = E.copy(est[:, 0:1], est[:, 0:1])
                i.then_inc(A["aSPD"].h, 1)
            eb.q["scalar"].append(s_echo)
            A["aSPD"].n += 1
            mk["wo_slice"][k] = A["aWO"].n
            mk["woe_slice"][k] = A["aWOE"].n
            mk["sim_slice"][k] = A["aSIM"].n
            mk["sp_slice"][k] = A["aSPD"].n
            yield 6.0

        FWE = 48

        def ffn_tt(tt):
            xb = tt % 2
            fxp = A["fXN0"] if xb == 0 else A["fXN1"]
            wl = [(S[n], v) for n, v in P1B]
            if tt >= 2:
                wl.append((A["fPG"], mk["ffn_mm_tt"][tt - 2][0]))
                wl.append((A["fPU"], mk["ffn_mm_tt"][tt - 2][1]))
            eb.op("sync", lambda E, xb=xb, tt=tt: E.dma_start(
                xtile[xb][:],
                xnf.rearrange("c p t -> p c t")[:, :, tt * 512:(tt + 1) * 512]),
                waits=wl, incs=[(fxp, 16)])
            wlp = [(S["sPFM"], 16 * NCH_L)]
            if tt >= 1:
                wlp.append((A["fBI"], A["fBI"].n))
            eb.op("sync", lambda E, tt=tt: E.dma_start(
                pfm_sb[:].rearrange("e (a t) -> e a t", a=4),
                tens["pfm"][:, tt * 4:tt * 4 + 4, :]),
                waits=wlp, incs=[(A["fPF"], 16)])
            yield 2.0
            for e in range(NE):
                wl = [(S["sFW"], S["sFW"].n)]
                if not (tt == 0 and e == 0):
                    wl.append((A["fPY"], A["fPY"].n))
                eb.op("sync", lambda E, e=e: E.dma_start(
                    Dsb[:], dwb[e].rearrange("(f p) d -> p f d", p=128)),
                    waits=wl, incs=[(A["fD"], 16)])
                fdn = A["fD"].n
                gu_marks = {}
                FWALL = S["sFW"].n
                for fcp in range(2):
                    gs = A["fG0"] if fcp % 2 == 0 else A["fG1"]
                    us = A["fU0"] if fcp % 2 == 0 else A["fU1"]
                    wl = [(S["sFW"], FWALL)]
                    wlu = []
                    if not (tt == 0 and e == 0 and fcp < 2):
                        wl.append((A["fPG"], A["fPG"].n))
                        wlu.append((A["fPU"], A["fPU"].n))
                    eb.op("sync", lambda E, e=e, fcp=fcp: E.dma_start(
                        gch[fcp % 2][:],
                        gwb[e].rearrange("(c p) f -> p c f", p=128)[
                            :, :, fcp * 256:(fcp + 1) * 256]),
                        waits=wl, incs=[(gs, 16)])
                    eb.op("sync", lambda E, e=e, fcp=fcp: E.dma_start(
                        uch[fcp % 2][:],
                        uwb[e].rearrange("(c p) f -> p c f", p=128)[
                            :, :, fcp * 256:(fcp + 1) * 256]),
                        waits=wlu, incs=[(us, 16)])
                    gu_marks[fcp] = (gs, gs.n, us, us.n)
                for fc in range(16):
                    fcp = fc // 2
                    if fc % 2 == 0 and fc >= 2 and fcp + 1 < 8:
                        gs = A["fG0"] if (fcp + 1) % 2 == 0 else A["fG1"]
                        us = A["fU0"] if (fcp + 1) % 2 == 0 else A["fU1"]
                        eb.op("sync", lambda E, e=e, fcp=fcp: E.dma_start(
                            gch[(fcp + 1) % 2][:],
                            gwb[e].rearrange("(c p) f -> p c f", p=128)[
                                :, :, (fcp + 1) * 256:(fcp + 2) * 256]),
                            waits=[(A["fPG"], A["fPG"].n)], incs=[(gs, 16)])
                        eb.op("sync", lambda E, e=e, fcp=fcp: E.dma_start(
                            uch[(fcp + 1) % 2][:],
                            uwb[e].rearrange("(c p) f -> p c f", p=128)[
                                :, :, (fcp + 1) * 256:(fcp + 2) * 256]),
                            waits=[(A["fPU"], A["fPU"].n)], incs=[(us, 16)])
                        gu_marks[fcp + 1] = (gs, gs.n, us, us.n)
                    for dc in range(8):
                        wl = []
                        if dc == 0:
                            wl = [(gu_marks[fcp][0], gu_marks[fcp][1]),
                                  (fxp, 16 * (tt // 2 + 1)),
                                  (A["fSIL"], A["fSIL"].n)]
                        eb.op("tensor", lambda E, dc=dc, fc=fc, fcp=fcp, xb=xb:
                              E.matmul(pg[:],
                                       gch[fcp % 2][:, dc, (fc % 2) * 128:(fc % 2) * 128 + 128],
                                       xtile[xb][:, dc, :],
                                       start=(dc == 0), stop=(dc == 7)),
                              waits=wl, incs=([(A["fPG"], 1)] if dc == 7 else []))
                    pgn = A["fPG"].n
                    wl = [(A["fPG"], pgn), (A["fH"], max(0, A["fH"].n - 1))]
                    eb.op("scalar", lambda E, e=e, fc=fc: E.activation(
                        hs_scr[fc % 2][:], pg[:], AF.Silu,
                        bias=P["gbs"][:, e * 16 + fc:e * 16 + fc + 1]),
                        waits=wl, incs=[(A["fSIL"], 1)])
                    siln = A["fSIL"].n
                    for dc in range(8):
                        wl = []
                        if dc == 0:
                            wl = [(gu_marks[fcp][2], gu_marks[fcp][3]),
                                  (A["fH"], A["fH"].n)]
                        eb.op("tensor", lambda E, dc=dc, fc=fc, fcp=fcp, xb=xb:
                              E.matmul(pu[:],
                                       uch[fcp % 2][:, dc, (fc % 2) * 128:(fc % 2) * 128 + 128],
                                       xtile[xb][:, dc, :],
                                       start=(dc == 0), stop=(dc == 7)),
                              waits=wl, incs=([(A["fPU"], 1)] if dc == 7 else []))
                    pun = A["fPU"].n
                    wl = [(A["fPU"], pun), (A["fSIL"], siln)]
                    if tt > 0 or e > 0:
                        wl.append((A["fPY"], A["fPY"].n))
                    eb.op("vector", lambda E, e=e, fc=fc: E.scalar_tensor_tensor(
                        hsb[:, fc, :], pu[:],
                        P["ubs"][:, e * 16 + fc:e * 16 + fc + 1],
                        hs_scr[fc % 2][:], ALU.add, ALU.mult),
                        waits=wl, incs=[(A["fH"], 1)])
                    if fc % 4 == 3:
                        yield 7.0
                fhn = A["fH"].n
                if e == 0:
                    for g8 in range(8):
                        t128, dh = divmod(g8, 2)
                        ct = tt * 4 + t128
                        bank = g8 % 2
                        wl = []
                        if g8 == 0:
                            wl = [(A["fYE"], mk["ye_tt"].get(tt - 1, 0)),
                                  (A["fPF"], A["fPF"].n)]
                        if g8 >= 2:
                            wl.append((A["fBIC"], A["fBIC"].n - 1))
                        eb.op("tensor", lambda E, t128=t128, dh=dh, bank=bank: E.matmul(
                            py[bank][:], pfm_sb[0:8, t128 * 128:(t128 + 1) * 128],
                            P["dbs"][0:8, dh * 512:(dh + 1) * 512],
                            start=True, stop=True),
                            waits=wl, incs=[(A["fBI"], 1)])
                        bin_ = A["fBI"].n
                        wl2 = [(A["fBI"], bin_)]
                        if g8 == 0:
                            wl2.append((A["fGT"], mk["gt_tt"].get(tt - 1, 0)))
                        eb.op("vector", lambda E, t128=t128, dh=dh, bank=bank: E.tensor_copy(
                            y_acc[:, t128, dh * 512:(dh + 1) * 512], py[bank][:]),
                            waits=wl2, incs=[(A["fBIC"], 1)])
                    bic_all = A["fBIC"].n
                for g8 in range(8):
                    t128, dh = divmod(g8, 2)
                    ct = tt * 4 + t128
                    bank = g8 % 2
                    for fc in range(16):
                        wl = []
                        if fc == 0:
                            wl = [(A["fH"], fhn), (A["fD"], fdn),
                                  (A["fYE"], max(0, A["fYE"].n - 1))]
                            if e == 0:
                                wl.append((A["fBIC"], bic_all))
                        eb.op("tensor", lambda E, fc=fc, t128=t128, dh=dh,
                              bank=bank: E.matmul(
                                  py[bank][:],
                                  hsb[:, fc, t128 * 128:(t128 + 1) * 128],
                                  Dsb[:, fc, dh * 512:(dh + 1) * 512],
                                  start=(fc == 0), stop=(fc == 15)),
                              waits=wl, incs=([(A["fPY"], 1)] if fc == 15 else []))
                    pyn = A["fPY"].n
                    eb.op("vector", lambda E, t128=t128, dh=dh, bank=bank, ct=ct,
                          e=e: E.scalar_tensor_tensor(
                              y_acc[:, t128, dh * 512:(dh + 1) * 512],
                              py[bank][:], P["probs_res"][:, ct, e:e + 1],
                              y_acc[:, t128, dh * 512:(dh + 1) * 512],
                              ALU.mult, ALU.add),
                          waits=[(A["fPY"], pyn)], incs=[(A["fYE"], 1)])
                    if g8 % 2 == 1:
                        yield 7.0
            mk["ffn_mm_tt"][tt] = (A["fPG"].n, A["fPU"].n)
            mk["ye_tt"][tt] = A["fYE"].n

        def ffn_tail(tt):
            yeall = mk["ye_tt"][tt]
            fhn = A["fH"].n
            assert tt in mk["sp_slice"], f"attn slice {tt} must precede ffn tail"
            spm = mk["sp_slice"][tt]
            for i in range(8):
                t128, i2 = divmod(i, 2)
                r0 = tt * 512 + t128 * 128
                cs = i2 * 512
                wl = [(A["fYE"], yeall), (A["fCT"], max(0, A["fCT"].n - 1))]
                if i == 0:
                    wl.append((A["fH"], fhn))
                eb.op("scalar", lambda E, t128=t128, cs=cs, i=i: E.activation(
                    hs_scr[i % 2][:], y_acc[:, t128, cs:cs + 512], AF.Silu),
                    waits=wl, incs=[(A["fGT"], 1)])
                gtn = A["fGT"].n
                fxx = A["fXSx0"] if i % 2 == 0 else A["fXSx1"]
                fxss = A["fXSs0"] if i % 2 == 0 else A["fXSs1"]
                wl = [(A["fCT"], max(0, A["fCT"].n - 1))]
                eb.op("sync", lambda E, r0=r0, cs=cs, i=i: E.dma_start(
                    xts[i % 2][:], x[r0:r0 + 128, cs:cs + 512]),
                    waits=wl, incs=[(fxx, 16)])
                fop = A["fOUT0"] if i % 2 == 0 else A["fOUT1"]
                eb.op("sync", lambda E, r0=r0, cs=cs, i=i: E.dma_start(
                    sts[i % 2][:], sS[r0:r0 + 128, cs:cs + 512]),
                    waits=[(A["aSPD"], spm), (A["fCT"], max(0, A["fCT"].n - 1)),
                           (fop, fop.n)],
                    incs=[(fxss, 16)])
                xsn = (fxx, fxx.n, fxss, fxss.n)
                vcc = S["sVC"].n

                def comb(E, i=i, gtn=gtn, xsn=xsn, vcc=vcc):
                    E.wait_ge(A["fGT"].h, gtn)
                    E.wait_ge(xsn[0].h, xsn[1])
                    E.wait_ge(xsn[2].h, xsn[3])
                    go = small_chain(E, S["sVC"], vcc)
                    go(lambda: E.tensor_tensor(sts[i % 2][:], sts[i % 2][:],
                                               hs_scr[i % 2][:], ALU.mult))
                    ii = go(lambda: E.tensor_tensor(sts[i % 2][:], sts[i % 2][:],
                                                    xts[i % 2][:], ALU.add),
                            inc=False)
                    ii.then_inc(A["fCT"].h, 1)
                eb.q["vector"].append(comb)
                S["sVC"].n += 1
                A["fCT"].n += 1
                ctn = A["fCT"].n
                eb.op("scalar", lambda E, r0=r0, cs=cs, i=i: E.dma_start(
                    out[r0:r0 + 128, cs:cs + 512], sts[i % 2][:]),
                    waits=[(A["fCT"], ctn)], incs=[(fop, 16)])
                yield 2.0
            mk["ct_tt"][tt] = A["fCT"].n
            mk["gt_tt"][tt] = A["fGT"].n

        only_attn = (phases == 2)
        for k in (SLICES if SLICES is not None else range(NSLICE)):
            a = attn_slice(k)
            f = None if only_attn else ffn_tt(k)
            a_done = False
            f_done = only_attn
            ta = tf = 0.0
            while not (a_done and f_done):
                if not a_done and (f_done or ta <= tf):
                    try:
                        ta += next(a)
                    except StopIteration:
                        a_done = True
                else:
                    try:
                        tf += next(f)
                    except StopIteration:
                        f_done = True
            if not only_attn:
                for _ in ffn_tail(k):
                    pass

        fins = [(A["fOUT0"], A["fOUT0"].n), (A["fOUT1"], A["fOUT1"].n),
                (A["aSP0"], A["aSP0"].n), (A["aSP1"], A["aSP1"].n)]

        def fin(E, fins=fins):
            for s, v in fins:
                if v:
                    E.wait_ge(s.h, v)
        eb.q["sync"].append(fin)
        eb.emit(nc)


_BF = ml_dtypes.bfloat16


def host_inputs(inputs):
    f32 = np.float32
    W = {}
    W["wqT"] = np.ascontiguousarray(inputs["L0_attn_Wq"].T.astype(f32, copy=False))
    W["wkT"] = np.ascontiguousarray(inputs["L0_attn_Wk"].T.astype(f32, copy=False))
    W["wvT"] = np.ascontiguousarray(inputs["L0_attn_Wv"].T.astype(f32, copy=False))
    W["woT"] = np.ascontiguousarray(inputs["L0_attn_Wo"].T.astype(f32, copy=False))
    W["rwT"] = np.ascontiguousarray(inputs["L0_ffn_router_w"].T.astype(f32, copy=False))
    W["polyT"] = np.ascontiguousarray(
        inputs["L0_attn_poly_weights"].T.astype(f32, copy=False))
    W["hdrow"] = np.ascontiguousarray(
        inputs["L0_attn_head_degrees"].reshape(1, 16).astype(f32, copy=False))
    nw = inputs["L0_attn_norm_w"].astype(f32, copy=False)
    W["nw2"] = np.ascontiguousarray(np.concatenate([nw, nw]).reshape(128, 1))
    W["gw"] = np.ascontiguousarray(inputs["L0_ffn_gate_w"].astype(f32, copy=False))
    W["uw"] = np.ascontiguousarray(inputs["L0_ffn_up_w"].astype(f32, copy=False))
    W["dw"] = np.ascontiguousarray(inputs["L0_ffn_down_w"].astype(f32, copy=False))
    W["gbl"] = np.ascontiguousarray(
        inputs["L0_ffn_gate_b"].reshape(NE, 16, 128).transpose(2, 0, 1)
        .reshape(128, 128).astype(f32, copy=False))
    W["ubl"] = np.ascontiguousarray(
        inputs["L0_ffn_up_b"].reshape(NE, 16, 128).transpose(2, 0, 1)
        .reshape(128, 128).astype(f32, copy=False))
    W["db"] = np.ascontiguousarray(inputs["L0_ffn_down_b"].astype(f32, copy=False))
    W["identF"] = np.eye(128, dtype=f32)
    W["identB"] = np.eye(128, dtype=_BF)
    W["rmat"] = gegen_monomials()
    W["kcol"] = np.arange(9, dtype=f32).reshape(9, 1)
    p = np.arange(128, dtype=np.float64)
    j = np.arange(16, dtype=np.float64)
    W["dvl"] = (1.0 / (j[None, :] * 128 + p[:, None] + 1.0)).astype(f32)
    dm = np.zeros((128, 2, 2, 512), np.float32)
    for v in range(2):
        for kcw in range(2):
            thr = (2 * v + kcw) * 128 + p[:, None]
            dm[:, v, kcw, :] = (np.arange(512)[None, :] >= thr)
    W["dmc"] = np.ascontiguousarray(dm.reshape(128, 2048).astype(_BF))
    return W


def kernel(**inputs):
    nc, _ = build_nc(debug=DEBUG, phases=BUILD_PHASES)
    W = host_inputs(inputs)
    xf = inputs["L0_attn_x"].astype(np.float32, copy=False)
    in_maps = []
    for c in range(NCORE):
        m = dict(W)
        m["x"] = np.ascontiguousarray(xf[c * 4:(c + 1) * 4].reshape(TOK, DD))
        in_maps.append(m)
    res = run_bass_kernel_spmd(nc, in_maps, core_ids=list(range(NCORE)))
    outs = [np.asarray(r["out"]) for r in res.results]
    return np.concatenate(outs, axis=0).reshape(32, 2048, 1024)


# version 08
# speedup vs baseline: 1.0586x; 1.0106x over previous
import numpy as np
import ml_dtypes
import concourse.bass as bass
import concourse.mybir as mybir
from concourse.bass_utils import run_bass_kernel_spmd
from contextlib import ExitStack

F32 = mybir.dt.float32
BF16 = mybir.dt.bfloat16
AF = mybir.ActivationFunctionType
ALU = mybir.AluOpType
AX = mybir.AxisListType

NCORE = 8
TOK = 8192
NCH = 64
DD = 1024
DFF = 2048
NE = 8
NSLICE = 16
LAM = 31.5

BUILD_PHASES = 3
DEBUG = 0
NCH_L = NCH
SLICES = None


def gegen_monomials():
    R = [np.zeros(9, np.float64) for _ in range(9)]
    R[0][0] = 1.0
    R[1][1] = 1.0
    for k in range(2, 9):
        c1 = 2.0 * (k + LAM - 1.0) / (k + 2.0 * LAM - 1.0)
        c2 = (k - 1.0) / (k + 2.0 * LAM - 1.0)
        R[k] = c1 * np.concatenate([[0.0], R[k - 1][:-1]]) - c2 * R[k - 2]
    return np.stack(R).astype(np.float32)


MAGIC = 0x5F3759DF


def emit_rsqrt(E, out_f32, in_f32, t2_f32, newton=2):
    I32 = mybir.dt.int32
    oi = out_f32.bitcast(I32)
    E.tensor_scalar(oi, in_f32.bitcast(I32), 1, None, ALU.arith_shift_right)
    E.tensor_scalar(oi, oi, -1, None, ALU.bitwise_xor)
    E.tensor_scalar(oi, oi, MAGIC + 1, None, ALU.add)
    for _ in range(newton):
        E.tensor_tensor(t2_f32, out_f32, out_f32, ALU.mult)
        E.tensor_tensor(t2_f32, t2_f32, in_f32, ALU.mult)
        E.tensor_scalar(t2_f32, t2_f32, -0.5, 1.5, ALU.mult, ALU.add)
        E.tensor_tensor(out_f32, out_f32, t2_f32, ALU.mult)


class Sem:
    __slots__ = ("h", "n", "name")

    def __init__(self, nc, ctx, name):
        self.h = ctx.enter_context(nc.semaphore(name))
        self.n = 0
        self.name = name


class EB:

    def __init__(self):
        self.q = {"tensor": [], "vector": [], "scalar": [], "gpsimd": [], "sync": []}

    def op(self, eng, build, waits=(), incs=()):
        for s, v in waits:
            assert 0 <= v <= s.n, f"wait {s.name}>={v} pledged {s.n}"
        for s, n in incs:
            s.n += n

        def f(E, build=build, waits=tuple(waits), incs=tuple(incs)):
            for s, v in waits:
                if v > 0:
                    E.wait_ge(s.h, v)
            inst = build(E)
            for s, n in incs:
                inst.then_inc(s.h, n)
        self.q[eng].append(f)

    def emit(self, nc):
        with nc.Block() as blk:
            for name, fns in self.q.items():
                if fns:
                    getattr(blk, name)(lambda E, fns=fns: [f(E) for f in fns])


def small_chain(E, sem, n_start):
    state = {"n": n_start}

    def go(make, inc=True):
        E.wait_ge(sem.h, state["n"])
        inst = make()
        if inc:
            inst.then_inc(sem.h, 1)
            state["n"] += 1
        return inst
    return go


def build_nc(debug=0, phases=3):
    nc = bass.Bass("TRN2", target_bir_lowering=False)

    def din(name, shape, dt=F32):
        return nc.dram_tensor(name, shape, dt, kind="ExternalInput").ap()

    def dout(name, shape, dt=F32):
        return nc.dram_tensor(name, shape, dt, kind="ExternalOutput").ap()

    dbg_names = ("xnf", "qf", "kf", "vt", "sS") if debug else ()

    def dint(name, shape, dt):
        kind = "ExternalOutput" if name in dbg_names else "Internal"
        return nc.dram_tensor(name, shape, dt, kind=kind).ap()

    x = din("x", [TOK, DD])
    wqT = din("wqT", [DD, DD]); wkT = din("wkT", [DD, DD])
    wvT = din("wvT", [DD, DD]); woT = din("woT", [DD, DD])
    rwT = din("rwT", [DD, NE])
    polyT = din("polyT", [9, 16]); hdrow = din("hdrow", [1, 16])
    nw2 = din("nw2", [128, 1])
    gwi = din("gw", [NE, DD, DFF]); uwi = din("uw", [NE, DD, DFF])
    dwi = din("dw", [NE, DFF, DD])
    gbl = din("gbl", [128, 128]); ubl = din("ubl", [128, 128])
    dbi = din("db", [NE, DD])
    identF = din("identF", [128, 128]); identB = din("identB", [128, 128], BF16)
    rmat = din("rmat", [9, 9]); kcol = din("kcol", [9, 1])
    dvl = din("dvl", [128, 16]); dmc = din("dmc", [128, 2048], BF16)

    out = dout("out", [TOK, DD])
    dbgt = {}
    if debug:
        dbgt["d_probs"] = dout("d_probs", [128, NCH, NE])
        dbgt["d_lns"] = dout("d_lns", [NCH, 128, 12])

    xnf = dint("xnf", [8, 128, TOK], BF16)
    qf = dint("qf", [8, 128, TOK], BF16)
    kf = dint("kf", [8, 128, TOK], BF16)
    vt = dint("vt", [TOK, DD], BF16)
    sS = dint("sS", [TOK, DD], F32)
    gwb = dint("gwb", [NE, DD, DFF], BF16)
    uwb = dint("uwb", [NE, DD, DFF], BF16)
    dwb = dint("dwb", [NE, DFF, DD], BF16)
    aTd = dint("aTd", [9, 16], F32)
    pfm = dint("pfm", [NE, NCH, 128], BF16)
    tens = dict(x=x, out=out, xnf=xnf, qf=qf, kf=kf, vt=vt, sS=sS,
                gwb=gwb, uwb=uwb, dwb=dwb, pfm=pfm)

    with ExitStack() as pctx:
        def psb(name, shape, dt=F32):
            return pctx.enter_context(nc.sbuf_tensor(name, shape, dt))

        P = {}
        for name, shape, dt in [
                ("ones_r", [1, 144], F32), ("sidF", [128, 128], F32),
                ("sidB", [128, 128], BF16), ("A_bc", [128, 144], F32),
                ("A_bcb", [128, 144], BF16),
                ("dvls", [128, 16], F32), ("dms", [128, 2048], BF16),
                ("WoTb", [128, 8, DD], BF16), ("probs_res", [128, NCH, NE], F32),
                ("pfm_st", [8, 128], BF16), ("gbs", [128, 128], F32),
                ("ubs", [128, 128], F32), ("dbs", [8, DD], BF16),
                ("nws", [128, 1], F32)]:
            P[name] = psb(name, shape, dt)

        S = {}
        for name in ["sC0", "sFW", "sAW", "gPE", "gV", "sEPC", "sVAR", "sVC",
                     "sX0", "sX1", "sST", "sSQ", "sNRM", "sXP", "sXE", "sXC",
                     "sRT", "sPR", "sPTR", "sPFM", "sWST", "sGD",
                     "sQM", "sQA", "sQV", "sQEq", "sQEv", "sQTP", "sQTE",
                     "sSPV0", "sSPV1", "sSPx", "sSPq", "sSPk", "sPFC", "fPF",
                     "sEC0", "sEC1", "sEC2",
                     "aQ", "aK0", "aK1", "aV0", "aV1", "aSIM", "aEV", "aPQ",
                     "aPHI", "aAV", "aEPI", "aEPV", "aEPA", "aTP", "aTE",
                     "aWO", "aWOE", "aSP0", "aSP1", "aSPD",
                     "fXN0", "fXN1", "fG0", "fG1", "fU0", "fU1", "fD", "fPG", "fPU", "fSIL",
                     "fH", "fPY", "fYE", "fBI", "fBIC", "fXSx0", "fXSx1", "fXSs0", "fXSs1", "fGT", "cACT", "cPOOL", "cDVE",
                     "fCT", "fOUT0", "fOUT1", "dC"]:
            S[name] = Sem(nc, pctx, name)

        consts = dict(identF=identF, identB=identB, rmat=rmat, kcol=kcol,
                      dvl=dvl, dmc=dmc, gbl=gbl, ubl=ubl, db=dbi, nw2=nw2,
                      polyT=polyT, hdrow=hdrow, rwT=rwT,
                      wqT=wqT, wkT=wkT, wvT=wvT, woT=woT,
                      gwi=gwi, uwi=uwi, dwi=dwi, aTd=aTd)
        meta = build_phase01(nc, S, P, tens, consts, dbgt, debug, phases)
        if phases >= 2:
            build_phase23(nc, S, P, tens, meta, dbgt, debug, phases)

    return nc, dbgt


def build_phase01(nc, S, P, tens, C, dbgt, debug, phases):
    x, xnf, qf, kf, vt = tens["x"], tens["xnf"], tens["qf"], tens["kf"], tens["vt"]
    with ExitStack() as c1:
        def sb1(name, shape, dt=F32):
            return c1.enter_context(nc.sbuf_tensor(name, shape, dt))

        def ps1(name, shape, dt=F32):
            return c1.enter_context(nc.psum_tensor(name, shape, dt))

        WqTb = sb1("WqTb", [128, 8, DD], BF16)
        WkTb = sb1("WkTb", [128, 8, DD], BF16)
        WvTb = sb1("WvTb", [128, 8, DD], BF16)
        wstage = sb1("wstage", [128, DD])
        rwTs = sb1("rwTs", [128, 8, NE])
        ptys = sb1("ptys", [9, 16]); hds = sb1("hds", [1, 16])
        rms = sb1("rms", [9, 9]); kcs = sb1("kcs", [9, 1])
        gats = sb1("gats", [9, 16]); wgs = sb1("wgs", [9, 16])
        aTs = sb1("aTs", [9, 16]); arow = sb1("arow", [1, 144])
        sx = [sb1(f"sx{i}", [128, DD]) for i in range(2)]
        lns = sb1("lns", [128, 12])
        sqh = [sb1(f"sqh{i}", [128, 512]) for i in range(2)]
        sxn = [sb1(f"sxn{i}", [128, DD]) for i in range(2)]
        xnfm = [sb1(f"xnfm{i}", [128, 8, 128]) for i in range(2)]
        xnstage = sb1("xnstage", [128, 8, 1024], BF16)
        qstage = sb1("qstage", [128, 8, 1024], BF16)
        kstage = sb1("kstage", [128, 8, 1024], BF16)
        qtb = sb1("qtb", [128, DD], BF16)
        ktb = sb1("ktb", [128, DD], BF16)
        vtb = [sb1(f"vtb{i}", [128, DD], BF16) for i in range(2)]
        qks = sb1("qks", [128, 32])
        prw = sb1("prw", [128, 24])

        ptr = ps1("ptr", [128, 8, 128])
        pqk = [ps1(f"pqk{i}", [128, 512]) for i in range(2)]
        psm = ps1("psm", [128, 160])
        pqb = ps1("pqb", [128, 16, 128], BF16)

        eb = EB()

        for dst, src in [(P["sidF"][:], C["identF"]), (P["sidB"][:], C["identB"]),
                         (P["dvls"][:], C["dvl"]), (P["dms"][:], C["dmc"]),
                         (P["gbs"][:], C["gbl"]), (P["ubs"][:], C["ubl"]),
                         (P["nws"][:], C["nw2"]),
                         (ptys[:], C["polyT"]), (hds[:], C["hdrow"]),
                         (rms[:], C["rmat"]), (kcs[:], C["kcol"]),
                         (rwTs[:],
                          C["rwT"].rearrange("(c p) e -> p c e", p=128))]:
            eb.op("sync", lambda E, d=dst, s=src: E.dma_start(d, s),
                  waits=[(S["sC0"], S["sC0"].n)], incs=[(S["sC0"], 16)])
        C0 = S["sC0"].n

        for e in range(NE):
            for dsrc, ddst in ((C["gwi"], tens["gwb"]), (C["uwi"], tens["uwb"]),
                               (C["dwi"], tens["dwb"])):
                eb.op("gpsimd", lambda E, a=ddst[e], b=dsrc[e]: E.dma_start(a, b),
                      waits=[(S["sFW"], S["sFW"].n)], incs=[(S["sFW"], 16)])

        eb.op("gpsimd", lambda E: E.dma_start(P["dbs"][:], C["db"]),
              waits=[(S["sAW"], S["sAW"].n)], incs=[(S["sAW"], 16)])
        for dst, src in ((WqTb, C["wqT"]), (WkTb, C["wkT"]), (WvTb, C["wvT"])):
            eb.op("gpsimd", lambda E, d=dst, s=src: E.dma_start(
                d[:], s.rearrange("(c p) f -> p c f", p=128)),
                waits=[(S["sAW"], S["sAW"].n)], incs=[(S["sAW"], 16)])
        CASTW = S["sAW"].n

        for jc in range(8):
            wl = [] if jc == 0 else [(S["gV"], jc)]
            eb.op("sync", lambda E, jc=jc: E.dma_start(
                wstage[:], C["woT"][jc * 128:(jc + 1) * 128, :]),
                waits=wl, incs=[(S["sWST"], 16)])
            eb.op("vector", lambda E, jc=jc: E.tensor_scalar_mul(
                P["WoTb"][:, jc, :], wstage[:], P["nws"][:, 0:1]),
                waits=[(S["sWST"], (jc + 1) * 16), (S["sC0"], C0)],
                incs=[(S["gV"], 1)])
        C1 = S["sC0"].n

        eb.op("vector", lambda E: E.memset(P["ones_r"][:], 1.0), incs=[(S["gV"], 1)])
        gv0 = S["gV"].n
        eb.op("tensor", lambda E: E.matmul(psm[0:9, 0:16], P["ones_r"][0:1, 0:9],
                                           hds[:], start=True, stop=True),
              waits=[(S["sC0"], C0), (S["gV"], gv0)], incs=[(S["gPE"], 1)])
        vc0 = S["sVC"].n

        def gchain(E, vc0=vc0):
            E.wait_ge(S["gPE"].h, 1)
            go = small_chain(E, S["sVC"], vc0)
            go(lambda: E.tensor_scalar(gats[:], psm[0:9, 0:16], kcs[:, 0:1], 1.0,
                                       ALU.subtract, ALU.add))
            go(lambda: E.tensor_scalar(gats[:], gats[:], 0.0, 1.0, ALU.max, ALU.min))
            i = go(lambda: E.tensor_tensor(wgs[:], ptys[:], gats[:], ALU.mult),
                   inc=False)
            i.then_inc(S["gV"].h, 3)
        eb.q["vector"].append(gchain)
        S["sVC"].n += 2
        S["gV"].n += 3
        eb.op("tensor", lambda E: E.matmul(psm[0:9, 16:32], rms[:], wgs[:],
                                           start=True, stop=True),
              waits=[(S["gV"], gv0 + 3)], incs=[(S["gPE"], 1)])
        eb.op("vector", lambda E: E.tensor_copy(aTs[:], psm[0:9, 16:32]),
              waits=[(S["gPE"], 2)], incs=[(S["gV"], 1)])
        eb.op("sync", lambda E: E.dma_start(C["aTd"], aTs[:]),
              waits=[(S["gV"], gv0 + 4)], incs=[(S["sGD"], 16)])
        eb.op("sync", lambda E: E.dma_start(
            arow[:].rearrange("a (h j) -> a h j", j=9),
            C["aTd"].transpose([1, 0]).unsqueeze(0),
            allow_slow_non_contiguous=True),
            waits=[(S["sGD"], 16)], incs=[(S["sGD"], 16)])
        eb.op("tensor", lambda E: E.matmul(psm[:, 16:160], P["ones_r"][0:1, 0:128],
                                           arow[:], start=True, stop=True),
              waits=[(S["sGD"], 32), (S["gV"], gv0)], incs=[(S["gPE"], 1)])
        eb.op("vector", lambda E: E.tensor_copy(P["A_bc"][:], psm[:, 16:160]),
              waits=[(S["gPE"], 3)], incs=[(S["gV"], 1)])
        eb.op("vector", lambda E: E.tensor_copy(P["A_bcb"][:], psm[:, 16:160]),
              incs=[(S["gV"], 1)])

        deltas0 = None
        spill_after = {}
        for ct in range(NCH_L):
            b = ct % 2
            tile, sl = ct // 8, ct % 8
            base = {k: S[k].n for k in S}
            epw = [(S["sEC0"], max(0, ct - 1)), (S["sEC1"], max(0, ct - 1)),
                   (S["sEC2"], max(0, ct - 1))]

            sxp = S["sX0"] if b == 0 else S["sX1"]
            eb.op("sync", lambda E, ct=ct, b=b: E.dma_start(
                sx[b][:], x[ct * 128:(ct + 1) * 128, :]),
                waits=[(S["sST"], max(0, ct - 1))], incs=[(sxp, 16)])

            eb.op("scalar", lambda E, b=b: E.activation(sqh[0][:], sx[b][:, 0:512],
                                                        AF.Square),
                  waits=epw + [(sxp, 16 * (ct // 2 + 1)),
                               (S["sQV"], max(0, 5 * ct))],
                  incs=[(S["sSQ"], 1)])
            eb.op("scalar", lambda E, b=b: E.activation(sqh[1][:], sx[b][:, 512:1024],
                                                        AF.Square),
                  incs=[(S["sSQ"], 1)])

            vcs = S["sVC"].n

            def ln_stats(E, ct=ct, b=b, epv=epw[0][1], vcs=vcs, sxp=sxp):
                E.wait_ge(S["sEC0"].h, epv)
                E.wait_ge(S["sEC1"].h, epv)
                E.wait_ge(S["sEC2"].h, epv)
                E.wait_ge(sxp.h, 16 * (ct // 2 + 1))
                E.wait_ge(S["sSQ"].h, 2 * (ct + 1))
                go = small_chain(E, S["sVC"], vcs)
                E.reduce_sum(lns[:, 0:1], sx[b][:], AX.X)
                E.reduce_sum(lns[:, 1:2], sqh[0][:], AX.X)
                go(lambda: E.reduce_sum(lns[:, 2:3], sqh[1][:], AX.X))
                go(lambda: E.tensor_scalar_mul(lns[:, 3:4], lns[:, 0:1], 1.0 / DD))
                go(lambda: E.tensor_add(lns[:, 4:5], lns[:, 1:2], lns[:, 2:3]))
                go(lambda: E.tensor_scalar_mul(lns[:, 5:6], lns[:, 4:5], 1.0 / DD))
                go(lambda: E.tensor_mul(lns[:, 6:7], lns[:, 3:4], lns[:, 3:4]))
                go(lambda: E.scalar_tensor_tensor(lns[:, 7:8], lns[:, 6:7], -1.0,
                                                  lns[:, 5:6], ALU.mult, ALU.add))
                i = go(lambda: E.tensor_scalar_add(lns[:, 8:9], lns[:, 7:8], 1e-5),
                       inc=False)
                i.then_inc(S["sVAR"].h, 1)
            eb.q["vector"].append(ln_stats)
            S["sVC"].n += 6
            S["sVAR"].n += 1
            eb.op("scalar", lambda E: E.activation(lns[:, 9:10], lns[:, 8:9], AF.Ln),
                  waits=[(S["sVAR"], ct + 1)], incs=[(S["cACT"], 1)])
            eb.op("scalar", lambda E: E.activation(lns[:, 10:11], lns[:, 9:10],
                                                   AF.Exp, scale=-0.5),
                  waits=[(S["cACT"], S["cACT"].n)], incs=[(S["sQA"], 1)])
            qa_rstd = S["sQA"].n

            vcn = S["sVC"].n

            def ln_nbias(E, b=b, qa_rstd=qa_rstd):
                E.wait_ge(S["sQA"].h, qa_rstd)
                i = E.scalar_tensor_tensor(lns[:, 11:12], lns[:, 3:4], -1.0,
                                           lns[:, 10:11], ALU.mult, ALU.mult)
                i.then_inc(S["sQV"].h, 1)
            eb.q["vector"].append(ln_nbias)
            S["sQV"].n += 1
            nb_now = S["sQV"].n
            eb.op("scalar", lambda E, b=b: E.activation(
                sxn[b][:], sx[b][:], AF.Identity, bias=lns[:, 11:12],
                scale=lns[:, 10:11]),
                waits=[(S["sQV"], nb_now), (S["cACT"], S["cACT"].n)],
                incs=[(S["sST"], 1)])
            S["sNRM"].n += 1
            if debug:
                eb.op("scalar", lambda E, ct=ct: E.dma_start(
                    dbgt["d_lns"][ct], lns[:]),
                    waits=[(S["sST"], ct + 1)], incs=[(S["dC"], 16)])

            rmap = lambda j: (j % 2) * 4 + j // 2
            for dc in range(8):
                wl = []
                if dc == 0:
                    wl = epw + [(S["sST"], ct + 1), (S["sXE"], max(0, ct * 8))]
                elif dc >= 2:
                    wl = [(S["sXE"], ct * 8 + dc - 1)]
                eb.op("tensor", lambda E, b=b, dc=dc, r=rmap(dc): E.transpose(
                    ptr[:, r, :], sxn[b][:, dc * 128:(dc + 1) * 128], P["sidF"][:]),
                    waits=wl, incs=[(S["sXP"], 1)])
                eb.op("scalar", lambda E, b=b, dc=dc, r=rmap(dc): E.activation(
                    xnfm[b][:, dc, :], ptr[:, r, :], AF.Identity),
                    waits=[(S["sXP"], ct * 8 + dc + 1)] +
                          ([(S["sXC"], max(0, ct - 1))] if dc == 0 else []),
                    incs=[(S["sXE"], 1)])
            eb.op("vector", lambda E, b=b, sl=sl: E.tensor_copy(
                xnstage[:, :, sl * 128:(sl + 1) * 128], xnfm[b][:]),
                waits=[(S["sXE"], (ct + 1) * 8),
                       (S["sSPx"], spill_after.get(tile - 1, 0) if sl == 0 else 0),
                       (S["sQM"], max(0, (ct - 7) * 6))],
                incs=[(S["sXC"], 1)])

            for dc in range(8):
                eb.op("tensor", lambda E, b=b, dc=dc: E.matmul(
                    psm[:, 0:8], xnfm[b][:, dc, :], rwTs[:, dc, :],
                    start=(dc == 0), stop=(dc == 7)),
                    waits=([(S["sXE"], ct * 8 + 8), (S["sPFC"], max(0, ct))]
                           if dc == 0 else []),
                    incs=([(S["sRT"], 1)] if dc == 7 else []))

            vpa = S["sVC"].n

            def probs_a(E, ct=ct, vpa=vpa):
                E.wait_ge(S["sRT"].h, ct + 1)
                go = small_chain(E, S["sVC"], vpa)
                E.tensor_copy(prw[:, 16:24], psm[:, 0:8])
                go(lambda: E.reduce_max(prw[:, 8:9], psm[:, 0:8], AX.X))
                i = go(lambda: E.tensor_scalar_mul(prw[:, 9:10], prw[:, 8:9], -1.0),
                       inc=False)
                i.then_inc(S["sPR"].h, 1)
            eb.q["vector"].append(probs_a)
            S["sVC"].n += 1
            S["sPR"].n += 1
            eb.op("scalar", lambda E: E.activation(prw[:, 0:8], prw[:, 16:24],
                                                   AF.Exp, bias=prw[:, 9:10]),
                  waits=[(S["sPR"], base["sPR"] + 1)], incs=[(S["sQA"], 1)])
            qa_pe = S["sQA"].n

            vpb = S["sVC"].n

            def probs_b(E, ct=ct, qa=qa_pe, vpb=vpb):
                E.wait_ge(S["sQA"].h, qa)
                go = small_chain(E, S["sVC"], vpb)
                go(lambda: E.reduce_sum(prw[:, 8:9], prw[:, 0:8], AX.X))
                go(lambda: E.reciprocal(prw[:, 9:10], prw[:, 8:9]))
                go(lambda: E.tensor_scalar_mul(prw[:, 0:8], prw[:, 0:8],
                                               prw[:, 9:10]))
                go(lambda: E.reduce_max(prw[:, 8:9], prw[:, 0:8], AX.X))
                go(lambda: E.tensor_scalar(prw[:, 10:18], prw[:, 0:8], prw[:, 8:9],
                                           None, ALU.is_ge))
                go(lambda: E.scalar_tensor_tensor(prw[:, 10:18], prw[:, 10:18],
                                                  -1e30, prw[:, 0:8],
                                                  ALU.mult, ALU.add))
                go(lambda: E.reduce_max(prw[:, 8:9], prw[:, 10:18], AX.X))
                go(lambda: E.tensor_scalar(prw[:, 10:18], prw[:, 0:8], prw[:, 8:9],
                                           None, ALU.is_ge))
                go(lambda: E.tensor_tensor(prw[:, 0:8], prw[:, 0:8], prw[:, 10:18],
                                           ALU.mult))
                go(lambda: E.reduce_sum(prw[:, 8:9], prw[:, 0:8], AX.X))
                go(lambda: E.tensor_scalar_add(prw[:, 8:9], prw[:, 8:9], 1e-9))
                go(lambda: E.reciprocal(prw[:, 9:10], prw[:, 8:9]))
                i = go(lambda: E.tensor_scalar_mul(P["probs_res"][:, ct, :],
                                                   prw[:, 0:8], prw[:, 9:10]),
                       inc=False)
                i.then_inc(S["sPR"].h, 1)
            eb.q["vector"].append(probs_b)
            S["sVC"].n += 12
            S["sPR"].n += 1
            eb.op("tensor", lambda E, ct=ct: E.transpose(
                psm[0:8, 16:144], P["probs_res"][:, ct, :], P["sidF"][:]),
                waits=[(S["sPR"], base["sPR"] + 2), (S["sPFC"], max(0, ct))],
                incs=[(S["sPTR"], 1)])
            eb.op("vector", lambda E, ct=ct: E.tensor_copy(
                P["pfm_st"][:], psm[0:8, 16:144]),
                waits=[(S["sPTR"], ct + 1), (S["sPFM"], max(0, 16 * ct))],
                incs=[(S["sPFC"], 1)])
            eb.op("scalar", lambda E, ct=ct: E.dma_start(
                tens["pfm"][:, ct, :], P["pfm_st"][:]),
                waits=[(S["sPFC"], ct + 1)], incs=[(S["sPFM"], 16)])

            qv_marks = []
            for mi, (Wb, kind) in enumerate(((WqTb, "q"), (WkTb, "k"), (WvTb, "v"))):
                for hf in range(2):
                    idx = 6 * ct + 2 * mi + hf
                    lidx = 2 * mi + hf
                    bank = idx % 2
                    for dc in range(8):
                        wl = []
                        if dc == 0:
                            wl = [(S["sXC"], ct + 1), (S["sAW"], CASTW)]
                            li2 = lidx - 2
                            if li2 >= -2 and idx >= 2:
                                if li2 >= 0:
                                    if li2 < 4:
                                        wl.append((S["sQEq"], 4 * ct + li2 + 1))
                                    else:
                                        wl.append((S["sQEv"], 2 * ct + li2 - 3))
                                else:
                                    wl.append((S["sQEv"], 2 * (ct - 1) + li2 + 3))
                        eb.op("tensor", lambda E, sl=sl, dc=dc, Wb=Wb, hf=hf,
                              bank=bank: E.matmul(
                                  pqk[bank][:],
                                  xnstage[:, dc, sl * 128:(sl + 1) * 128],
                                  Wb[:, dc, hf * 512:(hf + 1) * 512],
                                  start=(dc == 0), stop=(dc == 7)),
                              waits=wl, incs=([(S["sQM"], 1)] if dc == 7 else []))
                    mmn = S["sQM"].n
                    if kind == "v":
                        spv = S["sSPV0"] if b == 0 else S["sSPV1"]
                        eb.op("scalar", lambda E, bank=bank, hf=hf, b=b:
                              E.activation(
                                  vtb[b][:, hf * 512:(hf + 1) * 512],
                                  pqk[bank][:], AF.Identity),
                              waits=[(S["sQM"], mmn), (spv, 16 * (ct // 2))],
                              incs=[(S["sQEv"], 1)])
                    else:
                        dsttb = qtb if kind == "q" else ktb
                        sqb = sqh[lidx % 2]
                        sqw = ([(S["sST"], ct + 1)] if lidx < 2
                               else [(S["sQV"], qv_marks[lidx - 2])])
                        eb.op("scalar", lambda E, bank=bank, sqb=sqb: E.activation(
                            sqb[:], pqk[bank][:], AF.Square),
                            waits=[(S["sQM"], mmn)] + sqw, incs=[(S["sQA"], 1)])
                        qa1 = S["sQA"].n

                        vqs = S["sVC"].n

                        def qk_ss(E, sqb=sqb, qa1=qa1, vqs=vqs):
                            E.wait_ge(S["sQA"].h, qa1)
                            go = small_chain(E, S["sVC"], vqs)
                            go(lambda: E.reduce_sum(
                                qks[:, 0:8].unsqueeze(2),
                                sqb[:].rearrange("p (h d) -> p h d", h=8), AX.X))
                            i = go(lambda: E.tensor_scalar(qks[:, 8:16], qks[:, 0:8],
                                                           1e-24, None, ALU.max),
                                   inc=False)
                            i.then_inc(S["sQV"].h, 1)
                        eb.q["vector"].append(qk_ss)
                        S["sVC"].n += 1
                        S["sQV"].n += 1
                        qv_marks.append(S["sQV"].n)
                        eb.op("scalar", lambda E: E.activation(
                            qks[:, 16:24], qks[:, 8:16], AF.Ln),
                            waits=[(S["sQV"], S["sQV"].n)], incs=[(S["cACT"], 1)])
                        eb.op("scalar", lambda E: E.activation(
                            qks[:, 16:24], qks[:, 16:24], AF.Exp, scale=-0.5),
                            waits=[(S["cACT"], S["cACT"].n)], incs=[(S["sQA"], 1)])
                        qa2 = S["sQA"].n
                        eb.op("vector", lambda E, bank=bank, hf=hf, dsttb=dsttb:
                              E.tensor_tensor(
                                  dsttb[:, hf * 512:(hf + 1) * 512].rearrange(
                                      "p (h d) -> p h d", h=8),
                                  pqk[bank][:].rearrange("p (h d) -> p h d", h=8),
                                  qks[:, 16:24].unsqueeze(2).broadcast_to(
                                      [128, 8, 64]),
                                  ALU.mult),
                              waits=[(S["sQA"], qa2)], incs=[(S["sQEq"], 1)])
                if kind == "v":
                    pass

            qe_now = S["sQEq"].n
            rmap16 = lambda j: (j % 2) * 8 + j // 2
            for ti, (srctb, stage) in enumerate(((qtb, qstage), (ktb, kstage))):
                for dc in range(8):
                    j = ti * 8 + dc
                    wl = [(S["sQEq"], qe_now)] if j == 0 else []
                    if j >= 2:
                        wl.append((S["sQTE"], ct * 16 + j - 1))
                    eb.op("tensor", lambda E, srctb=srctb, dc=dc, r=rmap16(ti * 8 + dc):
                          E.transpose(pqb[:, r, :],
                                      srctb[:, dc * 128:(dc + 1) * 128], P["sidB"][:]),
                          waits=wl, incs=[(S["sQTP"], 1)])
                    wl2 = [(S["sQTP"], ct * 16 + j + 1)]
                    if dc == 0 and sl == 0:
                        wl2.append(((S["sSPq"] if ti == 0 else S["sSPk"]),
                                    spill_after.get(tile - 1, 0)))
                    eb.op("scalar", lambda E, stage=stage, dc=dc, sl=sl,
                          r=rmap16(ti * 8 + dc):
                          E.activation(stage[:, dc, sl * 128:(sl + 1) * 128],
                                       pqb[:, r, :], AF.Identity),
                          waits=wl2, incs=[(S["sQTE"], 1)])
            spv = S["sSPV0"] if b == 0 else S["sSPV1"]
            eb.op("scalar", lambda E, ct=ct, b=b: E.dma_start(
                vt[ct * 128:(ct + 1) * 128, :], vtb[b][:]),
                waits=[(S["sQEv"], 2 * (ct + 1))], incs=[(spv, 16)])

            if sl == 7:
                te_now = S["sQTE"].n
                xc_now = S["sXC"].n
                for stage, dst, ssem in ((xnstage, xnf, S["sSPx"]),
                                         (qstage, qf, S["sSPq"]),
                                         (kstage, kf, S["sSPk"])):
                    wl = ([(S["sXC"], xc_now)] if stage is xnstage
                          else [(S["sQTE"], te_now)])
                    eb.op("scalar", lambda E, stage=stage, dst=dst, tile=tile:
                          E.dma_start(
                              dst.rearrange("c p t -> p c t")[
                                  :, :, tile * 1024:(tile + 1) * 1024],
                              stage[:]),
                          waits=wl, incs=[(ssem, 16)])
                spill_after[tile] = 16 * (tile + 1)

            eb.op("vector", lambda E: E.tensor_copy(qks[:, 24:25], qks[:, 0:1]),
                  incs=[(S["sEC0"], 1)])
            eb.op("scalar", lambda E: E.copy(qks[:, 25:26], qks[:, 0:1]),
                  incs=[(S["sEC1"], 1)])
            eb.op("tensor", lambda E: E.transpose(
                pqb[:, 15, :], ktb[:, 896:1024], P["sidB"][:]),
                waits=[(S["sQTE"], ct * 16 + 16)], incs=[(S["sEC2"], 1)])

            d = {}
            for k2 in S:
                if S[k2].n != base[k2]:
                    key = k2.rstrip("01") if k2[-1] in "01" else k2
                    d[key] = d.get(key, 0) + S[k2].n - base[k2]
            if deltas0 is None and sl not in (0, 7):
                deltas0 = d
            elif sl not in (0, 7) and deltas0 is not None:
                assert d == deltas0, f"chunk {ct} deltas {d} != {deltas0}"

        meta = dict(SPx=S["sSPx"].n, SPq=S["sSPq"].n, SPk=S["sSPk"].n,
                    SPV0=S["sSPV0"].n, SPV1=S["sSPV1"].n, CASTW=CASTW)

        if debug:
            eb.op("scalar", lambda E: E.dma_start(dbgt["d_probs"], P["probs_res"][:]),
                  waits=[(S["sPR"], S["sPR"].n)], incs=[(S["dC"], 16)])
        if phases == 1:
            dcw = S["dC"].n
            fins = [(S["sSPx"], S["sSPx"].n), (S["sSPq"], S["sSPq"].n),
                    (S["sSPk"], S["sSPk"].n), (S["sSPV0"], S["sSPV0"].n),
                    (S["sSPV1"], S["sSPV1"].n), (S["sFW"], S["sFW"].n),
                    (S["dC"], dcw), (S["sPFM"], S["sPFM"].n)]

            def fin2(E, fins=fins):
                for s, v in fins:
                    if v:
                        E.wait_ge(s.h, v)
            eb.q["sync"].append(fin2)
        eb.emit(nc)
    return meta


def build_phase23(nc, S, P, tens, meta, dbgt, debug, phases):
    x, out, sS = tens["x"], tens["out"], tens["sS"]
    qf, kf, vt = tens["qf"], tens["kf"], tens["vt"]
    xnf, gwb, uwb, dwb = tens["xnf"], tens["gwb"], tens["uwb"], tens["dwb"]
    P1B = [("sSPx", meta["SPx"]), ("sSPq", meta["SPq"]), ("sSPk", meta["SPk"]),
           ("sSPV0", meta["SPV0"]), ("sSPV1", meta["SPV1"])]
    with ExitStack() as c2:
        def sb2(name, shape, dt=F32):
            return c2.enter_context(nc.sbuf_tensor(name, shape, dt))

        def ps2(name, shape, dt=F32):
            return c2.enter_context(nc.psum_tensor(name, shape, dt))

        qtl = [sb2("qtl0", [128, 8, 512], BF16)] * 2
        ksp = [sb2(f"ksp{i}", [128, 8, 256], BF16) for i in range(2)]
        vsp = [sb2(f"vsp{i}", [128, 2, 1024], BF16) for i in range(2)]
        st_s = [sb2(f"st_s{i}", [128, 2, 512], BF16) for i in range(4)]
        st_z = [sb2(f"st_z{i}", [128, 2, 512], BF16) for i in range(4)]
        stp2 = [sb2(f"st_p{i}", [128, 1024], BF16) for i in range(2)]
        st_q = [sb2(f"st_q{i}", [128, 1024], BF16) for i in range(2)]
        st_f = [sb2(f"st_f{i}", [128, 2, 512], BF16) for i in range(3)]
        ao_hq = sb2("ao_hq", [128, 4, 4, 64])
        ao_fm = sb2("ao_fm", [128, 8, 512], BF16)
        scr1 = sb2("scr1", [128, 1024])
        est = sb2("est", [128, 64])
        xtile = [sb2(f"xtile{i}", [128, 8, 512], BF16) for i in range(2)]
        gch = [sb2(f"gch{i}", [128, 8, 256], BF16) for i in range(2)]
        uch = [sb2(f"uch{i}", [128, 8, 256], BF16) for i in range(2)]
        Dsb = sb2("Dsb", [128, 16, 1024], BF16)
        hsb = sb2("hsb", [128, 16, 512], BF16)
        y_acc = sb2("y_acc", [128, 4, 1024])
        hs_scr = [sb2(f"hs_scr{i}", [128, 512]) for i in range(2)]
        pfm_sb = sb2("pfm_sb", [8, 512], BF16)
        xts = [sb2(f"xts{i}", [128, 512]) for i in range(2)]
        sts = [sb2(f"sts{i}", [128, 512]) for i in range(2)]

        psim = [ps2(f"psim{i}", [128, 512]) for i in range(2)]
        pav = ps2("pav", [128, 1024])
        pg = ps2("pg", [128, 512])
        pu = ps2("pu", [128, 512])
        py = [ps2(f"py{i}", [128, 512]) for i in range(2)]

        eb = EB()
        A = S

        mk = dict(sim_slice={}, woe_slice={}, sp_slice={}, kv={}, te=0, tp=0,
                  epa=0, epv=0, epi=0, wo_slice={}, ffn_mm_tt={}, ye_tt={},
                  ct_tt={}, gt_tt={}, fout_tt={})
        phi_ctr = [0]
        kv_ctr = [0]
        pend_av = []

        def co(h, j):
            return P["A_bc"][:, h * 9 + j:h * 9 + j + 1]

        def coB(h, j):
            return P["A_bcb"][:, h * 9 + j:h * 9 + j + 1].broadcast_to([128, 1024])

        def emit_av(pidx, k, qt, sp, hh, h, sb_, kvb, first_q, nsp):
            sb_ = pidx % 3
            wl = [(A["aPHI"], pidx + 1)]
            if first_q:
                wl.append((A["aTE"], mk["te"]))
                wl.append((A["aEPI"], mk["epi"]))
            valid = [(kcw, ts) for kcw in range(2) for ts in range(4)
                     if 2 * sp + kcw <= 4 * qt + ts]
            for n_i, (kcw, ts) in enumerate(valid):
                st = (sp == 0 and kcw == 0 and ts == 0 and hh in (0, 2))
                stp = (2 * sp + kcw == 4 * qt + ts)
                eb.op("tensor", lambda E, kcw=kcw, ts=ts, hh=hh, h=h, sb_=sb_,
                      kvb=kvb, st=st, stp=stp: E.matmul(
                          pav[:, hh * 256 + ts * 64:hh * 256 + (ts + 1) * 64],
                          st_f[sb_][:, kcw, ts * 128:(ts + 1) * 128],
                          vsp[kvb][:, kcw, h * 64:(h + 1) * 64],
                          start=st, stop=stp, skip_group_check=True),
                      waits=(wl if n_i == 0 else []),
                      incs=([(A["aAV"], 1)] if n_i == len(valid) - 1 else []))

        def attn_slice(k):
            seq, qt = divmod(k, 4)
            nsp = 2 * (qt + 1)
            t0 = k * 512
            qb = k % 2
            wl = [(S[n], v) for n, v in P1B]
            if k >= 1:
                wl.append((A["aSIM"], mk["sim_slice"][k - 1]))
            eb.op("sync", lambda E, qb=qb, t0=t0: E.dma_start(
                qtl[qb][:], qf.rearrange("c p t -> p c t")[:, :, t0:t0 + 512]),
                waits=wl, incs=[(A["aQ"], 16)])
            aQ_k = A["aQ"].n
            yield 1.0
            def kv_load(sp):
                kvi = kv_ctr[0]
                kv_ctr[0] += 1
                kvb = kvi % 2
                tk0 = seq * 2048 + sp * 256
                sK = A["aK0"] if kvb == 0 else A["aK1"]
                sV = A["aV0"] if kvb == 0 else A["aV1"]
                wlk = [(S[n], v) for n, v in P1B]
                wlv = []
                if kvi >= 2:
                    simm, avm = mk["kv"][kvi - 2]
                    wlk.append((A["aSIM"], simm))
                    wlv.append((A["aAV"], avm))
                    while A["aAV"].n < avm and pend_av:
                        pend_av.pop(0)()
                eb.op("sync", lambda E, kvb=kvb, tk0=tk0: E.dma_start(
                    ksp[kvb][:], kf.rearrange("c p t -> p c t")[:, :, tk0:tk0 + 256]),
                    waits=wlk, incs=[(sK, 16)])
                eb.op("sync", lambda E, kvb=kvb, tk0=tk0: E.dma_start(
                    vsp[kvb][:],
                    vt.rearrange("(a w p) d -> a p w d", p=128, w=2)[tk0 // 256]),
                    waits=wlv, incs=[(sV, 16)])
                return kvi, kvb, (sK.n, sV.n, sK, sV)

            CP, CV, CA = S["cPOOL"], S["cDVE"], S["cACT"]

            def emit_prep(hq, sp, hpair, kvi, kvb, kvn, diag):
                CH = []
                for cc in range(2):
                    hh = hpair + cc
                    h = hq * 4 + hh
                    pidx = phi_ctr[0]
                    phi_ctr[0] += 1
                    CH.append(dict(hh=hh, h=h, pidx=pidx, sb=pidx % 4,
                                   c=h // 2, par=(h % 2) * 64))
                for ch in CH:
                    pidx, sb_ = ch["pidx"], ch["sb"]
                    for kcw in range(2):
                        ei = 2 * pidx + kcw
                        bank = ei % 2
                        wl = []
                        if kcw == 0:
                            wl = [(kvn[2], kvn[0]), (kvn[3], kvn[1]),
                                  (A["aQ"], aQ_k)]
                        if ei >= 2:
                            wl.append((A["aEV"], 2 * (ei - 2) + 2))
                        eb.op("tensor", lambda E, c=ch["c"], par=ch["par"],
                              kcw=kcw, bank=bank, kvb=kvb, qb=qb: E.matmul(
                                  psim[bank][:],
                                  ksp[kvb][par:par + 64, c,
                                           kcw * 128:(kcw + 1) * 128],
                                  qtl[qb][par:par + 64, c, :],
                                  start=True, stop=True),
                              waits=wl, incs=[(A["aSIM"], 1)])
                        smn = A["aSIM"].n
                        wl2 = [(A["aSIM"], smn)]
                        if pidx >= 4:
                            wl2.append((A["aPHI"], 2 * (pidx // 2) - 2))
                        eb.op("scalar", lambda E, bank=bank, sb_=sb_,
                              kcw=kcw: E.activation(
                                  st_s[sb_][:, kcw, :], psim[bank][:],
                                  AF.Identity),
                              waits=wl2, incs=[(A["aEV"], 2)])
                    ch["evn"] = A["aEV"].n
                for ch in CH:
                    sb_, pidx = ch["sb"], ch["pidx"]
                    zc = st_z[sb_][:].rearrange("p w t -> p (w t)")
                    sc = st_s[sb_][:].rearrange("p w t -> p (w t)")
                    ch["zc"], ch["sc"] = zc, sc
                    ch["qs"] = st_q[pidx % 2][:]
                    eb.op("gpsimd", lambda E, zc=zc, sc=sc: E.tensor_tensor(
                        zc, sc, sc, ALU.mult),
                        waits=[(A["aEV"], ch["evn"]),
                               (A["aPHI"], max(0, 2 * (pidx // 2) - 2)),
                               (CP, CP.n)],
                        incs=[(CP, 1)])
                    ch["zn"] = CP.n
                return dict(CH=CH, kvi=kvi, kvb=kvb, diag=diag, sp=sp,
                            simn=A["aSIM"].n)

            def emit_chain(g, foq, k, qt, nsp):
                CH, kvb, diag, sp = g["CH"], g["kvb"], g["diag"], g["sp"]
                while len(pend_av) > 1:
                    pend_av.pop(0)()
                for ch in CH:
                    h, pidx = ch["h"], ch["pidx"]
                    pst = stp2[pidx % 2][:]
                    ch["pst"] = pst
                    eb.op("scalar", lambda E, zc=ch["zc"], h=h, pst=pst:
                          E.activation(pst, zc, AF.Identity,
                                       bias=co(h, 6), scale=co(h, 8)),
                          waits=[(CP, ch["zn"]),
                                 (A["aPHI"], max(0, pidx - 1)),
                                 (CA, max(0, CA.n - 1))],
                          incs=[(CA, 1)])
                    eb.op("scalar", lambda E, zc=ch["zc"], h=h,
                          qs=ch["qs"]: E.activation(
                              qs, zc, AF.Identity, bias=co(h, 5),
                              scale=co(h, 7)),
                          waits=[(CP, ch["zn"]),
                                 (A["aPHI"], max(0, pidx - 1)),
                                 (CA, max(0, CA.n - 1))],
                          incs=[(CA, 1)])
                    ch["initn"] = CA.n

                def dve(ch, f, waits=()):
                    eb.op("vector", f,
                          waits=list(waits) + [(CV, max(0, CV.n - 1))],
                          incs=[(CV, 1)])
                    return CV.n

                def act(ch, f, waits=()):
                    eb.op("scalar", f,
                          waits=list(waits) + [(CA, max(0, CA.n - 1))],
                          incs=[(CA, 1)])
                    return CA.n
                for ch in CH:
                    ch["m"] = dve(ch, lambda E, p=ch["pst"], zc=ch["zc"]:
                                  E.tensor_tensor(p, p, zc, ALU.mult),
                                  [(CA, ch["initn"])])
                for ch in CH:
                    bc4 = coB(ch["h"], 4)
                    eb.op("gpsimd", lambda E, p=ch["pst"], b=bc4:
                          E.tensor_tensor(p, p, b, ALU.add),
                          waits=[(CV, ch["m"]), (CP, max(0, CP.n - 1))],
                          incs=[(CP, 1)])
                    ch["mp"] = CP.n
                for ch in CH:
                    ch["mq"] = dve(ch, lambda E, q=ch["qs"], zc=ch["zc"]:
                                   E.tensor_tensor(q, q, zc, ALU.mult))
                for ch in CH:
                    ch["m"] = dve(ch, lambda E, p=ch["pst"], zc=ch["zc"]:
                                  E.tensor_tensor(p, p, zc, ALU.mult),
                                  [(CP, ch["mp"])])
                for ch in CH:
                    ch["aq"] = act(ch, lambda E, q=ch["qs"], h=ch["h"]:
                                   E.activation(q, q, AF.Identity,
                                                bias=co(h, 3)),
                                   [(CV, ch["mq"])])
                for ch in CH:
                    ch["ap"] = act(ch, lambda E, p=ch["pst"], h=ch["h"]:
                                   E.activation(p, p, AF.Identity,
                                                bias=co(h, 2)),
                                   [(CV, ch["m"])])
                for ch in CH:
                    ch["mq"] = dve(ch, lambda E, q=ch["qs"], zc=ch["zc"]:
                                   E.tensor_tensor(q, q, zc, ALU.mult),
                                   [(CA, ch["aq"])])
                for ch in CH:
                    ch["m"] = dve(ch, lambda E, p=ch["pst"], zc=ch["zc"]:
                                  E.tensor_tensor(p, p, zc, ALU.mult),
                                  [(CA, ch["ap"])])
                for ch in CH:
                    bc1 = coB(ch["h"], 1)
                    eb.op("gpsimd", lambda E, q=ch["qs"], b=bc1:
                          E.tensor_tensor(q, q, b, ALU.add),
                          waits=[(CV, ch["mq"]), (CP, max(0, CP.n - 1))],
                          incs=[(CP, 1)])
                    ch["mp"] = CP.n
                for ch in CH:
                    bc0 = coB(ch["h"], 0)
                    ch["m"] = dve(ch, lambda E, p=ch["pst"], b=bc0:
                                  E.tensor_tensor(p, p, b, ALU.add))
                for ch in CH:
                    ch["mq"] = dve(ch, lambda E, q=ch["qs"], sc=ch["sc"]:
                                   E.tensor_tensor(q, q, sc, ALU.mult),
                                   [(CP, ch["mp"])])
                for ch in CH:
                    pidx = ch["pidx"]
                    fcs = st_f[pidx % 3][:].rearrange("p w t -> p (w t)")
                    wl = [(CV, max(0, CV.n - 1))]
                    if pidx >= 3:
                        wl.append((A["aAV"], pidx - 2))
                    if diag:
                        eb.op("vector", lambda E, fcs=fcs, p=ch["pst"],
                              q=ch["qs"]: E.tensor_tensor(
                                  fcs, p, q, ALU.add),
                              waits=wl, incs=[(CV, 1)])
                        dmo = 0 if sp == nsp - 2 else 1024
                        eb.op("vector", lambda E, fcs=fcs, dmo=dmo:
                              E.tensor_tensor(
                                  fcs, fcs, P["dms"][:, dmo:dmo + 1024],
                                  ALU.mult),
                              waits=[(CV, CV.n)],
                              incs=[(A["aPHI"], 1)])
                    else:
                        eb.op("vector", lambda E, fcs=fcs, p=ch["pst"],
                              q=ch["qs"]: E.tensor_tensor(
                                  fcs, p, q, ALU.add),
                              waits=wl, incs=[(A["aPHI"], 1)])
                for ch in CH:
                    pend_av.append(
                        lambda pidx=ch["pidx"], k=k, qt=qt, sp=sp,
                        hh=ch["hh"], h=ch["h"], sb_=ch["sb"], kvb=kvb,
                        fq=foq[0], nsp=nsp:
                        emit_av(pidx, k, qt, sp, hh, h, sb_, kvb, fq,
                                nsp))
                    foq[0] = False
                mk["kv"][g["kvi"]] = (g["simn"],
                                      A["aAV"].n + len(pend_av))

            for hq in range(4):
                foq = [True]
                pend_kv = kv_load(0)
                pend_chain = None
                for sp in range(nsp):
                    kvi, kvb, kvn = pend_kv
                    diag = sp >= nsp - 2
                    for hpair in (0, 2):
                        g = emit_prep(hq, sp, hpair, kvi, kvb, kvn, diag)
                        if pend_chain is not None:
                            pend_chain()
                        pend_chain = (lambda g=g, foq=foq, k=k, qt=qt,
                                      nsp=nsp: emit_chain(g, foq, k, qt, nsp))
                        if hpair == 0:
                            yield 11.0
                    if sp + 1 < nsp:
                        pend_kv = kv_load(sp + 1)
                    yield 11.0
                if pend_chain is not None:
                    pend_chain()
                while pend_av:
                    pend_av.pop(0)()
                avq = A["aAV"].n
                epi_wl = [(A["aAV"], avq), (A["aEPA"], mk["epa"]),
                          (A["aTP"], mk["tp"])]
                for hi in range(4):
                    pv = pav[:, hi * 256:(hi + 1) * 256].rearrange(
                        "p (ts d) -> p ts d", ts=4)
                    dvb = P["dvls"][:, 4 * qt:4 * qt + 4].unsqueeze(2).broadcast_to(
                        [128, 4, 64])
                    eb.op("vector", lambda E, hi=hi, pv=pv, dvb=dvb: E.tensor_tensor(
                        ao_hq[:, :, hi, :], pv, dvb, ALU.mult),
                        waits=(epi_wl if hi == 0 else []),
                        incs=([(A["aEPI"], 1)] if hi == 3 else []))
                epin = A["aEPI"].n
                mk["epi"] = epin
                eb.op("scalar", lambda E: E.activation(
                    scr1[:], ao_hq[:].rearrange("p a b c -> p (a b c)"), AF.Square),
                    waits=[(A["aEPI"], epin), (A["aEPV"], mk["epv"]),
                           (A["aSPD"], mk["sp_slice"].get(k - 1, 0))],
                    incs=[(A["aEPA"], 1)])
                mk["epa"] = A["aEPA"].n
                vcm = S["sVC"].n

                def epi_dve(E, epan=mk["epa"], vcm=vcm):
                    E.wait_ge(A["aEPA"].h, epan)
                    go = small_chain(E, S["sVC"], vcm)
                    go(lambda: E.reduce_sum(
                        est[:, 0:16].unsqueeze(2),
                        scr1[:].rearrange("p (g d) -> p g d", d=64), AX.X))
                    go(lambda: E.tensor_scalar(est[:, 16:32], est[:, 0:16],
                                               1.0 / 64, 1e-6, ALU.mult, ALU.add))
                    I32 = mybir.dt.int32
                    t1 = est[:, 32:48]
                    t2 = est[:, 48:64]
                    oi = t1.bitcast(I32)
                    go(lambda: E.tensor_scalar(oi, est[:, 16:32].bitcast(I32), 23,
                                               None, ALU.arith_shift_right))
                    go(lambda: E.tensor_scalar(oi, oi, -1, None, ALU.bitwise_xor))
                    go(lambda: E.tensor_scalar(oi, oi, 382, None, ALU.add))
                    go(lambda: E.tensor_scalar(oi, oi, 1, None, ALU.arith_shift_right))
                    go(lambda: E.tensor_scalar(oi, oi, 23, None, ALU.arith_shift_left))
                    for _ in range(5):
                        go(lambda: E.tensor_tensor(t2, t1, t1, ALU.mult))
                        go(lambda: E.tensor_tensor(t2, t2, est[:, 16:32], ALU.mult))
                        go(lambda: E.tensor_scalar(t2, t2, -0.5, 1.5, ALU.mult,
                                                   ALU.add))
                        go(lambda: E.tensor_tensor(t1, t1, t2, ALU.mult))
                    i = go(lambda: E.tensor_tensor(
                        ao_hq[:].rearrange("p ts h d -> p (ts h) d"),
                        ao_hq[:].rearrange("p ts h d -> p (ts h) d"),
                        est[:, 32:48].rearrange("p (ts h) -> p ts h", ts=4)
                        .rearrange("p ts h -> p (ts h)").unsqueeze(2)
                        .broadcast_to([128, 16, 64]), ALU.mult), inc=False)
                    i.then_inc(A["aEPV"].h, 1)
                eb.q["vector"].append(epi_dve)
                S["sVC"].n += 27
                A["aEPV"].n += 1
                mk["epv"] = A["aEPV"].n
                rmap8 = lambda j: (j % 2) * 4 + j // 2
                tpw0 = [(A["aEPV"], mk["epv"]), (A["aWO"], mk["wo_slice"].get(k - 1, 0))]
                for j in range(8):
                    ts, jj = divmod(j, 2)
                    jc = hq * 2 + jj
                    r = rmap8(j)
                    wl = list(tpw0) if j == 0 else []
                    if j >= 2:
                        wl.append((A["aTE"], mk["te"] + j - 1))
                    eb.op("tensor", lambda E, ts=ts, jj=jj, r=r: E.transpose(
                        pav[:, r * 128:(r + 1) * 128],
                        ao_hq[:, ts, jj * 2:jj * 2 + 2, :].rearrange(
                            "p h d -> p (h d)"),
                        P["sidF"][:]),
                        waits=wl, incs=[(A["aTP"], 1)])
                    eb.op("scalar", lambda E, ts=ts, jc=jc, r=r: E.activation(
                        ao_fm[:, jc, ts * 128:(ts + 1) * 128],
                        pav[:, r * 128:(r + 1) * 128], AF.Identity),
                        waits=[(A["aTP"], A["aTP"].n)], incs=[(A["aTE"], 1)])
                mk["tp"] = A["aTP"].n
                mk["te"] = A["aTE"].n
                yield 8.0
            evall = A["aEV"].n
            wo0 = A["aWO"].n
            for g in range(8):
                ts, ih = divmod(g, 2)
                bank = g % 2
                wl = []
                if g == 0:
                    wl = [(A["aEV"], evall), (A["aTE"], mk["te"]),
                          (A["aEPV"], mk["epv"])]
                if g >= 2:
                    wl.append((A["aWOE"], wo0 // 1 + g - 1 if False else
                               A["aWOE"].n - 1))
                for jc in range(8):
                    eb.op("tensor", lambda E, jc=jc, ts=ts, ih=ih, bank=bank:
                          E.matmul(psim[bank][:],
                                   ao_fm[:, jc, ts * 128:(ts + 1) * 128],
                                   P["WoTb"][:, jc, ih * 512:(ih + 1) * 512],
                                   start=(jc == 0), stop=(jc == 7)),
                          waits=(wl if jc == 0 else []),
                          incs=([(A["aWO"], 1)] if jc == 7 else []))
                won = A["aWO"].n
                spp = A["aSP0"] if g % 2 == 0 else A["aSP1"]
                wle = [(A["aWO"], won)]
                if g >= 2:
                    wle.append((spp, spp.n))
                eb.op("scalar", lambda E, bank=bank, g=g: E.activation(
                    scr1[:, (g % 2) * 512:(g % 2) * 512 + 512], psim[bank][:],
                    AF.Identity),
                    waits=wle, incs=[(A["aWOE"], 1)])
                woen = A["aWOE"].n
                r0 = t0 + ts * 128
                eb.op("scalar", lambda E, r0=r0, ih=ih, g=g: E.dma_start(
                    sS[r0:r0 + 128, ih * 512:(ih + 1) * 512],
                    scr1[:, (g % 2) * 512:(g % 2) * 512 + 512]),
                    waits=[(A["aWOE"], woen)], incs=[(spp, 16)])
            sp0n, sp1n = A["aSP0"].n, A["aSP1"].n

            def s_echo(E, sp0n=sp0n, sp1n=sp1n):
                E.wait_ge(A["aSP0"].h, sp0n)
                E.wait_ge(A["aSP1"].h, sp1n)
                i = E.copy(est[:, 0:1], est[:, 0:1])
                i.then_inc(A["aSPD"].h, 1)
            eb.q["scalar"].append(s_echo)
            A["aSPD"].n += 1
            mk["wo_slice"][k] = A["aWO"].n
            mk["woe_slice"][k] = A["aWOE"].n
            mk["sim_slice"][k] = A["aSIM"].n
            mk["sp_slice"][k] = A["aSPD"].n
            yield 6.0

        FWE = 48

        def ffn_tt(tt):
            xb = tt % 2
            fxp = A["fXN0"] if xb == 0 else A["fXN1"]
            wl = [(S[n], v) for n, v in P1B]
            if tt >= 2:
                wl.append((A["fPG"], mk["ffn_mm_tt"][tt - 2][0]))
                wl.append((A["fPU"], mk["ffn_mm_tt"][tt - 2][1]))
            eb.op("sync", lambda E, xb=xb, tt=tt: E.dma_start(
                xtile[xb][:],
                xnf.rearrange("c p t -> p c t")[:, :, tt * 512:(tt + 1) * 512]),
                waits=wl, incs=[(fxp, 16)])
            wlp = [(S["sPFM"], 16 * NCH_L)]
            if tt >= 1:
                wlp.append((A["fBI"], A["fBI"].n))
            eb.op("sync", lambda E, tt=tt: E.dma_start(
                pfm_sb[:].rearrange("e (a t) -> e a t", a=4),
                tens["pfm"][:, tt * 4:tt * 4 + 4, :]),
                waits=wlp, incs=[(A["fPF"], 16)])
            yield 2.0
            for e in range(NE):
                wl = [(S["sFW"], S["sFW"].n)]
                if not (tt == 0 and e == 0):
                    wl.append((A["fPY"], A["fPY"].n))
                eb.op("sync", lambda E, e=e: E.dma_start(
                    Dsb[:], dwb[e].rearrange("(f p) d -> p f d", p=128)),
                    waits=wl, incs=[(A["fD"], 16)])
                fdn = A["fD"].n
                gu_marks = {}
                FWALL = S["sFW"].n
                for fcp in range(2):
                    gs = A["fG0"] if fcp % 2 == 0 else A["fG1"]
                    us = A["fU0"] if fcp % 2 == 0 else A["fU1"]
                    wl = [(S["sFW"], FWALL)]
                    wlu = []
                    if not (tt == 0 and e == 0 and fcp < 2):
                        wl.append((A["fPG"], A["fPG"].n))
                        wlu.append((A["fPU"], A["fPU"].n))
                    eb.op("sync", lambda E, e=e, fcp=fcp: E.dma_start(
                        gch[fcp % 2][:],
                        gwb[e].rearrange("(c p) f -> p c f", p=128)[
                            :, :, fcp * 256:(fcp + 1) * 256]),
                        waits=wl, incs=[(gs, 16)])
                    eb.op("sync", lambda E, e=e, fcp=fcp: E.dma_start(
                        uch[fcp % 2][:],
                        uwb[e].rearrange("(c p) f -> p c f", p=128)[
                            :, :, fcp * 256:(fcp + 1) * 256]),
                        waits=wlu, incs=[(us, 16)])
                    gu_marks[fcp] = (gs, gs.n, us, us.n)
                for fc in range(16):
                    fcp = fc // 2
                    if fc % 2 == 0 and fc >= 2 and fcp + 1 < 8:
                        gs = A["fG0"] if (fcp + 1) % 2 == 0 else A["fG1"]
                        us = A["fU0"] if (fcp + 1) % 2 == 0 else A["fU1"]
                        eb.op("sync", lambda E, e=e, fcp=fcp: E.dma_start(
                            gch[(fcp + 1) % 2][:],
                            gwb[e].rearrange("(c p) f -> p c f", p=128)[
                                :, :, (fcp + 1) * 256:(fcp + 2) * 256]),
                            waits=[(A["fPG"], A["fPG"].n)], incs=[(gs, 16)])
                        eb.op("sync", lambda E, e=e, fcp=fcp: E.dma_start(
                            uch[(fcp + 1) % 2][:],
                            uwb[e].rearrange("(c p) f -> p c f", p=128)[
                                :, :, (fcp + 1) * 256:(fcp + 2) * 256]),
                            waits=[(A["fPU"], A["fPU"].n)], incs=[(us, 16)])
                        gu_marks[fcp + 1] = (gs, gs.n, us, us.n)
                    for dc in range(8):
                        wl = []
                        if dc == 0:
                            wl = [(gu_marks[fcp][0], gu_marks[fcp][1]),
                                  (fxp, 16 * (tt // 2 + 1)),
                                  (A["fSIL"], A["fSIL"].n)]
                        eb.op("tensor", lambda E, dc=dc, fc=fc, fcp=fcp, xb=xb:
                              E.matmul(pg[:],
                                       gch[fcp % 2][:, dc, (fc % 2) * 128:(fc % 2) * 128 + 128],
                                       xtile[xb][:, dc, :],
                                       start=(dc == 0), stop=(dc == 7)),
                              waits=wl, incs=([(A["fPG"], 1)] if dc == 7 else []))
                    pgn = A["fPG"].n
                    wl = [(A["fPG"], pgn), (A["fH"], max(0, A["fH"].n - 1))]
                    eb.op("scalar", lambda E, e=e, fc=fc: E.activation(
                        hs_scr[fc % 2][:], pg[:], AF.Silu,
                        bias=P["gbs"][:, e * 16 + fc:e * 16 + fc + 1]),
                        waits=wl, incs=[(A["fSIL"], 1)])
                    siln = A["fSIL"].n
                    for dc in range(8):
                        wl = []
                        if dc == 0:
                            wl = [(gu_marks[fcp][2], gu_marks[fcp][3]),
                                  (A["fH"], A["fH"].n)]
                        eb.op("tensor", lambda E, dc=dc, fc=fc, fcp=fcp, xb=xb:
                              E.matmul(pu[:],
                                       uch[fcp % 2][:, dc, (fc % 2) * 128:(fc % 2) * 128 + 128],
                                       xtile[xb][:, dc, :],
                                       start=(dc == 0), stop=(dc == 7)),
                              waits=wl, incs=([(A["fPU"], 1)] if dc == 7 else []))
                    pun = A["fPU"].n
                    wl = [(A["fPU"], pun), (A["fSIL"], siln)]
                    if tt > 0 or e > 0:
                        wl.append((A["fPY"], A["fPY"].n))
                    eb.op("vector", lambda E, e=e, fc=fc: E.scalar_tensor_tensor(
                        hsb[:, fc, :], pu[:],
                        P["ubs"][:, e * 16 + fc:e * 16 + fc + 1],
                        hs_scr[fc % 2][:], ALU.add, ALU.mult),
                        waits=wl, incs=[(A["fH"], 1)])
                    if fc % 2 == 1:
                        yield 3.5
                fhn = A["fH"].n
                if e == 0:
                    for g8 in range(8):
                        t128, dh = divmod(g8, 2)
                        ct = tt * 4 + t128
                        bank = g8 % 2
                        wl = []
                        if g8 == 0:
                            wl = [(A["fYE"], mk["ye_tt"].get(tt - 1, 0)),
                                  (A["fPF"], A["fPF"].n)]
                        if g8 >= 2:
                            wl.append((A["fBIC"], A["fBIC"].n - 1))
                        eb.op("tensor", lambda E, t128=t128, dh=dh, bank=bank: E.matmul(
                            py[bank][:], pfm_sb[0:8, t128 * 128:(t128 + 1) * 128],
                            P["dbs"][0:8, dh * 512:(dh + 1) * 512],
                            start=True, stop=True),
                            waits=wl, incs=[(A["fBI"], 1)])
                        bin_ = A["fBI"].n
                        wl2 = [(A["fBI"], bin_)]
                        if g8 == 0:
                            wl2.append((A["fGT"], mk["gt_tt"].get(tt - 1, 0)))
                        eb.op("vector", lambda E, t128=t128, dh=dh, bank=bank: E.tensor_copy(
                            y_acc[:, t128, dh * 512:(dh + 1) * 512], py[bank][:]),
                            waits=wl2, incs=[(A["fBIC"], 1)])
                    bic_all = A["fBIC"].n
                for g8 in range(8):
                    t128, dh = divmod(g8, 2)
                    ct = tt * 4 + t128
                    bank = g8 % 2
                    for fc in range(16):
                        wl = []
                        if fc == 0:
                            wl = [(A["fH"], fhn), (A["fD"], fdn),
                                  (A["fYE"], max(0, A["fYE"].n - 1))]
                            if e == 0:
                                wl.append((A["fBIC"], bic_all))
                        eb.op("tensor", lambda E, fc=fc, t128=t128, dh=dh,
                              bank=bank: E.matmul(
                                  py[bank][:],
                                  hsb[:, fc, t128 * 128:(t128 + 1) * 128],
                                  Dsb[:, fc, dh * 512:(dh + 1) * 512],
                                  start=(fc == 0), stop=(fc == 15)),
                              waits=wl, incs=([(A["fPY"], 1)] if fc == 15 else []))
                    pyn = A["fPY"].n
                    eb.op("vector", lambda E, t128=t128, dh=dh, bank=bank, ct=ct,
                          e=e: E.scalar_tensor_tensor(
                              y_acc[:, t128, dh * 512:(dh + 1) * 512],
                              py[bank][:], P["probs_res"][:, ct, e:e + 1],
                              y_acc[:, t128, dh * 512:(dh + 1) * 512],
                              ALU.mult, ALU.add),
                          waits=[(A["fPY"], pyn)], incs=[(A["fYE"], 1)])
                    yield 3.5
            mk["ffn_mm_tt"][tt] = (A["fPG"].n, A["fPU"].n)
            mk["ye_tt"][tt] = A["fYE"].n

        def ffn_tail(tt):
            yeall = mk["ye_tt"][tt]
            fhn = A["fH"].n
            assert tt in mk["sp_slice"], f"attn slice {tt} must precede ffn tail"
            spm = mk["sp_slice"][tt]
            for i in range(8):
                t128, i2 = divmod(i, 2)
                r0 = tt * 512 + t128 * 128
                cs = i2 * 512
                wl = [(A["fYE"], yeall), (A["fCT"], max(0, A["fCT"].n - 1))]
                if i == 0:
                    wl.append((A["fH"], fhn))
                eb.op("scalar", lambda E, t128=t128, cs=cs, i=i: E.activation(
                    hs_scr[i % 2][:], y_acc[:, t128, cs:cs + 512], AF.Silu),
                    waits=wl, incs=[(A["fGT"], 1)])
                gtn = A["fGT"].n
                fxx = A["fXSx0"] if i % 2 == 0 else A["fXSx1"]
                fxss = A["fXSs0"] if i % 2 == 0 else A["fXSs1"]
                wl = [(A["fCT"], max(0, A["fCT"].n - 1))]
                eb.op("sync", lambda E, r0=r0, cs=cs, i=i: E.dma_start(
                    xts[i % 2][:], x[r0:r0 + 128, cs:cs + 512]),
                    waits=wl, incs=[(fxx, 16)])
                fop = A["fOUT0"] if i % 2 == 0 else A["fOUT1"]
                eb.op("sync", lambda E, r0=r0, cs=cs, i=i: E.dma_start(
                    sts[i % 2][:], sS[r0:r0 + 128, cs:cs + 512]),
                    waits=[(A["aSPD"], spm), (A["fCT"], max(0, A["fCT"].n - 1)),
                           (fop, fop.n)],
                    incs=[(fxss, 16)])
                xsn = (fxx, fxx.n, fxss, fxss.n)
                vcc = S["sVC"].n

                def comb(E, i=i, gtn=gtn, xsn=xsn, vcc=vcc):
                    E.wait_ge(A["fGT"].h, gtn)
                    E.wait_ge(xsn[0].h, xsn[1])
                    E.wait_ge(xsn[2].h, xsn[3])
                    go = small_chain(E, S["sVC"], vcc)
                    go(lambda: E.tensor_tensor(sts[i % 2][:], sts[i % 2][:],
                                               hs_scr[i % 2][:], ALU.mult))
                    ii = go(lambda: E.tensor_tensor(sts[i % 2][:], sts[i % 2][:],
                                                    xts[i % 2][:], ALU.add),
                            inc=False)
                    ii.then_inc(A["fCT"].h, 1)
                eb.q["vector"].append(comb)
                S["sVC"].n += 1
                A["fCT"].n += 1
                ctn = A["fCT"].n
                eb.op("scalar", lambda E, r0=r0, cs=cs, i=i: E.dma_start(
                    out[r0:r0 + 128, cs:cs + 512], sts[i % 2][:]),
                    waits=[(A["fCT"], ctn)], incs=[(fop, 16)])
                yield 2.0
            mk["ct_tt"][tt] = A["fCT"].n
            mk["gt_tt"][tt] = A["fGT"].n

        only_attn = (phases == 2)
        for k in (SLICES if SLICES is not None else range(NSLICE)):
            a = attn_slice(k)
            f = None if only_attn else ffn_tt(k)
            a_done = False
            f_done = only_attn
            ta = tf = 0.0
            while not (a_done and f_done):
                if not a_done and (f_done or ta <= tf):
                    try:
                        ta += next(a)
                    except StopIteration:
                        a_done = True
                else:
                    try:
                        tf += next(f)
                    except StopIteration:
                        f_done = True
            if not only_attn:
                for _ in ffn_tail(k):
                    pass

        fins = [(A["fOUT0"], A["fOUT0"].n), (A["fOUT1"], A["fOUT1"].n),
                (A["aSP0"], A["aSP0"].n), (A["aSP1"], A["aSP1"].n)]

        def fin(E, fins=fins):
            for s, v in fins:
                if v:
                    E.wait_ge(s.h, v)
        eb.q["sync"].append(fin)
        eb.emit(nc)


_BF = ml_dtypes.bfloat16


def host_inputs(inputs):
    f32 = np.float32
    W = {}
    W["wqT"] = np.ascontiguousarray(inputs["L0_attn_Wq"].T.astype(f32, copy=False))
    W["wkT"] = np.ascontiguousarray(inputs["L0_attn_Wk"].T.astype(f32, copy=False))
    W["wvT"] = np.ascontiguousarray(inputs["L0_attn_Wv"].T.astype(f32, copy=False))
    W["woT"] = np.ascontiguousarray(inputs["L0_attn_Wo"].T.astype(f32, copy=False))
    W["rwT"] = np.ascontiguousarray(inputs["L0_ffn_router_w"].T.astype(f32, copy=False))
    W["polyT"] = np.ascontiguousarray(
        inputs["L0_attn_poly_weights"].T.astype(f32, copy=False))
    W["hdrow"] = np.ascontiguousarray(
        inputs["L0_attn_head_degrees"].reshape(1, 16).astype(f32, copy=False))
    nw = inputs["L0_attn_norm_w"].astype(f32, copy=False)
    W["nw2"] = np.ascontiguousarray(np.concatenate([nw, nw]).reshape(128, 1))
    W["gw"] = np.ascontiguousarray(inputs["L0_ffn_gate_w"].astype(f32, copy=False))
    W["uw"] = np.ascontiguousarray(inputs["L0_ffn_up_w"].astype(f32, copy=False))
    W["dw"] = np.ascontiguousarray(inputs["L0_ffn_down_w"].astype(f32, copy=False))
    W["gbl"] = np.ascontiguousarray(
        inputs["L0_ffn_gate_b"].reshape(NE, 16, 128).transpose(2, 0, 1)
        .reshape(128, 128).astype(f32, copy=False))
    W["ubl"] = np.ascontiguousarray(
        inputs["L0_ffn_up_b"].reshape(NE, 16, 128).transpose(2, 0, 1)
        .reshape(128, 128).astype(f32, copy=False))
    W["db"] = np.ascontiguousarray(inputs["L0_ffn_down_b"].astype(f32, copy=False))
    W["identF"] = np.eye(128, dtype=f32)
    W["identB"] = np.eye(128, dtype=_BF)
    W["rmat"] = gegen_monomials()
    W["kcol"] = np.arange(9, dtype=f32).reshape(9, 1)
    p = np.arange(128, dtype=np.float64)
    j = np.arange(16, dtype=np.float64)
    W["dvl"] = (1.0 / (j[None, :] * 128 + p[:, None] + 1.0)).astype(f32)
    dm = np.zeros((128, 2, 2, 512), np.float32)
    for v in range(2):
        for kcw in range(2):
            thr = (2 * v + kcw) * 128 + p[:, None]
            dm[:, v, kcw, :] = (np.arange(512)[None, :] >= thr)
    W["dmc"] = np.ascontiguousarray(dm.reshape(128, 2048).astype(_BF))
    return W


def kernel(**inputs):
    nc, _ = build_nc(debug=DEBUG, phases=BUILD_PHASES)
    W = host_inputs(inputs)
    xf = inputs["L0_attn_x"].astype(np.float32, copy=False)
    in_maps = []
    for c in range(NCORE):
        m = dict(W)
        m["x"] = np.ascontiguousarray(xf[c * 4:(c + 1) * 4].reshape(TOK, DD))
        in_maps.append(m)
    res = run_bass_kernel_spmd(nc, in_maps, core_ids=list(range(NCORE)))
    outs = [np.asarray(r["out"]) for r in res.results]
    return np.concatenate(outs, axis=0).reshape(32, 2048, 1024)


# version 09
# speedup vs baseline: 1.0614x; 1.0027x over previous
import numpy as np
import ml_dtypes
import concourse.bass as bass
import concourse.mybir as mybir
from concourse.bass_utils import run_bass_kernel_spmd
from contextlib import ExitStack

F32 = mybir.dt.float32
BF16 = mybir.dt.bfloat16
AF = mybir.ActivationFunctionType
ALU = mybir.AluOpType
AX = mybir.AxisListType

NCORE = 8
TOK = 8192
NCH = 64
DD = 1024
DFF = 2048
NE = 8
NSLICE = 16
LAM = 31.5

BUILD_PHASES = 3
DEBUG = 0
NCH_L = NCH
SLICES = None


def gegen_monomials():
    R = [np.zeros(9, np.float64) for _ in range(9)]
    R[0][0] = 1.0
    R[1][1] = 1.0
    for k in range(2, 9):
        c1 = 2.0 * (k + LAM - 1.0) / (k + 2.0 * LAM - 1.0)
        c2 = (k - 1.0) / (k + 2.0 * LAM - 1.0)
        R[k] = c1 * np.concatenate([[0.0], R[k - 1][:-1]]) - c2 * R[k - 2]
    return np.stack(R).astype(np.float32)


MAGIC = 0x5F3759DF


def emit_rsqrt(E, out_f32, in_f32, t2_f32, newton=2):
    I32 = mybir.dt.int32
    oi = out_f32.bitcast(I32)
    E.tensor_scalar(oi, in_f32.bitcast(I32), 1, None, ALU.arith_shift_right)
    E.tensor_scalar(oi, oi, -1, None, ALU.bitwise_xor)
    E.tensor_scalar(oi, oi, MAGIC + 1, None, ALU.add)
    for _ in range(newton):
        E.tensor_tensor(t2_f32, out_f32, out_f32, ALU.mult)
        E.tensor_tensor(t2_f32, t2_f32, in_f32, ALU.mult)
        E.tensor_scalar(t2_f32, t2_f32, -0.5, 1.5, ALU.mult, ALU.add)
        E.tensor_tensor(out_f32, out_f32, t2_f32, ALU.mult)


class Sem:
    __slots__ = ("h", "n", "name")

    def __init__(self, nc, ctx, name):
        self.h = ctx.enter_context(nc.semaphore(name))
        self.n = 0
        self.name = name


class EB:

    def __init__(self):
        self.q = {"tensor": [], "vector": [], "scalar": [], "gpsimd": [], "sync": []}

    def op(self, eng, build, waits=(), incs=()):
        for s, v in waits:
            assert 0 <= v <= s.n, f"wait {s.name}>={v} pledged {s.n}"
        for s, n in incs:
            s.n += n

        def f(E, build=build, waits=tuple(waits), incs=tuple(incs)):
            for s, v in waits:
                if v > 0:
                    E.wait_ge(s.h, v)
            inst = build(E)
            for s, n in incs:
                inst.then_inc(s.h, n)
        self.q[eng].append(f)

    def emit(self, nc):
        with nc.Block() as blk:
            for name, fns in self.q.items():
                if fns:
                    getattr(blk, name)(lambda E, fns=fns: [f(E) for f in fns])


def small_chain(E, sem, n_start):
    state = {"n": n_start}

    def go(make, inc=True):
        E.wait_ge(sem.h, state["n"])
        inst = make()
        if inc:
            inst.then_inc(sem.h, 1)
            state["n"] += 1
        return inst
    return go


def build_nc(debug=0, phases=3):
    nc = bass.Bass("TRN2", target_bir_lowering=False)

    def din(name, shape, dt=F32):
        return nc.dram_tensor(name, shape, dt, kind="ExternalInput").ap()

    def dout(name, shape, dt=F32):
        return nc.dram_tensor(name, shape, dt, kind="ExternalOutput").ap()

    dbg_names = ("xnf", "qf", "kf", "vt", "sS") if debug else ()

    def dint(name, shape, dt):
        kind = "ExternalOutput" if name in dbg_names else "Internal"
        return nc.dram_tensor(name, shape, dt, kind=kind).ap()

    x = din("x", [TOK, DD])
    wqT = din("wqT", [DD, DD]); wkT = din("wkT", [DD, DD])
    wvT = din("wvT", [DD, DD]); woT = din("woT", [DD, DD])
    rwT = din("rwT", [DD, NE])
    polyT = din("polyT", [9, 16]); hdrow = din("hdrow", [1, 16])
    nw2 = din("nw2", [128, 1])
    gwi = din("gw", [NE, DD, DFF]); uwi = din("uw", [NE, DD, DFF])
    dwi = din("dw", [NE, DFF, DD])
    gbl = din("gbl", [128, 128]); ubl = din("ubl", [128, 128])
    dbi = din("db", [NE, DD])
    identF = din("identF", [128, 128]); identB = din("identB", [128, 128], BF16)
    rmat = din("rmat", [9, 9]); kcol = din("kcol", [9, 1])
    dvl = din("dvl", [128, 16]); dmc = din("dmc", [128, 2048], BF16)

    out = dout("out", [TOK, DD])
    dbgt = {}
    if debug:
        dbgt["d_probs"] = dout("d_probs", [128, NCH, NE])
        dbgt["d_lns"] = dout("d_lns", [NCH, 128, 12])

    xnf = dint("xnf", [8, 128, TOK], BF16)
    qf = dint("qf", [8, 128, TOK], BF16)
    kf = dint("kf", [8, 128, TOK], BF16)
    vt = dint("vt", [TOK, DD], BF16)
    sS = dint("sS", [TOK, DD], F32)
    gwb = dint("gwb", [NE, DD, DFF], BF16)
    uwb = dint("uwb", [NE, DD, DFF], BF16)
    dwb = dint("dwb", [NE, DFF, DD], BF16)
    aTd = dint("aTd", [9, 16], F32)
    pfm = dint("pfm", [NE, NCH, 128], BF16)
    tens = dict(x=x, out=out, xnf=xnf, qf=qf, kf=kf, vt=vt, sS=sS,
                gwb=gwb, uwb=uwb, dwb=dwb, pfm=pfm)

    with ExitStack() as pctx:
        def psb(name, shape, dt=F32):
            return pctx.enter_context(nc.sbuf_tensor(name, shape, dt))

        P = {}
        for name, shape, dt in [
                ("ones_r", [1, 144], F32), ("sidF", [128, 128], F32),
                ("sidB", [128, 128], BF16), ("A_bc", [128, 144], F32),
                ("A_bcb", [128, 144], BF16),
                ("dvls", [128, 16], F32), ("dms", [128, 2048], BF16),
                ("WoTb", [128, 8, DD], BF16), ("probs_res", [128, NCH, NE], F32),
                ("pfm_st", [8, 128], BF16), ("gbs", [128, 128], F32),
                ("ubs", [128, 128], F32), ("dbs", [8, DD], BF16),
                ("nws", [128, 1], F32)]:
            P[name] = psb(name, shape, dt)

        S = {}
        for name in ["sC0", "sFW", "sAW", "gPE", "gV", "sEPC", "sVAR", "sVC",
                     "sX0", "sX1", "sST", "sSQ", "sNRM", "sXP", "sXE", "sXC",
                     "sRT", "sPR", "sPTR", "sPFM", "sWST", "sGD",
                     "sQM", "sQA", "sQV", "sQEq", "sQEv", "sQTP", "sQTE",
                     "sSPV0", "sSPV1", "sSPx", "sSPq", "sSPk", "sPFC", "fPF",
                     "sEC0", "sEC1", "sEC2",
                     "aQ", "aK0", "aK1", "aV0", "aV1", "aSIM", "aEV", "aPQ",
                     "aPHI", "aAV", "aEPI", "aEPV", "aEPA", "aTP", "aTE",
                     "aWO", "aWOE", "aSP0", "aSP1", "aSPD",
                     "fXN0", "fXN1", "fG0", "fG1", "fU0", "fU1", "fD", "fPG", "fPU", "fSIL",
                     "fH", "fPY", "fYE", "fBI", "fBIC", "fXSx0", "fXSx1", "fXSs0", "fXSs1", "fGT", "cACT", "cPOOL", "cDVE",
                     "fCT", "fOUT0", "fOUT1", "dC"]:
            S[name] = Sem(nc, pctx, name)

        consts = dict(identF=identF, identB=identB, rmat=rmat, kcol=kcol,
                      dvl=dvl, dmc=dmc, gbl=gbl, ubl=ubl, db=dbi, nw2=nw2,
                      polyT=polyT, hdrow=hdrow, rwT=rwT,
                      wqT=wqT, wkT=wkT, wvT=wvT, woT=woT,
                      gwi=gwi, uwi=uwi, dwi=dwi, aTd=aTd)
        meta = build_phase01(nc, S, P, tens, consts, dbgt, debug, phases)
        if phases >= 2:
            build_phase23(nc, S, P, tens, meta, dbgt, debug, phases)

    return nc, dbgt


def build_phase01(nc, S, P, tens, C, dbgt, debug, phases):
    x, xnf, qf, kf, vt = tens["x"], tens["xnf"], tens["qf"], tens["kf"], tens["vt"]
    with ExitStack() as c1:
        def sb1(name, shape, dt=F32):
            return c1.enter_context(nc.sbuf_tensor(name, shape, dt))

        def ps1(name, shape, dt=F32):
            return c1.enter_context(nc.psum_tensor(name, shape, dt))

        WqTb = sb1("WqTb", [128, 8, DD], BF16)
        WkTb = sb1("WkTb", [128, 8, DD], BF16)
        WvTb = sb1("WvTb", [128, 8, DD], BF16)
        wstage = sb1("wstage", [128, DD])
        rwTs = sb1("rwTs", [128, 8, NE])
        ptys = sb1("ptys", [9, 16]); hds = sb1("hds", [1, 16])
        rms = sb1("rms", [9, 9]); kcs = sb1("kcs", [9, 1])
        gats = sb1("gats", [9, 16]); wgs = sb1("wgs", [9, 16])
        aTs = sb1("aTs", [9, 16]); arow = sb1("arow", [1, 144])
        sx = [sb1(f"sx{i}", [128, DD]) for i in range(2)]
        lns = sb1("lns", [128, 12])
        sqh = [sb1(f"sqh{i}", [128, 512]) for i in range(2)]
        sxn = [sb1(f"sxn{i}", [128, DD]) for i in range(2)]
        xnfm = [sb1(f"xnfm{i}", [128, 8, 128]) for i in range(2)]
        xnstage = sb1("xnstage", [128, 8, 1024], BF16)
        qstage = sb1("qstage", [128, 8, 1024], BF16)
        kstage = sb1("kstage", [128, 8, 1024], BF16)
        qtb = sb1("qtb", [128, DD], BF16)
        ktb = sb1("ktb", [128, DD], BF16)
        vtb = [sb1(f"vtb{i}", [128, DD], BF16) for i in range(2)]
        qks = sb1("qks", [128, 32])
        prw = sb1("prw", [128, 24])

        ptr = ps1("ptr", [128, 8, 128])
        pqk = [ps1(f"pqk{i}", [128, 512]) for i in range(2)]
        psm = ps1("psm", [128, 160])
        pqb = ps1("pqb", [128, 16, 128], BF16)

        eb = EB()

        for dst, src in [(P["sidF"][:], C["identF"]), (P["sidB"][:], C["identB"]),
                         (P["dvls"][:], C["dvl"]), (P["dms"][:], C["dmc"]),
                         (P["gbs"][:], C["gbl"]), (P["ubs"][:], C["ubl"]),
                         (P["nws"][:], C["nw2"]),
                         (ptys[:], C["polyT"]), (hds[:], C["hdrow"]),
                         (rms[:], C["rmat"]), (kcs[:], C["kcol"]),
                         (rwTs[:],
                          C["rwT"].rearrange("(c p) e -> p c e", p=128))]:
            eb.op("sync", lambda E, d=dst, s=src: E.dma_start(d, s),
                  waits=[(S["sC0"], S["sC0"].n)], incs=[(S["sC0"], 16)])
        C0 = S["sC0"].n

        for e in range(NE):
            for dsrc, ddst in ((C["gwi"], tens["gwb"]), (C["uwi"], tens["uwb"]),
                               (C["dwi"], tens["dwb"])):
                eb.op("gpsimd", lambda E, a=ddst[e], b=dsrc[e]: E.dma_start(a, b),
                      waits=[(S["sFW"], S["sFW"].n)], incs=[(S["sFW"], 16)])

        eb.op("gpsimd", lambda E: E.dma_start(P["dbs"][:], C["db"]),
              waits=[(S["sAW"], S["sAW"].n)], incs=[(S["sAW"], 16)])
        for dst, src in ((WqTb, C["wqT"]), (WkTb, C["wkT"]), (WvTb, C["wvT"])):
            eb.op("gpsimd", lambda E, d=dst, s=src: E.dma_start(
                d[:], s.rearrange("(c p) f -> p c f", p=128)),
                waits=[(S["sAW"], S["sAW"].n)], incs=[(S["sAW"], 16)])
        CASTW = S["sAW"].n

        for jc in range(8):
            wl = [] if jc == 0 else [(S["gV"], jc)]
            eb.op("sync", lambda E, jc=jc: E.dma_start(
                wstage[:], C["woT"][jc * 128:(jc + 1) * 128, :]),
                waits=wl, incs=[(S["sWST"], 16)])
            eb.op("vector", lambda E, jc=jc: E.tensor_scalar_mul(
                P["WoTb"][:, jc, :], wstage[:], P["nws"][:, 0:1]),
                waits=[(S["sWST"], (jc + 1) * 16), (S["sC0"], C0)],
                incs=[(S["gV"], 1)])
        C1 = S["sC0"].n

        eb.op("vector", lambda E: E.memset(P["ones_r"][:], 1.0), incs=[(S["gV"], 1)])
        gv0 = S["gV"].n
        eb.op("tensor", lambda E: E.matmul(psm[0:9, 0:16], P["ones_r"][0:1, 0:9],
                                           hds[:], start=True, stop=True),
              waits=[(S["sC0"], C0), (S["gV"], gv0)], incs=[(S["gPE"], 1)])
        vc0 = S["sVC"].n

        def gchain(E, vc0=vc0):
            E.wait_ge(S["gPE"].h, 1)
            go = small_chain(E, S["sVC"], vc0)
            go(lambda: E.tensor_scalar(gats[:], psm[0:9, 0:16], kcs[:, 0:1], 1.0,
                                       ALU.subtract, ALU.add))
            go(lambda: E.tensor_scalar(gats[:], gats[:], 0.0, 1.0, ALU.max, ALU.min))
            i = go(lambda: E.tensor_tensor(wgs[:], ptys[:], gats[:], ALU.mult),
                   inc=False)
            i.then_inc(S["gV"].h, 3)
        eb.q["vector"].append(gchain)
        S["sVC"].n += 2
        S["gV"].n += 3
        eb.op("tensor", lambda E: E.matmul(psm[0:9, 16:32], rms[:], wgs[:],
                                           start=True, stop=True),
              waits=[(S["gV"], gv0 + 3)], incs=[(S["gPE"], 1)])
        eb.op("vector", lambda E: E.tensor_copy(aTs[:], psm[0:9, 16:32]),
              waits=[(S["gPE"], 2)], incs=[(S["gV"], 1)])
        eb.op("sync", lambda E: E.dma_start(C["aTd"], aTs[:]),
              waits=[(S["gV"], gv0 + 4)], incs=[(S["sGD"], 16)])
        eb.op("sync", lambda E: E.dma_start(
            arow[:].rearrange("a (h j) -> a h j", j=9),
            C["aTd"].transpose([1, 0]).unsqueeze(0),
            allow_slow_non_contiguous=True),
            waits=[(S["sGD"], 16)], incs=[(S["sGD"], 16)])
        eb.op("tensor", lambda E: E.matmul(psm[:, 16:160], P["ones_r"][0:1, 0:128],
                                           arow[:], start=True, stop=True),
              waits=[(S["sGD"], 32), (S["gV"], gv0)], incs=[(S["gPE"], 1)])
        eb.op("vector", lambda E: E.tensor_copy(P["A_bc"][:], psm[:, 16:160]),
              waits=[(S["gPE"], 3)], incs=[(S["gV"], 1)])
        eb.op("vector", lambda E: E.tensor_copy(P["A_bcb"][:], psm[:, 16:160]),
              incs=[(S["gV"], 1)])

        deltas0 = None
        spill_after = {}
        for ct in range(NCH_L):
            b = ct % 2
            tile, sl = ct // 8, ct % 8
            base = {k: S[k].n for k in S}
            epw = [(S["sEC0"], max(0, ct - 1)), (S["sEC1"], max(0, ct - 1)),
                   (S["sEC2"], max(0, ct - 1))]

            sxp = S["sX0"] if b == 0 else S["sX1"]
            eb.op("sync", lambda E, ct=ct, b=b: E.dma_start(
                sx[b][:], x[ct * 128:(ct + 1) * 128, :]),
                waits=[(S["sST"], max(0, ct - 1))], incs=[(sxp, 16)])

            eb.op("scalar", lambda E, b=b: E.activation(sqh[0][:], sx[b][:, 0:512],
                                                        AF.Square),
                  waits=epw + [(sxp, 16 * (ct // 2 + 1)),
                               (S["sQV"], max(0, 5 * ct))],
                  incs=[(S["sSQ"], 1)])
            eb.op("scalar", lambda E, b=b: E.activation(sqh[1][:], sx[b][:, 512:1024],
                                                        AF.Square),
                  incs=[(S["sSQ"], 1)])

            vcs = S["sVC"].n

            def ln_stats(E, ct=ct, b=b, epv=epw[0][1], vcs=vcs, sxp=sxp):
                E.wait_ge(S["sEC0"].h, epv)
                E.wait_ge(S["sEC1"].h, epv)
                E.wait_ge(S["sEC2"].h, epv)
                E.wait_ge(sxp.h, 16 * (ct // 2 + 1))
                E.wait_ge(S["sSQ"].h, 2 * (ct + 1))
                go = small_chain(E, S["sVC"], vcs)
                E.reduce_sum(lns[:, 0:1], sx[b][:], AX.X)
                E.reduce_sum(lns[:, 1:2], sqh[0][:], AX.X)
                go(lambda: E.reduce_sum(lns[:, 2:3], sqh[1][:], AX.X))
                go(lambda: E.tensor_scalar_mul(lns[:, 3:4], lns[:, 0:1], 1.0 / DD))
                go(lambda: E.tensor_add(lns[:, 4:5], lns[:, 1:2], lns[:, 2:3]))
                go(lambda: E.tensor_scalar_mul(lns[:, 5:6], lns[:, 4:5], 1.0 / DD))
                go(lambda: E.tensor_mul(lns[:, 6:7], lns[:, 3:4], lns[:, 3:4]))
                go(lambda: E.scalar_tensor_tensor(lns[:, 7:8], lns[:, 6:7], -1.0,
                                                  lns[:, 5:6], ALU.mult, ALU.add))
                i = go(lambda: E.tensor_scalar_add(lns[:, 8:9], lns[:, 7:8], 1e-5),
                       inc=False)
                i.then_inc(S["sVAR"].h, 1)
            eb.q["vector"].append(ln_stats)
            S["sVC"].n += 6
            S["sVAR"].n += 1
            eb.op("scalar", lambda E: E.activation(lns[:, 9:10], lns[:, 8:9], AF.Ln),
                  waits=[(S["sVAR"], ct + 1)], incs=[(S["cACT"], 1)])
            eb.op("scalar", lambda E: E.activation(lns[:, 10:11], lns[:, 9:10],
                                                   AF.Exp, scale=-0.5),
                  waits=[(S["cACT"], S["cACT"].n)], incs=[(S["sQA"], 1)])
            qa_rstd = S["sQA"].n

            vcn = S["sVC"].n

            def ln_nbias(E, b=b, qa_rstd=qa_rstd):
                E.wait_ge(S["sQA"].h, qa_rstd)
                i = E.scalar_tensor_tensor(lns[:, 11:12], lns[:, 3:4], -1.0,
                                           lns[:, 10:11], ALU.mult, ALU.mult)
                i.then_inc(S["sQV"].h, 1)
            eb.q["vector"].append(ln_nbias)
            S["sQV"].n += 1
            nb_now = S["sQV"].n
            eb.op("scalar", lambda E, b=b: E.activation(
                sxn[b][:], sx[b][:], AF.Identity, bias=lns[:, 11:12],
                scale=lns[:, 10:11]),
                waits=[(S["sQV"], nb_now), (S["cACT"], S["cACT"].n)],
                incs=[(S["sST"], 1)])
            S["sNRM"].n += 1
            if debug:
                eb.op("scalar", lambda E, ct=ct: E.dma_start(
                    dbgt["d_lns"][ct], lns[:]),
                    waits=[(S["sST"], ct + 1)], incs=[(S["dC"], 16)])

            rmap = lambda j: (j % 2) * 4 + j // 2
            for dc in range(8):
                wl = []
                if dc == 0:
                    wl = epw + [(S["sST"], ct + 1), (S["sXE"], max(0, ct * 8))]
                elif dc >= 2:
                    wl = [(S["sXE"], ct * 8 + dc - 1)]
                eb.op("tensor", lambda E, b=b, dc=dc, r=rmap(dc): E.transpose(
                    ptr[:, r, :], sxn[b][:, dc * 128:(dc + 1) * 128], P["sidF"][:]),
                    waits=wl, incs=[(S["sXP"], 1)])
                eb.op("scalar", lambda E, b=b, dc=dc, r=rmap(dc): E.activation(
                    xnfm[b][:, dc, :], ptr[:, r, :], AF.Identity),
                    waits=[(S["sXP"], ct * 8 + dc + 1)] +
                          ([(S["sXC"], max(0, ct - 1))] if dc == 0 else []),
                    incs=[(S["sXE"], 1)])
            eb.op("vector", lambda E, b=b, sl=sl: E.tensor_copy(
                xnstage[:, :, sl * 128:(sl + 1) * 128], xnfm[b][:]),
                waits=[(S["sXE"], (ct + 1) * 8),
                       (S["sSPx"], spill_after.get(tile - 1, 0) if sl == 0 else 0),
                       (S["sQM"], max(0, (ct - 7) * 6))],
                incs=[(S["sXC"], 1)])

            for dc in range(8):
                eb.op("tensor", lambda E, b=b, dc=dc: E.matmul(
                    psm[:, 0:8], xnfm[b][:, dc, :], rwTs[:, dc, :],
                    start=(dc == 0), stop=(dc == 7)),
                    waits=([(S["sXE"], ct * 8 + 8), (S["sPFC"], max(0, ct))]
                           if dc == 0 else []),
                    incs=([(S["sRT"], 1)] if dc == 7 else []))

            vpa = S["sVC"].n

            def probs_a(E, ct=ct, vpa=vpa):
                E.wait_ge(S["sRT"].h, ct + 1)
                go = small_chain(E, S["sVC"], vpa)
                E.tensor_copy(prw[:, 16:24], psm[:, 0:8])
                go(lambda: E.reduce_max(prw[:, 8:9], psm[:, 0:8], AX.X))
                i = go(lambda: E.tensor_scalar_mul(prw[:, 9:10], prw[:, 8:9], -1.0),
                       inc=False)
                i.then_inc(S["sPR"].h, 1)
            eb.q["vector"].append(probs_a)
            S["sVC"].n += 1
            S["sPR"].n += 1
            eb.op("scalar", lambda E: E.activation(prw[:, 0:8], prw[:, 16:24],
                                                   AF.Exp, bias=prw[:, 9:10]),
                  waits=[(S["sPR"], base["sPR"] + 1)], incs=[(S["sQA"], 1)])
            qa_pe = S["sQA"].n

            vpb = S["sVC"].n

            def probs_b(E, ct=ct, qa=qa_pe, vpb=vpb):
                E.wait_ge(S["sQA"].h, qa)
                go = small_chain(E, S["sVC"], vpb)
                go(lambda: E.reduce_sum(prw[:, 8:9], prw[:, 0:8], AX.X))
                go(lambda: E.reciprocal(prw[:, 9:10], prw[:, 8:9]))
                go(lambda: E.tensor_scalar_mul(prw[:, 0:8], prw[:, 0:8],
                                               prw[:, 9:10]))
                go(lambda: E.reduce_max(prw[:, 8:9], prw[:, 0:8], AX.X))
                go(lambda: E.tensor_scalar(prw[:, 10:18], prw[:, 0:8], prw[:, 8:9],
                                           None, ALU.is_ge))
                go(lambda: E.scalar_tensor_tensor(prw[:, 10:18], prw[:, 10:18],
                                                  -1e30, prw[:, 0:8],
                                                  ALU.mult, ALU.add))
                go(lambda: E.reduce_max(prw[:, 8:9], prw[:, 10:18], AX.X))
                go(lambda: E.tensor_scalar(prw[:, 10:18], prw[:, 0:8], prw[:, 8:9],
                                           None, ALU.is_ge))
                go(lambda: E.tensor_tensor(prw[:, 0:8], prw[:, 0:8], prw[:, 10:18],
                                           ALU.mult))
                go(lambda: E.reduce_sum(prw[:, 8:9], prw[:, 0:8], AX.X))
                go(lambda: E.tensor_scalar_add(prw[:, 8:9], prw[:, 8:9], 1e-9))
                go(lambda: E.reciprocal(prw[:, 9:10], prw[:, 8:9]))
                i = go(lambda: E.tensor_scalar_mul(P["probs_res"][:, ct, :],
                                                   prw[:, 0:8], prw[:, 9:10]),
                       inc=False)
                i.then_inc(S["sPR"].h, 1)
            eb.q["vector"].append(probs_b)
            S["sVC"].n += 12
            S["sPR"].n += 1
            eb.op("tensor", lambda E, ct=ct: E.transpose(
                psm[0:8, 16:144], P["probs_res"][:, ct, :], P["sidF"][:]),
                waits=[(S["sPR"], base["sPR"] + 2), (S["sPFC"], max(0, ct))],
                incs=[(S["sPTR"], 1)])
            eb.op("vector", lambda E, ct=ct: E.tensor_copy(
                P["pfm_st"][:], psm[0:8, 16:144]),
                waits=[(S["sPTR"], ct + 1), (S["sPFM"], max(0, 16 * ct))],
                incs=[(S["sPFC"], 1)])
            eb.op("scalar", lambda E, ct=ct: E.dma_start(
                tens["pfm"][:, ct, :], P["pfm_st"][:]),
                waits=[(S["sPFC"], ct + 1)], incs=[(S["sPFM"], 16)])

            qv_marks = []
            for mi, (Wb, kind) in enumerate(((WqTb, "q"), (WkTb, "k"), (WvTb, "v"))):
                for hf in range(2):
                    idx = 6 * ct + 2 * mi + hf
                    lidx = 2 * mi + hf
                    bank = idx % 2
                    for dc in range(8):
                        wl = []
                        if dc == 0:
                            wl = [(S["sXC"], ct + 1), (S["sAW"], CASTW)]
                            li2 = lidx - 2
                            if li2 >= -2 and idx >= 2:
                                if li2 >= 0:
                                    if li2 < 4:
                                        wl.append((S["sQEq"], 4 * ct + li2 + 1))
                                    else:
                                        wl.append((S["sQEv"], 2 * ct + li2 - 3))
                                else:
                                    wl.append((S["sQEv"], 2 * (ct - 1) + li2 + 3))
                        eb.op("tensor", lambda E, sl=sl, dc=dc, Wb=Wb, hf=hf,
                              bank=bank: E.matmul(
                                  pqk[bank][:],
                                  xnstage[:, dc, sl * 128:(sl + 1) * 128],
                                  Wb[:, dc, hf * 512:(hf + 1) * 512],
                                  start=(dc == 0), stop=(dc == 7)),
                              waits=wl, incs=([(S["sQM"], 1)] if dc == 7 else []))
                    mmn = S["sQM"].n
                    if kind == "v":
                        spv = S["sSPV0"] if b == 0 else S["sSPV1"]
                        eb.op("scalar", lambda E, bank=bank, hf=hf, b=b:
                              E.activation(
                                  vtb[b][:, hf * 512:(hf + 1) * 512],
                                  pqk[bank][:], AF.Identity),
                              waits=[(S["sQM"], mmn), (spv, 16 * (ct // 2))],
                              incs=[(S["sQEv"], 1)])
                    else:
                        dsttb = qtb if kind == "q" else ktb
                        sqb = sqh[lidx % 2]
                        sqw = ([(S["sST"], ct + 1)] if lidx < 2
                               else [(S["sQV"], qv_marks[lidx - 2])])
                        eb.op("scalar", lambda E, bank=bank, sqb=sqb: E.activation(
                            sqb[:], pqk[bank][:], AF.Square),
                            waits=[(S["sQM"], mmn)] + sqw, incs=[(S["sQA"], 1)])
                        qa1 = S["sQA"].n

                        vqs = S["sVC"].n

                        def qk_ss(E, sqb=sqb, qa1=qa1, vqs=vqs):
                            E.wait_ge(S["sQA"].h, qa1)
                            go = small_chain(E, S["sVC"], vqs)
                            go(lambda: E.reduce_sum(
                                qks[:, 0:8].unsqueeze(2),
                                sqb[:].rearrange("p (h d) -> p h d", h=8), AX.X))
                            i = go(lambda: E.tensor_scalar(qks[:, 8:16], qks[:, 0:8],
                                                           1e-24, None, ALU.max),
                                   inc=False)
                            i.then_inc(S["sQV"].h, 1)
                        eb.q["vector"].append(qk_ss)
                        S["sVC"].n += 1
                        S["sQV"].n += 1
                        qv_marks.append(S["sQV"].n)
                        eb.op("scalar", lambda E: E.activation(
                            qks[:, 16:24], qks[:, 8:16], AF.Ln),
                            waits=[(S["sQV"], S["sQV"].n)], incs=[(S["cACT"], 1)])
                        eb.op("scalar", lambda E: E.activation(
                            qks[:, 16:24], qks[:, 16:24], AF.Exp, scale=-0.5),
                            waits=[(S["cACT"], S["cACT"].n)], incs=[(S["sQA"], 1)])
                        qa2 = S["sQA"].n
                        eb.op("vector", lambda E, bank=bank, hf=hf, dsttb=dsttb:
                              E.tensor_tensor(
                                  dsttb[:, hf * 512:(hf + 1) * 512].rearrange(
                                      "p (h d) -> p h d", h=8),
                                  pqk[bank][:].rearrange("p (h d) -> p h d", h=8),
                                  qks[:, 16:24].unsqueeze(2).broadcast_to(
                                      [128, 8, 64]),
                                  ALU.mult),
                              waits=[(S["sQA"], qa2)], incs=[(S["sQEq"], 1)])
                if kind == "v":
                    pass

            qe_now = S["sQEq"].n
            rmap16 = lambda j: (j % 2) * 8 + j // 2
            for ti, (srctb, stage) in enumerate(((qtb, qstage), (ktb, kstage))):
                for dc in range(8):
                    j = ti * 8 + dc
                    wl = [(S["sQEq"], qe_now)] if j == 0 else []
                    if j >= 2:
                        wl.append((S["sQTE"], ct * 16 + j - 1))
                    eb.op("tensor", lambda E, srctb=srctb, dc=dc, r=rmap16(ti * 8 + dc):
                          E.transpose(pqb[:, r, :],
                                      srctb[:, dc * 128:(dc + 1) * 128], P["sidB"][:]),
                          waits=wl, incs=[(S["sQTP"], 1)])
                    wl2 = [(S["sQTP"], ct * 16 + j + 1)]
                    if dc == 0 and sl == 0:
                        wl2.append(((S["sSPq"] if ti == 0 else S["sSPk"]),
                                    spill_after.get(tile - 1, 0)))
                    eb.op("scalar", lambda E, stage=stage, dc=dc, sl=sl,
                          r=rmap16(ti * 8 + dc):
                          E.activation(stage[:, dc, sl * 128:(sl + 1) * 128],
                                       pqb[:, r, :], AF.Identity),
                          waits=wl2, incs=[(S["sQTE"], 1)])
            spv = S["sSPV0"] if b == 0 else S["sSPV1"]
            eb.op("scalar", lambda E, ct=ct, b=b: E.dma_start(
                vt[ct * 128:(ct + 1) * 128, :], vtb[b][:]),
                waits=[(S["sQEv"], 2 * (ct + 1))], incs=[(spv, 16)])

            if sl == 7:
                te_now = S["sQTE"].n
                xc_now = S["sXC"].n
                for stage, dst, ssem in ((xnstage, xnf, S["sSPx"]),
                                         (qstage, qf, S["sSPq"]),
                                         (kstage, kf, S["sSPk"])):
                    wl = ([(S["sXC"], xc_now)] if stage is xnstage
                          else [(S["sQTE"], te_now)])
                    eb.op("scalar", lambda E, stage=stage, dst=dst, tile=tile:
                          E.dma_start(
                              dst.rearrange("c p t -> p c t")[
                                  :, :, tile * 1024:(tile + 1) * 1024],
                              stage[:]),
                          waits=wl, incs=[(ssem, 16)])
                spill_after[tile] = 16 * (tile + 1)

            eb.op("vector", lambda E: E.tensor_copy(qks[:, 24:25], qks[:, 0:1]),
                  incs=[(S["sEC0"], 1)])
            eb.op("scalar", lambda E: E.copy(qks[:, 25:26], qks[:, 0:1]),
                  incs=[(S["sEC1"], 1)])
            eb.op("tensor", lambda E: E.transpose(
                pqb[:, 15, :], ktb[:, 896:1024], P["sidB"][:]),
                waits=[(S["sQTE"], ct * 16 + 16)], incs=[(S["sEC2"], 1)])

            d = {}
            for k2 in S:
                if S[k2].n != base[k2]:
                    key = k2.rstrip("01") if k2[-1] in "01" else k2
                    d[key] = d.get(key, 0) + S[k2].n - base[k2]
            if deltas0 is None and sl not in (0, 7):
                deltas0 = d
            elif sl not in (0, 7) and deltas0 is not None:
                assert d == deltas0, f"chunk {ct} deltas {d} != {deltas0}"

        meta = dict(SPx=S["sSPx"].n, SPq=S["sSPq"].n, SPk=S["sSPk"].n,
                    SPV0=S["sSPV0"].n, SPV1=S["sSPV1"].n, CASTW=CASTW)

        if debug:
            eb.op("scalar", lambda E: E.dma_start(dbgt["d_probs"], P["probs_res"][:]),
                  waits=[(S["sPR"], S["sPR"].n)], incs=[(S["dC"], 16)])
        if phases == 1:
            dcw = S["dC"].n
            fins = [(S["sSPx"], S["sSPx"].n), (S["sSPq"], S["sSPq"].n),
                    (S["sSPk"], S["sSPk"].n), (S["sSPV0"], S["sSPV0"].n),
                    (S["sSPV1"], S["sSPV1"].n), (S["sFW"], S["sFW"].n),
                    (S["dC"], dcw), (S["sPFM"], S["sPFM"].n)]

            def fin2(E, fins=fins):
                for s, v in fins:
                    if v:
                        E.wait_ge(s.h, v)
            eb.q["sync"].append(fin2)
        eb.emit(nc)
    return meta


def build_phase23(nc, S, P, tens, meta, dbgt, debug, phases):
    x, out, sS = tens["x"], tens["out"], tens["sS"]
    qf, kf, vt = tens["qf"], tens["kf"], tens["vt"]
    xnf, gwb, uwb, dwb = tens["xnf"], tens["gwb"], tens["uwb"], tens["dwb"]
    P1B = [("sSPx", meta["SPx"]), ("sSPq", meta["SPq"]), ("sSPk", meta["SPk"]),
           ("sSPV0", meta["SPV0"]), ("sSPV1", meta["SPV1"])]
    with ExitStack() as c2:
        def sb2(name, shape, dt=F32):
            return c2.enter_context(nc.sbuf_tensor(name, shape, dt))

        def ps2(name, shape, dt=F32):
            return c2.enter_context(nc.psum_tensor(name, shape, dt))

        qtl = [sb2("qtl0", [128, 8, 512], BF16)] * 2
        ksp = [sb2(f"ksp{i}", [128, 8, 256], BF16) for i in range(2)]
        vsp = [sb2(f"vsp{i}", [128, 2, 1024], BF16) for i in range(2)]
        st_s = [sb2(f"st_s{i}", [128, 2, 512], BF16) for i in range(4)]
        st_z = [sb2(f"st_z{i}", [128, 2, 512], BF16) for i in range(4)]
        stp2 = [sb2(f"st_p{i}", [128, 1024], BF16) for i in range(2)]
        st_q = [sb2(f"st_q{i}", [128, 1024], BF16) for i in range(2)]
        st_f = [sb2(f"st_f{i}", [128, 2, 512], BF16) for i in range(3)]
        ao_hq = sb2("ao_hq", [128, 4, 4, 64])
        ao_fm = sb2("ao_fm", [128, 8, 512], BF16)
        scr1 = sb2("scr1", [128, 1024])
        est = sb2("est", [128, 64])
        xtile = [sb2(f"xtile{i}", [128, 8, 512], BF16) for i in range(2)]
        gch = [sb2(f"gch{i}", [128, 8, 256], BF16) for i in range(2)]
        uch = [sb2(f"uch{i}", [128, 8, 256], BF16) for i in range(2)]
        Dsb = sb2("Dsb", [128, 16, 1024], BF16)
        hsb = sb2("hsb", [128, 16, 512], BF16)
        y_acc = sb2("y_acc", [128, 4, 1024])
        hs_scr = [sb2(f"hs_scr{i}", [128, 512]) for i in range(2)]
        pfm_sb = sb2("pfm_sb", [8, 512], BF16)
        xts = [sb2(f"xts{i}", [128, 512]) for i in range(2)]
        sts = [sb2(f"sts{i}", [128, 512]) for i in range(2)]

        psim = [ps2(f"psim{i}", [128, 512]) for i in range(2)]
        pav = ps2("pav", [128, 1024])
        pg = ps2("pg", [128, 512])
        pu = ps2("pu", [128, 512])
        py = [ps2(f"py{i}", [128, 512]) for i in range(2)]

        eb = EB()
        A = S

        mk = dict(sim_slice={}, woe_slice={}, sp_slice={}, kv={}, te=0, tp=0,
                  epa=0, epv=0, epi=0, wo_slice={}, ffn_mm_tt={}, ye_tt={},
                  ct_tt={}, gt_tt={}, fout_tt={})
        phi_ctr = [0]
        kv_ctr = [0]
        pend_av = []

        def co(h, j):
            return P["A_bc"][:, h * 9 + j:h * 9 + j + 1]

        def coB(h, j):
            return P["A_bcb"][:, h * 9 + j:h * 9 + j + 1].broadcast_to([128, 1024])

        def emit_av(pidx, k, qt, sp, hh, h, sb_, kvb, first_q, nsp):
            sb_ = pidx % 3
            wl = [(A["aPHI"], pidx + 1)]
            if first_q:
                wl.append((A["aTE"], mk["te"]))
                wl.append((A["aEPI"], mk["epi"]))
            valid = [(kcw, ts) for kcw in range(2) for ts in range(4)
                     if 2 * sp + kcw <= 4 * qt + ts]
            for n_i, (kcw, ts) in enumerate(valid):
                st = (sp == 0 and kcw == 0 and ts == 0 and hh in (0, 2))
                stp = (2 * sp + kcw == 4 * qt + ts)
                eb.op("tensor", lambda E, kcw=kcw, ts=ts, hh=hh, h=h, sb_=sb_,
                      kvb=kvb, st=st, stp=stp: E.matmul(
                          pav[:, hh * 256 + ts * 64:hh * 256 + (ts + 1) * 64],
                          st_f[sb_][:, kcw, ts * 128:(ts + 1) * 128],
                          vsp[kvb][:, kcw, h * 64:(h + 1) * 64],
                          start=st, stop=stp, skip_group_check=True),
                      waits=(wl if n_i == 0 else []),
                      incs=([(A["aAV"], 1)] if n_i == len(valid) - 1 else []))

        def attn_slice(k):
            seq, qt = divmod(k, 4)
            nsp = 2 * (qt + 1)
            t0 = k * 512
            qb = k % 2
            wl = [(S[n], v) for n, v in P1B]
            if k >= 1:
                wl.append((A["aSIM"], mk["sim_slice"][k - 1]))
            eb.op("sync", lambda E, qb=qb, t0=t0: E.dma_start(
                qtl[qb][:], qf.rearrange("c p t -> p c t")[:, :, t0:t0 + 512]),
                waits=wl, incs=[(A["aQ"], 16)])
            aQ_k = A["aQ"].n
            yield 1.0
            def kv_load(sp):
                kvi = kv_ctr[0]
                kv_ctr[0] += 1
                kvb = kvi % 2
                tk0 = seq * 2048 + sp * 256
                sK = A["aK0"] if kvb == 0 else A["aK1"]
                sV = A["aV0"] if kvb == 0 else A["aV1"]
                wlk = [(S[n], v) for n, v in P1B]
                wlv = []
                if kvi >= 2:
                    simm, avm = mk["kv"][kvi - 2]
                    wlk.append((A["aSIM"], simm))
                    wlv.append((A["aAV"], avm))
                    while A["aAV"].n < avm and pend_av:
                        pend_av.pop(0)()
                eb.op("sync", lambda E, kvb=kvb, tk0=tk0: E.dma_start(
                    ksp[kvb][:], kf.rearrange("c p t -> p c t")[:, :, tk0:tk0 + 256]),
                    waits=wlk, incs=[(sK, 16)])
                eb.op("sync", lambda E, kvb=kvb, tk0=tk0: E.dma_start(
                    vsp[kvb][:],
                    vt.rearrange("(a w p) d -> a p w d", p=128, w=2)[tk0 // 256]),
                    waits=wlv, incs=[(sV, 16)])
                return kvi, kvb, (sK.n, sV.n, sK, sV)

            CP, CV, CA = S["cPOOL"], S["cDVE"], S["cACT"]

            def emit_prep(hq, sp, hpair, kvi, kvb, kvn, diag):
                CH = []
                for cc in range(2):
                    hh = hpair + cc
                    h = hq * 4 + hh
                    pidx = phi_ctr[0]
                    phi_ctr[0] += 1
                    CH.append(dict(hh=hh, h=h, pidx=pidx, sb=pidx % 4,
                                   c=h // 2, par=(h % 2) * 64))
                for ch in CH:
                    pidx, sb_ = ch["pidx"], ch["sb"]
                    for kcw in range(2):
                        ei = 2 * pidx + kcw
                        bank = ei % 2
                        wl = []
                        if kcw == 0:
                            wl = [(kvn[2], kvn[0]), (kvn[3], kvn[1]),
                                  (A["aQ"], aQ_k)]
                        if ei >= 2:
                            wl.append((A["aEV"], 2 * (ei - 2) + 2))
                        eb.op("tensor", lambda E, c=ch["c"], par=ch["par"],
                              kcw=kcw, bank=bank, kvb=kvb, qb=qb: E.matmul(
                                  psim[bank][:],
                                  ksp[kvb][par:par + 64, c,
                                           kcw * 128:(kcw + 1) * 128],
                                  qtl[qb][par:par + 64, c, :],
                                  start=True, stop=True),
                              waits=wl, incs=[(A["aSIM"], 1)])
                        smn = A["aSIM"].n
                        wl2 = [(A["aSIM"], smn)]
                        if pidx >= 4:
                            wl2.append((A["aPHI"], 2 * (pidx // 2) - 2))
                        eb.op("scalar", lambda E, bank=bank, sb_=sb_,
                              kcw=kcw: E.activation(
                                  st_s[sb_][:, kcw, :], psim[bank][:],
                                  AF.Identity),
                              waits=wl2, incs=[(A["aEV"], 2)])
                    ch["evn"] = A["aEV"].n
                for ch in CH:
                    sb_, pidx = ch["sb"], ch["pidx"]
                    zc = st_z[sb_][:].rearrange("p w t -> p (w t)")
                    sc = st_s[sb_][:].rearrange("p w t -> p (w t)")
                    ch["zc"], ch["sc"] = zc, sc
                    ch["qs"] = st_q[pidx % 2][:]
                    eb.op("gpsimd", lambda E, zc=zc, sc=sc: E.tensor_tensor(
                        zc, sc, sc, ALU.mult),
                        waits=[(A["aEV"], ch["evn"]),
                               (A["aPHI"], max(0, 2 * (pidx // 2) - 2)),
                               (CP, CP.n)],
                        incs=[(CP, 1)])
                    ch["zn"] = CP.n
                return dict(CH=CH, kvi=kvi, kvb=kvb, diag=diag, sp=sp,
                            simn=A["aSIM"].n)

            def emit_chain(g, foq, k, qt, nsp):
                CH, kvb, diag, sp = g["CH"], g["kvb"], g["diag"], g["sp"]
                while len(pend_av) > 1:
                    pend_av.pop(0)()
                for ch in CH:
                    h, pidx = ch["h"], ch["pidx"]
                    pst = stp2[pidx % 2][:]
                    ch["pst"] = pst
                    eb.op("scalar", lambda E, zc=ch["zc"], h=h, pst=pst:
                          E.activation(pst, zc, AF.Identity,
                                       bias=co(h, 6), scale=co(h, 8)),
                          waits=[(CP, ch["zn"]),
                                 (A["aPHI"], max(0, pidx - 1)),
                                 (CA, max(0, CA.n - 1))],
                          incs=[(CA, 1)])
                    eb.op("scalar", lambda E, zc=ch["zc"], h=h,
                          qs=ch["qs"]: E.activation(
                              qs, zc, AF.Identity, bias=co(h, 5),
                              scale=co(h, 7)),
                          waits=[(CP, ch["zn"]),
                                 (A["aPHI"], max(0, pidx - 1)),
                                 (CA, max(0, CA.n - 1))],
                          incs=[(CA, 1)])
                    ch["initn"] = CA.n

                def dve(ch, f, waits=()):
                    eb.op("vector", f,
                          waits=list(waits) + [(CV, max(0, CV.n - 1))],
                          incs=[(CV, 1)])
                    return CV.n

                def act(ch, f, waits=()):
                    eb.op("scalar", f,
                          waits=list(waits) + [(CA, max(0, CA.n - 1))],
                          incs=[(CA, 1)])
                    return CA.n
                for ch in CH:
                    ch["m"] = dve(ch, lambda E, p=ch["pst"], zc=ch["zc"]:
                                  E.tensor_tensor(p, p, zc, ALU.mult),
                                  [(CA, ch["initn"])])
                for ch in CH:
                    bc4 = coB(ch["h"], 4)
                    eb.op("gpsimd", lambda E, p=ch["pst"], b=bc4:
                          E.tensor_tensor(p, p, b, ALU.add),
                          waits=[(CV, ch["m"]), (CP, max(0, CP.n - 1))],
                          incs=[(CP, 1)])
                    ch["mp"] = CP.n
                for ch in CH:
                    ch["mq"] = dve(ch, lambda E, q=ch["qs"], zc=ch["zc"]:
                                   E.tensor_tensor(q, q, zc, ALU.mult))
                for ch in CH:
                    ch["m"] = dve(ch, lambda E, p=ch["pst"], zc=ch["zc"]:
                                  E.tensor_tensor(p, p, zc, ALU.mult),
                                  [(CP, ch["mp"])])
                for ch in CH:
                    ch["aq"] = act(ch, lambda E, q=ch["qs"], h=ch["h"]:
                                   E.activation(q, q, AF.Identity,
                                                bias=co(h, 3)),
                                   [(CV, ch["mq"])])
                for ch in CH:
                    ch["ap"] = act(ch, lambda E, p=ch["pst"], h=ch["h"]:
                                   E.activation(p, p, AF.Identity,
                                                bias=co(h, 2)),
                                   [(CV, ch["m"])])
                for ch in CH:
                    ch["mq"] = dve(ch, lambda E, q=ch["qs"], zc=ch["zc"]:
                                   E.tensor_tensor(q, q, zc, ALU.mult),
                                   [(CA, ch["aq"])])
                for ch in CH:
                    ch["m"] = dve(ch, lambda E, p=ch["pst"], zc=ch["zc"]:
                                  E.tensor_tensor(p, p, zc, ALU.mult),
                                  [(CA, ch["ap"])])
                for ch in CH:
                    bc1 = coB(ch["h"], 1)
                    eb.op("gpsimd", lambda E, q=ch["qs"], b=bc1:
                          E.tensor_tensor(q, q, b, ALU.add),
                          waits=[(CV, ch["mq"]), (CP, max(0, CP.n - 1))],
                          incs=[(CP, 1)])
                    ch["mp"] = CP.n
                for ch in CH:
                    bc0 = coB(ch["h"], 0)
                    ch["m"] = dve(ch, lambda E, p=ch["pst"], b=bc0:
                                  E.tensor_tensor(p, p, b, ALU.add))
                for ch in CH:
                    ch["mq"] = dve(ch, lambda E, q=ch["qs"], sc=ch["sc"]:
                                   E.tensor_tensor(q, q, sc, ALU.mult),
                                   [(CP, ch["mp"])])
                for ch in CH:
                    pidx = ch["pidx"]
                    fcs = st_f[pidx % 3][:].rearrange("p w t -> p (w t)")
                    wl = [(CV, max(0, CV.n - 1))]
                    if pidx >= 3:
                        wl.append((A["aAV"], pidx - 2))
                    if diag:
                        eb.op("vector", lambda E, fcs=fcs, p=ch["pst"],
                              q=ch["qs"]: E.tensor_tensor(
                                  fcs, p, q, ALU.add),
                              waits=wl, incs=[(CV, 1)])
                        dmo = 0 if sp == nsp - 2 else 1024
                        eb.op("vector", lambda E, fcs=fcs, dmo=dmo:
                              E.tensor_tensor(
                                  fcs, fcs, P["dms"][:, dmo:dmo + 1024],
                                  ALU.mult),
                              waits=[(CV, CV.n)],
                              incs=[(A["aPHI"], 1)])
                    else:
                        eb.op("vector", lambda E, fcs=fcs, p=ch["pst"],
                              q=ch["qs"]: E.tensor_tensor(
                                  fcs, p, q, ALU.add),
                              waits=wl, incs=[(A["aPHI"], 1)])
                for ch in CH:
                    pend_av.append(
                        lambda pidx=ch["pidx"], k=k, qt=qt, sp=sp,
                        hh=ch["hh"], h=ch["h"], sb_=ch["sb"], kvb=kvb,
                        fq=foq[0], nsp=nsp:
                        emit_av(pidx, k, qt, sp, hh, h, sb_, kvb, fq,
                                nsp))
                    foq[0] = False
                mk["kv"][g["kvi"]] = (g["simn"],
                                      A["aAV"].n + len(pend_av))

            for hq in range(4):
                foq = [True]
                pend_kv = kv_load(0)
                pend_chain = None
                for sp in range(nsp):
                    kvi, kvb, kvn = pend_kv
                    diag = sp >= nsp - 2
                    for hpair in (0, 2):
                        g = emit_prep(hq, sp, hpair, kvi, kvb, kvn, diag)
                        if pend_chain is not None:
                            pend_chain()
                        pend_chain = (lambda g=g, foq=foq, k=k, qt=qt,
                                      nsp=nsp: emit_chain(g, foq, k, qt, nsp))
                        if hpair == 0:
                            yield 6.0
                    if sp + 1 < nsp:
                        pend_kv = kv_load(sp + 1)
                    yield 6.0
                if pend_chain is not None:
                    pend_chain()
                while pend_av:
                    pend_av.pop(0)()
                avq = A["aAV"].n
                epi_wl = [(A["aAV"], avq), (A["aEPA"], mk["epa"]),
                          (A["aTP"], mk["tp"])]
                for hi in range(4):
                    pv = pav[:, hi * 256:(hi + 1) * 256].rearrange(
                        "p (ts d) -> p ts d", ts=4)
                    dvb = P["dvls"][:, 4 * qt:4 * qt + 4].unsqueeze(2).broadcast_to(
                        [128, 4, 64])
                    eb.op("vector", lambda E, hi=hi, pv=pv, dvb=dvb: E.tensor_tensor(
                        ao_hq[:, :, hi, :], pv, dvb, ALU.mult),
                        waits=(epi_wl if hi == 0 else []),
                        incs=([(A["aEPI"], 1)] if hi == 3 else []))
                epin = A["aEPI"].n
                mk["epi"] = epin
                eb.op("scalar", lambda E: E.activation(
                    scr1[:], ao_hq[:].rearrange("p a b c -> p (a b c)"), AF.Square),
                    waits=[(A["aEPI"], epin), (A["aEPV"], mk["epv"]),
                           (A["aSPD"], mk["sp_slice"].get(k - 1, 0))],
                    incs=[(A["aEPA"], 1)])
                mk["epa"] = A["aEPA"].n
                vcm = S["sVC"].n

                def epi_dve(E, epan=mk["epa"], vcm=vcm):
                    E.wait_ge(A["aEPA"].h, epan)
                    go = small_chain(E, S["sVC"], vcm)
                    go(lambda: E.reduce_sum(
                        est[:, 0:16].unsqueeze(2),
                        scr1[:].rearrange("p (g d) -> p g d", d=64), AX.X))
                    go(lambda: E.tensor_scalar(est[:, 16:32], est[:, 0:16],
                                               1.0 / 64, 1e-6, ALU.mult, ALU.add))
                    I32 = mybir.dt.int32
                    t1 = est[:, 32:48]
                    t2 = est[:, 48:64]
                    oi = t1.bitcast(I32)
                    go(lambda: E.tensor_scalar(oi, est[:, 16:32].bitcast(I32), 23,
                                               None, ALU.arith_shift_right))
                    go(lambda: E.tensor_scalar(oi, oi, -1, None, ALU.bitwise_xor))
                    go(lambda: E.tensor_scalar(oi, oi, 382, None, ALU.add))
                    go(lambda: E.tensor_scalar(oi, oi, 1, None, ALU.arith_shift_right))
                    go(lambda: E.tensor_scalar(oi, oi, 23, None, ALU.arith_shift_left))
                    for _ in range(5):
                        go(lambda: E.tensor_tensor(t2, t1, t1, ALU.mult))
                        go(lambda: E.tensor_tensor(t2, t2, est[:, 16:32], ALU.mult))
                        go(lambda: E.tensor_scalar(t2, t2, -0.5, 1.5, ALU.mult,
                                                   ALU.add))
                        go(lambda: E.tensor_tensor(t1, t1, t2, ALU.mult))
                    i = go(lambda: E.tensor_tensor(
                        ao_hq[:].rearrange("p ts h d -> p (ts h) d"),
                        ao_hq[:].rearrange("p ts h d -> p (ts h) d"),
                        est[:, 32:48].rearrange("p (ts h) -> p ts h", ts=4)
                        .rearrange("p ts h -> p (ts h)").unsqueeze(2)
                        .broadcast_to([128, 16, 64]), ALU.mult), inc=False)
                    i.then_inc(A["aEPV"].h, 1)
                eb.q["vector"].append(epi_dve)
                S["sVC"].n += 27
                A["aEPV"].n += 1
                mk["epv"] = A["aEPV"].n
                rmap8 = lambda j: (j % 2) * 4 + j // 2
                tpw0 = [(A["aEPV"], mk["epv"]), (A["aWO"], mk["wo_slice"].get(k - 1, 0))]
                for j in range(8):
                    ts, jj = divmod(j, 2)
                    jc = hq * 2 + jj
                    r = rmap8(j)
                    wl = list(tpw0) if j == 0 else []
                    if j >= 2:
                        wl.append((A["aTE"], mk["te"] + j - 1))
                    eb.op("tensor", lambda E, ts=ts, jj=jj, r=r: E.transpose(
                        pav[:, r * 128:(r + 1) * 128],
                        ao_hq[:, ts, jj * 2:jj * 2 + 2, :].rearrange(
                            "p h d -> p (h d)"),
                        P["sidF"][:]),
                        waits=wl, incs=[(A["aTP"], 1)])
                    eb.op("scalar", lambda E, ts=ts, jc=jc, r=r: E.activation(
                        ao_fm[:, jc, ts * 128:(ts + 1) * 128],
                        pav[:, r * 128:(r + 1) * 128], AF.Identity),
                        waits=[(A["aTP"], A["aTP"].n)], incs=[(A["aTE"], 1)])
                mk["tp"] = A["aTP"].n
                mk["te"] = A["aTE"].n
                yield 8.0
            evall = A["aEV"].n
            wo0 = A["aWO"].n
            for g in range(8):
                ts, ih = divmod(g, 2)
                bank = g % 2
                wl = []
                if g == 0:
                    wl = [(A["aEV"], evall), (A["aTE"], mk["te"]),
                          (A["aEPV"], mk["epv"])]
                if g >= 2:
                    wl.append((A["aWOE"], wo0 // 1 + g - 1 if False else
                               A["aWOE"].n - 1))
                for jc in range(8):
                    eb.op("tensor", lambda E, jc=jc, ts=ts, ih=ih, bank=bank:
                          E.matmul(psim[bank][:],
                                   ao_fm[:, jc, ts * 128:(ts + 1) * 128],
                                   P["WoTb"][:, jc, ih * 512:(ih + 1) * 512],
                                   start=(jc == 0), stop=(jc == 7)),
                          waits=(wl if jc == 0 else []),
                          incs=([(A["aWO"], 1)] if jc == 7 else []))
                won = A["aWO"].n
                spp = A["aSP0"] if g % 2 == 0 else A["aSP1"]
                wle = [(A["aWO"], won)]
                if g >= 2:
                    wle.append((spp, spp.n))
                eb.op("scalar", lambda E, bank=bank, g=g: E.activation(
                    scr1[:, (g % 2) * 512:(g % 2) * 512 + 512], psim[bank][:],
                    AF.Identity),
                    waits=wle, incs=[(A["aWOE"], 1)])
                woen = A["aWOE"].n
                r0 = t0 + ts * 128
                eb.op("scalar", lambda E, r0=r0, ih=ih, g=g: E.dma_start(
                    sS[r0:r0 + 128, ih * 512:(ih + 1) * 512],
                    scr1[:, (g % 2) * 512:(g % 2) * 512 + 512]),
                    waits=[(A["aWOE"], woen)], incs=[(spp, 16)])
            sp0n, sp1n = A["aSP0"].n, A["aSP1"].n

            def s_echo(E, sp0n=sp0n, sp1n=sp1n):
                E.wait_ge(A["aSP0"].h, sp0n)
                E.wait_ge(A["aSP1"].h, sp1n)
                i = E.copy(est[:, 0:1], est[:, 0:1])
                i.then_inc(A["aSPD"].h, 1)
            eb.q["scalar"].append(s_echo)
            A["aSPD"].n += 1
            mk["wo_slice"][k] = A["aWO"].n
            mk["woe_slice"][k] = A["aWOE"].n
            mk["sim_slice"][k] = A["aSIM"].n
            mk["sp_slice"][k] = A["aSPD"].n
            yield 6.0

        FWE = 48

        def ffn_tt(tt):
            xb = tt % 2
            fxp = A["fXN0"] if xb == 0 else A["fXN1"]
            wl = [(S[n], v) for n, v in P1B]
            if tt >= 2:
                wl.append((A["fPG"], mk["ffn_mm_tt"][tt - 2][0]))
                wl.append((A["fPU"], mk["ffn_mm_tt"][tt - 2][1]))
            eb.op("sync", lambda E, xb=xb, tt=tt: E.dma_start(
                xtile[xb][:],
                xnf.rearrange("c p t -> p c t")[:, :, tt * 512:(tt + 1) * 512]),
                waits=wl, incs=[(fxp, 16)])
            wlp = [(S["sPFM"], 16 * NCH_L)]
            if tt >= 1:
                wlp.append((A["fBI"], A["fBI"].n))
            eb.op("sync", lambda E, tt=tt: E.dma_start(
                pfm_sb[:].rearrange("e (a t) -> e a t", a=4),
                tens["pfm"][:, tt * 4:tt * 4 + 4, :]),
                waits=wlp, incs=[(A["fPF"], 16)])
            yield 2.0
            for e in range(NE):
                wl = [(S["sFW"], S["sFW"].n)]
                if not (tt == 0 and e == 0):
                    wl.append((A["fPY"], A["fPY"].n))
                eb.op("sync", lambda E, e=e: E.dma_start(
                    Dsb[:], dwb[e].rearrange("(f p) d -> p f d", p=128)),
                    waits=wl, incs=[(A["fD"], 16)])
                fdn = A["fD"].n
                gu_marks = {}
                FWALL = S["sFW"].n
                for fcp in range(2):
                    gs = A["fG0"] if fcp % 2 == 0 else A["fG1"]
                    us = A["fU0"] if fcp % 2 == 0 else A["fU1"]
                    wl = [(S["sFW"], FWALL)]
                    wlu = []
                    if not (tt == 0 and e == 0 and fcp < 2):
                        wl.append((A["fPG"], A["fPG"].n))
                        wlu.append((A["fPU"], A["fPU"].n))
                    eb.op("sync", lambda E, e=e, fcp=fcp: E.dma_start(
                        gch[fcp % 2][:],
                        gwb[e].rearrange("(c p) f -> p c f", p=128)[
                            :, :, fcp * 256:(fcp + 1) * 256]),
                        waits=wl, incs=[(gs, 16)])
                    eb.op("sync", lambda E, e=e, fcp=fcp: E.dma_start(
                        uch[fcp % 2][:],
                        uwb[e].rearrange("(c p) f -> p c f", p=128)[
                            :, :, fcp * 256:(fcp + 1) * 256]),
                        waits=wlu, incs=[(us, 16)])
                    gu_marks[fcp] = (gs, gs.n, us, us.n)
                for fc in range(16):
                    fcp = fc // 2
                    if fc % 2 == 0 and fc >= 2 and fcp + 1 < 8:
                        gs = A["fG0"] if (fcp + 1) % 2 == 0 else A["fG1"]
                        us = A["fU0"] if (fcp + 1) % 2 == 0 else A["fU1"]
                        eb.op("sync", lambda E, e=e, fcp=fcp: E.dma_start(
                            gch[(fcp + 1) % 2][:],
                            gwb[e].rearrange("(c p) f -> p c f", p=128)[
                                :, :, (fcp + 1) * 256:(fcp + 2) * 256]),
                            waits=[(A["fPG"], A["fPG"].n)], incs=[(gs, 16)])
                        eb.op("sync", lambda E, e=e, fcp=fcp: E.dma_start(
                            uch[(fcp + 1) % 2][:],
                            uwb[e].rearrange("(c p) f -> p c f", p=128)[
                                :, :, (fcp + 1) * 256:(fcp + 2) * 256]),
                            waits=[(A["fPU"], A["fPU"].n)], incs=[(us, 16)])
                        gu_marks[fcp + 1] = (gs, gs.n, us, us.n)
                    for dc in range(8):
                        wl = []
                        if dc == 0:
                            wl = [(gu_marks[fcp][0], gu_marks[fcp][1]),
                                  (fxp, 16 * (tt // 2 + 1)),
                                  (A["fSIL"], A["fSIL"].n)]
                        eb.op("tensor", lambda E, dc=dc, fc=fc, fcp=fcp, xb=xb:
                              E.matmul(pg[:],
                                       gch[fcp % 2][:, dc, (fc % 2) * 128:(fc % 2) * 128 + 128],
                                       xtile[xb][:, dc, :],
                                       start=(dc == 0), stop=(dc == 7)),
                              waits=wl, incs=([(A["fPG"], 1)] if dc == 7 else []))
                    pgn = A["fPG"].n
                    wl = [(A["fPG"], pgn), (A["fH"], max(0, A["fH"].n - 1))]
                    eb.op("scalar", lambda E, e=e, fc=fc: E.activation(
                        hs_scr[fc % 2][:], pg[:], AF.Silu,
                        bias=P["gbs"][:, e * 16 + fc:e * 16 + fc + 1]),
                        waits=wl, incs=[(A["fSIL"], 1)])
                    siln = A["fSIL"].n
                    for dc in range(8):
                        wl = []
                        if dc == 0:
                            wl = [(gu_marks[fcp][2], gu_marks[fcp][3]),
                                  (A["fH"], A["fH"].n)]
                        eb.op("tensor", lambda E, dc=dc, fc=fc, fcp=fcp, xb=xb:
                              E.matmul(pu[:],
                                       uch[fcp % 2][:, dc, (fc % 2) * 128:(fc % 2) * 128 + 128],
                                       xtile[xb][:, dc, :],
                                       start=(dc == 0), stop=(dc == 7)),
                              waits=wl, incs=([(A["fPU"], 1)] if dc == 7 else []))
                    pun = A["fPU"].n
                    wl = [(A["fPU"], pun), (A["fSIL"], siln)]
                    if tt > 0 or e > 0:
                        wl.append((A["fPY"], A["fPY"].n))
                    eb.op("vector", lambda E, e=e, fc=fc: E.scalar_tensor_tensor(
                        hsb[:, fc, :], pu[:],
                        P["ubs"][:, e * 16 + fc:e * 16 + fc + 1],
                        hs_scr[fc % 2][:], ALU.add, ALU.mult),
                        waits=wl, incs=[(A["fH"], 1)])
                    if fc % 2 == 1:
                        yield 7.0
                fhn = A["fH"].n
                if e == 0:
                    for g8 in range(8):
                        t128, dh = divmod(g8, 2)
                        ct = tt * 4 + t128
                        bank = g8 % 2
                        wl = []
                        if g8 == 0:
                            wl = [(A["fYE"], mk["ye_tt"].get(tt - 1, 0)),
                                  (A["fPF"], A["fPF"].n)]
                        if g8 >= 2:
                            wl.append((A["fBIC"], A["fBIC"].n - 1))
                        eb.op("tensor", lambda E, t128=t128, dh=dh, bank=bank: E.matmul(
                            py[bank][:], pfm_sb[0:8, t128 * 128:(t128 + 1) * 128],
                            P["dbs"][0:8, dh * 512:(dh + 1) * 512],
                            start=True, stop=True),
                            waits=wl, incs=[(A["fBI"], 1)])
                        bin_ = A["fBI"].n
                        wl2 = [(A["fBI"], bin_)]
                        if g8 == 0:
                            wl2.append((A["fGT"], mk["gt_tt"].get(tt - 1, 0)))
                        eb.op("vector", lambda E, t128=t128, dh=dh, bank=bank: E.tensor_copy(
                            y_acc[:, t128, dh * 512:(dh + 1) * 512], py[bank][:]),
                            waits=wl2, incs=[(A["fBIC"], 1)])
                    bic_all = A["fBIC"].n
                for g8 in range(8):
                    t128, dh = divmod(g8, 2)
                    ct = tt * 4 + t128
                    bank = g8 % 2
                    for fc in range(16):
                        wl = []
                        if fc == 0:
                            wl = [(A["fH"], fhn), (A["fD"], fdn),
                                  (A["fYE"], max(0, A["fYE"].n - 1))]
                            if e == 0:
                                wl.append((A["fBIC"], bic_all))
                        eb.op("tensor", lambda E, fc=fc, t128=t128, dh=dh,
                              bank=bank: E.matmul(
                                  py[bank][:],
                                  hsb[:, fc, t128 * 128:(t128 + 1) * 128],
                                  Dsb[:, fc, dh * 512:(dh + 1) * 512],
                                  start=(fc == 0), stop=(fc == 15)),
                              waits=wl, incs=([(A["fPY"], 1)] if fc == 15 else []))
                    pyn = A["fPY"].n
                    eb.op("vector", lambda E, t128=t128, dh=dh, bank=bank, ct=ct,
                          e=e: E.scalar_tensor_tensor(
                              y_acc[:, t128, dh * 512:(dh + 1) * 512],
                              py[bank][:], P["probs_res"][:, ct, e:e + 1],
                              y_acc[:, t128, dh * 512:(dh + 1) * 512],
                              ALU.mult, ALU.add),
                          waits=[(A["fPY"], pyn)], incs=[(A["fYE"], 1)])
                    yield 3.5
            mk["ffn_mm_tt"][tt] = (A["fPG"].n, A["fPU"].n)
            mk["ye_tt"][tt] = A["fYE"].n

        def ffn_tail(tt):
            yeall = mk["ye_tt"][tt]
            fhn = A["fH"].n
            assert tt in mk["sp_slice"], f"attn slice {tt} must precede ffn tail"
            spm = mk["sp_slice"][tt]
            for i in range(8):
                t128, i2 = divmod(i, 2)
                r0 = tt * 512 + t128 * 128
                cs = i2 * 512
                wl = [(A["fYE"], yeall), (A["fCT"], max(0, A["fCT"].n - 1))]
                if i == 0:
                    wl.append((A["fH"], fhn))
                eb.op("scalar", lambda E, t128=t128, cs=cs, i=i: E.activation(
                    hs_scr[i % 2][:], y_acc[:, t128, cs:cs + 512], AF.Silu),
                    waits=wl, incs=[(A["fGT"], 1)])
                gtn = A["fGT"].n
                fxx = A["fXSx0"] if i % 2 == 0 else A["fXSx1"]
                fxss = A["fXSs0"] if i % 2 == 0 else A["fXSs1"]
                wl = [(A["fCT"], max(0, A["fCT"].n - 1))]
                eb.op("sync", lambda E, r0=r0, cs=cs, i=i: E.dma_start(
                    xts[i % 2][:], x[r0:r0 + 128, cs:cs + 512]),
                    waits=wl, incs=[(fxx, 16)])
                fop = A["fOUT0"] if i % 2 == 0 else A["fOUT1"]
                eb.op("sync", lambda E, r0=r0, cs=cs, i=i: E.dma_start(
                    sts[i % 2][:], sS[r0:r0 + 128, cs:cs + 512]),
                    waits=[(A["aSPD"], spm), (A["fCT"], max(0, A["fCT"].n - 1)),
                           (fop, fop.n)],
                    incs=[(fxss, 16)])
                xsn = (fxx, fxx.n, fxss, fxss.n)
                vcc = S["sVC"].n

                def comb(E, i=i, gtn=gtn, xsn=xsn, vcc=vcc):
                    E.wait_ge(A["fGT"].h, gtn)
                    E.wait_ge(xsn[0].h, xsn[1])
                    E.wait_ge(xsn[2].h, xsn[3])
                    go = small_chain(E, S["sVC"], vcc)
                    go(lambda: E.tensor_tensor(sts[i % 2][:], sts[i % 2][:],
                                               hs_scr[i % 2][:], ALU.mult))
                    ii = go(lambda: E.tensor_tensor(sts[i % 2][:], sts[i % 2][:],
                                                    xts[i % 2][:], ALU.add),
                            inc=False)
                    ii.then_inc(A["fCT"].h, 1)
                eb.q["vector"].append(comb)
                S["sVC"].n += 1
                A["fCT"].n += 1
                ctn = A["fCT"].n
                eb.op("scalar", lambda E, r0=r0, cs=cs, i=i: E.dma_start(
                    out[r0:r0 + 128, cs:cs + 512], sts[i % 2][:]),
                    waits=[(A["fCT"], ctn)], incs=[(fop, 16)])
                yield 2.0
            mk["ct_tt"][tt] = A["fCT"].n
            mk["gt_tt"][tt] = A["fGT"].n

        only_attn = (phases == 2)
        for k in (SLICES if SLICES is not None else range(NSLICE)):
            a = attn_slice(k)
            f = None if only_attn else ffn_tt(k)
            a_done = False
            f_done = only_attn
            ta = tf = 0.0
            while not (a_done and f_done):
                if not a_done and (f_done or ta <= tf):
                    try:
                        ta += next(a)
                    except StopIteration:
                        a_done = True
                else:
                    try:
                        tf += next(f)
                    except StopIteration:
                        f_done = True
            if not only_attn:
                for _ in ffn_tail(k):
                    pass

        fins = [(A["fOUT0"], A["fOUT0"].n), (A["fOUT1"], A["fOUT1"].n),
                (A["aSP0"], A["aSP0"].n), (A["aSP1"], A["aSP1"].n)]

        def fin(E, fins=fins):
            for s, v in fins:
                if v:
                    E.wait_ge(s.h, v)
        eb.q["sync"].append(fin)
        eb.emit(nc)


_BF = ml_dtypes.bfloat16


def host_inputs(inputs):
    f32 = np.float32
    W = {}
    W["wqT"] = np.ascontiguousarray(inputs["L0_attn_Wq"].T.astype(f32, copy=False))
    W["wkT"] = np.ascontiguousarray(inputs["L0_attn_Wk"].T.astype(f32, copy=False))
    W["wvT"] = np.ascontiguousarray(inputs["L0_attn_Wv"].T.astype(f32, copy=False))
    W["woT"] = np.ascontiguousarray(inputs["L0_attn_Wo"].T.astype(f32, copy=False))
    W["rwT"] = np.ascontiguousarray(inputs["L0_ffn_router_w"].T.astype(f32, copy=False))
    W["polyT"] = np.ascontiguousarray(
        inputs["L0_attn_poly_weights"].T.astype(f32, copy=False))
    W["hdrow"] = np.ascontiguousarray(
        inputs["L0_attn_head_degrees"].reshape(1, 16).astype(f32, copy=False))
    nw = inputs["L0_attn_norm_w"].astype(f32, copy=False)
    W["nw2"] = np.ascontiguousarray(np.concatenate([nw, nw]).reshape(128, 1))
    W["gw"] = np.ascontiguousarray(inputs["L0_ffn_gate_w"].astype(f32, copy=False))
    W["uw"] = np.ascontiguousarray(inputs["L0_ffn_up_w"].astype(f32, copy=False))
    W["dw"] = np.ascontiguousarray(inputs["L0_ffn_down_w"].astype(f32, copy=False))
    W["gbl"] = np.ascontiguousarray(
        inputs["L0_ffn_gate_b"].reshape(NE, 16, 128).transpose(2, 0, 1)
        .reshape(128, 128).astype(f32, copy=False))
    W["ubl"] = np.ascontiguousarray(
        inputs["L0_ffn_up_b"].reshape(NE, 16, 128).transpose(2, 0, 1)
        .reshape(128, 128).astype(f32, copy=False))
    W["db"] = np.ascontiguousarray(inputs["L0_ffn_down_b"].astype(f32, copy=False))
    W["identF"] = np.eye(128, dtype=f32)
    W["identB"] = np.eye(128, dtype=_BF)
    W["rmat"] = gegen_monomials()
    W["kcol"] = np.arange(9, dtype=f32).reshape(9, 1)
    p = np.arange(128, dtype=np.float64)
    j = np.arange(16, dtype=np.float64)
    W["dvl"] = (1.0 / (j[None, :] * 128 + p[:, None] + 1.0)).astype(f32)
    dm = np.zeros((128, 2, 2, 512), np.float32)
    for v in range(2):
        for kcw in range(2):
            thr = (2 * v + kcw) * 128 + p[:, None]
            dm[:, v, kcw, :] = (np.arange(512)[None, :] >= thr)
    W["dmc"] = np.ascontiguousarray(dm.reshape(128, 2048).astype(_BF))
    return W


def kernel(**inputs):
    nc, _ = build_nc(debug=DEBUG, phases=BUILD_PHASES)
    W = host_inputs(inputs)
    xf = inputs["L0_attn_x"].astype(np.float32, copy=False)
    in_maps = []
    for c in range(NCORE):
        m = dict(W)
        m["x"] = np.ascontiguousarray(xf[c * 4:(c + 1) * 4].reshape(TOK, DD))
        in_maps.append(m)
    res = run_bass_kernel_spmd(nc, in_maps, core_ids=list(range(NCORE)))
    outs = [np.asarray(r["out"]) for r in res.results]
    return np.concatenate(outs, axis=0).reshape(32, 2048, 1024)


# version 10
# speedup vs baseline: 1.0912x; 1.0280x over previous
import numpy as np
import ml_dtypes
import concourse.bass as bass
import concourse.mybir as mybir
from concourse.bass_utils import run_bass_kernel_spmd
from contextlib import ExitStack

F32 = mybir.dt.float32
BF16 = mybir.dt.bfloat16
AF = mybir.ActivationFunctionType
ALU = mybir.AluOpType
AX = mybir.AxisListType

NCORE = 8
TOK = 8192
NCH = 64
DD = 1024
DFF = 2048
NE = 8
NSLICE = 16
LAM = 31.5

BUILD_PHASES = 3
DEBUG = 0
NCH_L = NCH
SLICES = None


def gegen_monomials():
    R = [np.zeros(9, np.float64) for _ in range(9)]
    R[0][0] = 1.0
    R[1][1] = 1.0
    for k in range(2, 9):
        c1 = 2.0 * (k + LAM - 1.0) / (k + 2.0 * LAM - 1.0)
        c2 = (k - 1.0) / (k + 2.0 * LAM - 1.0)
        R[k] = c1 * np.concatenate([[0.0], R[k - 1][:-1]]) - c2 * R[k - 2]
    return np.stack(R).astype(np.float32)


MAGIC = 0x5F3759DF


def emit_rsqrt(E, out_f32, in_f32, t2_f32, newton=2):
    I32 = mybir.dt.int32
    oi = out_f32.bitcast(I32)
    E.tensor_scalar(oi, in_f32.bitcast(I32), 1, None, ALU.arith_shift_right)
    E.tensor_scalar(oi, oi, -1, None, ALU.bitwise_xor)
    E.tensor_scalar(oi, oi, MAGIC + 1, None, ALU.add)
    for _ in range(newton):
        E.tensor_tensor(t2_f32, out_f32, out_f32, ALU.mult)
        E.tensor_tensor(t2_f32, t2_f32, in_f32, ALU.mult)
        E.tensor_scalar(t2_f32, t2_f32, -0.5, 1.5, ALU.mult, ALU.add)
        E.tensor_tensor(out_f32, out_f32, t2_f32, ALU.mult)


class Sem:
    __slots__ = ("h", "n", "name")

    def __init__(self, nc, ctx, name):
        self.h = ctx.enter_context(nc.semaphore(name))
        self.n = 0
        self.name = name


class EB:

    def __init__(self):
        self.q = {"tensor": [], "vector": [], "scalar": [], "gpsimd": [], "sync": []}

    def op(self, eng, build, waits=(), incs=()):
        for s, v in waits:
            assert 0 <= v <= s.n, f"wait {s.name}>={v} pledged {s.n}"
        for s, n in incs:
            s.n += n

        def f(E, build=build, waits=tuple(waits), incs=tuple(incs)):
            for s, v in waits:
                if v > 0:
                    E.wait_ge(s.h, v)
            inst = build(E)
            for s, n in incs:
                inst.then_inc(s.h, n)
        self.q[eng].append(f)

    def emit(self, nc):
        with nc.Block() as blk:
            for name, fns in self.q.items():
                if fns:
                    getattr(blk, name)(lambda E, fns=fns: [f(E) for f in fns])


def small_chain(E, sem, n_start):
    state = {"n": n_start}

    def go(make, inc=True):
        E.wait_ge(sem.h, state["n"])
        inst = make()
        if inc:
            inst.then_inc(sem.h, 1)
            state["n"] += 1
        return inst
    return go


def build_nc(debug=0, phases=3):
    nc = bass.Bass("TRN2", target_bir_lowering=False)

    def din(name, shape, dt=F32):
        return nc.dram_tensor(name, shape, dt, kind="ExternalInput").ap()

    def dout(name, shape, dt=F32):
        return nc.dram_tensor(name, shape, dt, kind="ExternalOutput").ap()

    dbg_names = ("xnf", "qf", "kf", "vt", "sS") if debug else ()

    def dint(name, shape, dt):
        kind = "ExternalOutput" if name in dbg_names else "Internal"
        return nc.dram_tensor(name, shape, dt, kind=kind).ap()

    x = din("x", [TOK, DD])
    wqT = din("wqT", [DD, DD]); wkT = din("wkT", [DD, DD])
    wvT = din("wvT", [DD, DD]); woT = din("woT", [DD, DD])
    rwT = din("rwT", [DD, NE])
    polyT = din("polyT", [9, 16]); hdrow = din("hdrow", [1, 16])
    nw2 = din("nw2", [128, 1])
    gwi = din("gw", [NE, DD, DFF]); uwi = din("uw", [NE, DD, DFF])
    dwi = din("dw", [NE, DFF, DD])
    gbl = din("gbl", [128, 128]); ubl = din("ubl", [128, 128])
    dbi = din("db", [NE, DD])
    identF = din("identF", [128, 128]); identB = din("identB", [128, 128], BF16)
    rmat = din("rmat", [9, 9]); kcol = din("kcol", [9, 1])
    dvl = din("dvl", [128, 16]); dmc = din("dmc", [128, 2048], BF16)

    out = dout("out", [TOK, DD])
    dbgt = {}
    if debug:
        dbgt["d_probs"] = dout("d_probs", [128, NCH, NE])
        dbgt["d_lns"] = dout("d_lns", [NCH, 128, 12])

    xnf = dint("xnf", [8, 128, TOK], BF16)
    qf = dint("qf", [8, 128, TOK], BF16)
    kf = dint("kf", [8, 128, TOK], BF16)
    vt = dint("vt", [TOK, DD], BF16)
    sS = dint("sS", [TOK, DD], F32)
    gwb = dint("gwb", [NE, DD, DFF], BF16)
    uwb = dint("uwb", [NE, DD, DFF], BF16)
    dwb = dint("dwb", [NE, DFF, DD], BF16)
    aTd = dint("aTd", [9, 16], F32)
    pfm = dint("pfm", [NE, NCH, 128], BF16)
    tens = dict(x=x, out=out, xnf=xnf, qf=qf, kf=kf, vt=vt, sS=sS,
                gwb=gwb, uwb=uwb, dwb=dwb, pfm=pfm)

    with ExitStack() as pctx:
        def psb(name, shape, dt=F32):
            return pctx.enter_context(nc.sbuf_tensor(name, shape, dt))

        P = {}
        for name, shape, dt in [
                ("ones_r", [1, 144], F32), ("sidF", [128, 128], F32),
                ("sidB", [128, 128], BF16), ("A_bc", [128, 144], F32),
                ("A_bcb", [128, 144], BF16),
                ("dvls", [128, 16], F32), ("dms", [128, 2048], BF16),
                ("WoTb", [128, 8, DD], BF16), ("probs_res", [128, NCH, NE], F32),
                ("pfm_st", [8, 128], BF16), ("gbs", [128, 128], F32),
                ("ubs", [128, 128], F32), ("dbs", [8, DD], BF16),
                ("nws", [128, 1], F32)]:
            P[name] = psb(name, shape, dt)

        S = {}
        for name in ["sC0", "sFW", "sAW", "gPE", "gV", "sEPC", "sVAR", "sVC",
                     "sX0", "sX1", "sST", "sSQ", "sNRM", "sXP", "sXE", "sXC",
                     "sRT", "sPR", "sPTR", "sPFM", "sWST", "sGD",
                     "sQM", "sQA", "sQV", "sQEq", "sQEv", "sQTP", "sQTE",
                     "sSPV0", "sSPV1", "sSPx", "sSPq", "sSPk", "sPFC", "fPF",
                     "sEC0", "sEC1", "sEC2",
                     "aQ", "aK0", "aK1", "aV0", "aV1", "aSIM", "aEV", "aPQ",
                     "aPHI", "aAV", "aEPI", "aEPV", "aEPA", "aTP", "aTE",
                     "aWO", "aWOE", "aSP0", "aSP1", "aSPD",
                     "fXN0", "fXN1", "fG0", "fG1", "fU0", "fU1", "fD", "fPG", "fPU", "fSIL",
                     "fH", "fPY", "fYE", "fBI", "fBIC", "fXSx0", "fXSx1", "fXSs0", "fXSs1", "fGT", "cACT", "cPOOL", "cDVE",
                     "fCT", "fOUT0", "fOUT1", "dC"]:
            S[name] = Sem(nc, pctx, name)

        consts = dict(identF=identF, identB=identB, rmat=rmat, kcol=kcol,
                      dvl=dvl, dmc=dmc, gbl=gbl, ubl=ubl, db=dbi, nw2=nw2,
                      polyT=polyT, hdrow=hdrow, rwT=rwT,
                      wqT=wqT, wkT=wkT, wvT=wvT, woT=woT,
                      gwi=gwi, uwi=uwi, dwi=dwi, aTd=aTd)
        meta = build_phase01(nc, S, P, tens, consts, dbgt, debug, phases)
        if phases >= 2:
            build_phase23(nc, S, P, tens, meta, dbgt, debug, phases)

    return nc, dbgt


def build_phase01(nc, S, P, tens, C, dbgt, debug, phases):
    x, xnf, qf, kf, vt = tens["x"], tens["xnf"], tens["qf"], tens["kf"], tens["vt"]
    with ExitStack() as c1:
        def sb1(name, shape, dt=F32):
            return c1.enter_context(nc.sbuf_tensor(name, shape, dt))

        def ps1(name, shape, dt=F32):
            return c1.enter_context(nc.psum_tensor(name, shape, dt))

        WqTb = sb1("WqTb", [128, 8, DD], BF16)
        WkTb = sb1("WkTb", [128, 8, DD], BF16)
        WvTb = sb1("WvTb", [128, 8, DD], BF16)
        wstage = sb1("wstage", [128, DD])
        rwTs = sb1("rwTs", [128, 8, NE])
        ptys = sb1("ptys", [9, 16]); hds = sb1("hds", [1, 16])
        rms = sb1("rms", [9, 9]); kcs = sb1("kcs", [9, 1])
        gats = sb1("gats", [9, 16]); wgs = sb1("wgs", [9, 16])
        aTs = sb1("aTs", [9, 16]); arow = sb1("arow", [1, 144])
        sx = [sb1(f"sx{i}", [128, DD]) for i in range(2)]
        lns = sb1("lns", [128, 12])
        sqh = [sb1(f"sqh{i}", [128, 512]) for i in range(2)]
        sxn = [sb1(f"sxn{i}", [128, DD]) for i in range(2)]
        xnfm = [sb1(f"xnfm{i}", [128, 8, 128]) for i in range(2)]
        xnstage = sb1("xnstage", [128, 8, 1024], BF16)
        qstage = sb1("qstage", [128, 8, 1024], BF16)
        kstage = sb1("kstage", [128, 8, 1024], BF16)
        qtb = sb1("qtb", [128, DD], BF16)
        ktb = sb1("ktb", [128, DD], BF16)
        vtb = [sb1(f"vtb{i}", [128, DD], BF16) for i in range(2)]
        qks = sb1("qks", [128, 32])
        prw = sb1("prw", [128, 24])

        ptr = ps1("ptr", [128, 8, 128])
        pqk = [ps1(f"pqk{i}", [128, 512]) for i in range(2)]
        psm = ps1("psm", [128, 160])
        pqb = ps1("pqb", [128, 16, 128], BF16)

        eb = EB()

        for dst, src in [(P["sidF"][:], C["identF"]), (P["sidB"][:], C["identB"]),
                         (P["dvls"][:], C["dvl"]), (P["dms"][:], C["dmc"]),
                         (P["gbs"][:], C["gbl"]), (P["ubs"][:], C["ubl"]),
                         (P["nws"][:], C["nw2"]),
                         (ptys[:], C["polyT"]), (hds[:], C["hdrow"]),
                         (rms[:], C["rmat"]), (kcs[:], C["kcol"]),
                         (rwTs[:],
                          C["rwT"].rearrange("(c p) e -> p c e", p=128))]:
            eb.op("sync", lambda E, d=dst, s=src: E.dma_start(d, s),
                  waits=[(S["sC0"], S["sC0"].n)], incs=[(S["sC0"], 16)])
        C0 = S["sC0"].n

        for e in range(NE):
            for dsrc, ddst in ((C["gwi"], tens["gwb"]), (C["uwi"], tens["uwb"]),
                               (C["dwi"], tens["dwb"])):
                eb.op("gpsimd", lambda E, a=ddst[e], b=dsrc[e]: E.dma_start(a, b),
                      waits=[(S["sFW"], S["sFW"].n)], incs=[(S["sFW"], 16)])

        eb.op("gpsimd", lambda E: E.dma_start(P["dbs"][:], C["db"]),
              waits=[(S["sAW"], S["sAW"].n)], incs=[(S["sAW"], 16)])
        for dst, src in ((WqTb, C["wqT"]), (WkTb, C["wkT"]), (WvTb, C["wvT"])):
            eb.op("gpsimd", lambda E, d=dst, s=src: E.dma_start(
                d[:], s.rearrange("(c p) f -> p c f", p=128)),
                waits=[(S["sAW"], S["sAW"].n)], incs=[(S["sAW"], 16)])
        CASTW = S["sAW"].n

        for jc in range(8):
            wl = [] if jc == 0 else [(S["gV"], jc)]
            eb.op("sync", lambda E, jc=jc: E.dma_start(
                wstage[:], C["woT"][jc * 128:(jc + 1) * 128, :]),
                waits=wl, incs=[(S["sWST"], 16)])
            eb.op("vector", lambda E, jc=jc: E.tensor_scalar_mul(
                P["WoTb"][:, jc, :], wstage[:], P["nws"][:, 0:1]),
                waits=[(S["sWST"], (jc + 1) * 16), (S["sC0"], C0)],
                incs=[(S["gV"], 1)])
        C1 = S["sC0"].n

        eb.op("vector", lambda E: E.memset(P["ones_r"][:], 1.0), incs=[(S["gV"], 1)])
        gv0 = S["gV"].n
        eb.op("tensor", lambda E: E.matmul(psm[0:9, 0:16], P["ones_r"][0:1, 0:9],
                                           hds[:], start=True, stop=True),
              waits=[(S["sC0"], C0), (S["gV"], gv0)], incs=[(S["gPE"], 1)])
        vc0 = S["sVC"].n

        def gchain(E, vc0=vc0):
            E.wait_ge(S["gPE"].h, 1)
            go = small_chain(E, S["sVC"], vc0)
            go(lambda: E.tensor_scalar(gats[:], psm[0:9, 0:16], kcs[:, 0:1], 1.0,
                                       ALU.subtract, ALU.add))
            go(lambda: E.tensor_scalar(gats[:], gats[:], 0.0, 1.0, ALU.max, ALU.min))
            i = go(lambda: E.tensor_tensor(wgs[:], ptys[:], gats[:], ALU.mult),
                   inc=False)
            i.then_inc(S["gV"].h, 3)
        eb.q["vector"].append(gchain)
        S["sVC"].n += 2
        S["gV"].n += 3
        eb.op("tensor", lambda E: E.matmul(psm[0:9, 16:32], rms[:], wgs[:],
                                           start=True, stop=True),
              waits=[(S["gV"], gv0 + 3)], incs=[(S["gPE"], 1)])
        eb.op("vector", lambda E: E.tensor_copy(aTs[:], psm[0:9, 16:32]),
              waits=[(S["gPE"], 2)], incs=[(S["gV"], 1)])
        eb.op("sync", lambda E: E.dma_start(C["aTd"], aTs[:]),
              waits=[(S["gV"], gv0 + 4)], incs=[(S["sGD"], 16)])
        eb.op("sync", lambda E: E.dma_start(
            arow[:].rearrange("a (h j) -> a h j", j=9),
            C["aTd"].transpose([1, 0]).unsqueeze(0),
            allow_slow_non_contiguous=True),
            waits=[(S["sGD"], 16)], incs=[(S["sGD"], 16)])
        eb.op("tensor", lambda E: E.matmul(psm[:, 16:160], P["ones_r"][0:1, 0:128],
                                           arow[:], start=True, stop=True),
              waits=[(S["sGD"], 32), (S["gV"], gv0)], incs=[(S["gPE"], 1)])
        eb.op("vector", lambda E: E.tensor_copy(P["A_bc"][:], psm[:, 16:160]),
              waits=[(S["gPE"], 3)], incs=[(S["gV"], 1)])
        eb.op("vector", lambda E: E.tensor_copy(P["A_bcb"][:], psm[:, 16:160]),
              incs=[(S["gV"], 1)])

        deltas0 = None
        spill_after = {}
        for ct in range(NCH_L):
            b = ct % 2
            tile, sl = ct // 8, ct % 8
            base = {k: S[k].n for k in S}
            epw = [(S["sEC0"], max(0, ct - 1)), (S["sEC1"], max(0, ct - 1)),
                   (S["sEC2"], max(0, ct - 1))]

            sxp = S["sX0"] if b == 0 else S["sX1"]
            eb.op("sync", lambda E, ct=ct, b=b: E.dma_start(
                sx[b][:], x[ct * 128:(ct + 1) * 128, :]),
                waits=[(S["sST"], max(0, ct - 1))], incs=[(sxp, 16)])

            eb.op("scalar", lambda E, b=b: E.activation(sqh[0][:], sx[b][:, 0:512],
                                                        AF.Square),
                  waits=epw + [(sxp, 16 * (ct // 2 + 1)),
                               (S["sQV"], max(0, 5 * ct))],
                  incs=[(S["sSQ"], 1)])
            eb.op("scalar", lambda E, b=b: E.activation(sqh[1][:], sx[b][:, 512:1024],
                                                        AF.Square),
                  incs=[(S["sSQ"], 1)])

            vcs = S["sVC"].n

            def ln_stats(E, ct=ct, b=b, epv=epw[0][1], vcs=vcs, sxp=sxp):
                E.wait_ge(S["sEC0"].h, epv)
                E.wait_ge(S["sEC1"].h, epv)
                E.wait_ge(S["sEC2"].h, epv)
                E.wait_ge(sxp.h, 16 * (ct // 2 + 1))
                E.wait_ge(S["sSQ"].h, 2 * (ct + 1))
                go = small_chain(E, S["sVC"], vcs)
                E.reduce_sum(lns[:, 0:1], sx[b][:], AX.X)
                E.reduce_sum(lns[:, 1:2], sqh[0][:], AX.X)
                go(lambda: E.reduce_sum(lns[:, 2:3], sqh[1][:], AX.X))
                go(lambda: E.tensor_scalar_mul(lns[:, 3:4], lns[:, 0:1], 1.0 / DD))
                go(lambda: E.tensor_add(lns[:, 4:5], lns[:, 1:2], lns[:, 2:3]))
                go(lambda: E.tensor_scalar_mul(lns[:, 5:6], lns[:, 4:5], 1.0 / DD))
                go(lambda: E.tensor_mul(lns[:, 6:7], lns[:, 3:4], lns[:, 3:4]))
                go(lambda: E.scalar_tensor_tensor(lns[:, 7:8], lns[:, 6:7], -1.0,
                                                  lns[:, 5:6], ALU.mult, ALU.add))
                i = go(lambda: E.tensor_scalar_add(lns[:, 8:9], lns[:, 7:8], 1e-5),
                       inc=False)
                i.then_inc(S["sVAR"].h, 1)
            eb.q["vector"].append(ln_stats)
            S["sVC"].n += 6
            S["sVAR"].n += 1
            eb.op("scalar", lambda E: E.activation(lns[:, 9:10], lns[:, 8:9], AF.Ln),
                  waits=[(S["sVAR"], ct + 1)], incs=[(S["cACT"], 1)])
            eb.op("scalar", lambda E: E.activation(lns[:, 10:11], lns[:, 9:10],
                                                   AF.Exp, scale=-0.5),
                  waits=[(S["cACT"], S["cACT"].n)], incs=[(S["sQA"], 1)])
            qa_rstd = S["sQA"].n

            vcn = S["sVC"].n

            def ln_nbias(E, b=b, qa_rstd=qa_rstd):
                E.wait_ge(S["sQA"].h, qa_rstd)
                i = E.scalar_tensor_tensor(lns[:, 11:12], lns[:, 3:4], -1.0,
                                           lns[:, 10:11], ALU.mult, ALU.mult)
                i.then_inc(S["sQV"].h, 1)
            eb.q["vector"].append(ln_nbias)
            S["sQV"].n += 1
            nb_now = S["sQV"].n
            eb.op("scalar", lambda E, b=b: E.activation(
                sxn[b][:], sx[b][:], AF.Identity, bias=lns[:, 11:12],
                scale=lns[:, 10:11]),
                waits=[(S["sQV"], nb_now), (S["cACT"], S["cACT"].n)],
                incs=[(S["sST"], 1)])
            S["sNRM"].n += 1
            if debug:
                eb.op("scalar", lambda E, ct=ct: E.dma_start(
                    dbgt["d_lns"][ct], lns[:]),
                    waits=[(S["sST"], ct + 1)], incs=[(S["dC"], 16)])

            rmap = lambda j: (j % 2) * 4 + j // 2
            for dc in range(8):
                wl = []
                if dc == 0:
                    wl = epw + [(S["sST"], ct + 1), (S["sXE"], max(0, ct * 8))]
                elif dc >= 2:
                    wl = [(S["sXE"], ct * 8 + dc - 1)]
                eb.op("tensor", lambda E, b=b, dc=dc, r=rmap(dc): E.transpose(
                    ptr[:, r, :], sxn[b][:, dc * 128:(dc + 1) * 128], P["sidF"][:]),
                    waits=wl, incs=[(S["sXP"], 1)])
                eb.op("scalar", lambda E, b=b, dc=dc, r=rmap(dc): E.activation(
                    xnfm[b][:, dc, :], ptr[:, r, :], AF.Identity),
                    waits=[(S["sXP"], ct * 8 + dc + 1)] +
                          ([(S["sXC"], max(0, ct - 1))] if dc == 0 else []),
                    incs=[(S["sXE"], 1)])
            eb.op("vector", lambda E, b=b, sl=sl: E.tensor_copy(
                xnstage[:, :, sl * 128:(sl + 1) * 128], xnfm[b][:]),
                waits=[(S["sXE"], (ct + 1) * 8),
                       (S["sSPx"], spill_after.get(tile - 1, 0) if sl == 0 else 0),
                       (S["sQM"], max(0, (ct - 7) * 6))],
                incs=[(S["sXC"], 1)])

            for dc in range(8):
                eb.op("tensor", lambda E, b=b, dc=dc: E.matmul(
                    psm[:, 0:8], xnfm[b][:, dc, :], rwTs[:, dc, :],
                    start=(dc == 0), stop=(dc == 7)),
                    waits=([(S["sXE"], ct * 8 + 8), (S["sPFC"], max(0, ct))]
                           if dc == 0 else []),
                    incs=([(S["sRT"], 1)] if dc == 7 else []))

            vpa = S["sVC"].n

            def probs_a(E, ct=ct, vpa=vpa):
                E.wait_ge(S["sRT"].h, ct + 1)
                go = small_chain(E, S["sVC"], vpa)
                E.tensor_copy(prw[:, 16:24], psm[:, 0:8])
                go(lambda: E.reduce_max(prw[:, 8:9], psm[:, 0:8], AX.X))
                i = go(lambda: E.tensor_scalar_mul(prw[:, 9:10], prw[:, 8:9], -1.0),
                       inc=False)
                i.then_inc(S["sPR"].h, 1)
            eb.q["vector"].append(probs_a)
            S["sVC"].n += 1
            S["sPR"].n += 1
            eb.op("scalar", lambda E: E.activation(prw[:, 0:8], prw[:, 16:24],
                                                   AF.Exp, bias=prw[:, 9:10]),
                  waits=[(S["sPR"], base["sPR"] + 1)], incs=[(S["sQA"], 1)])
            qa_pe = S["sQA"].n

            vpb = S["sVC"].n

            def probs_b(E, ct=ct, qa=qa_pe, vpb=vpb):
                E.wait_ge(S["sQA"].h, qa)
                go = small_chain(E, S["sVC"], vpb)
                go(lambda: E.reduce_sum(prw[:, 8:9], prw[:, 0:8], AX.X))
                go(lambda: E.reciprocal(prw[:, 9:10], prw[:, 8:9]))
                go(lambda: E.tensor_scalar_mul(prw[:, 0:8], prw[:, 0:8],
                                               prw[:, 9:10]))
                go(lambda: E.reduce_max(prw[:, 8:9], prw[:, 0:8], AX.X))
                go(lambda: E.tensor_scalar(prw[:, 10:18], prw[:, 0:8], prw[:, 8:9],
                                           None, ALU.is_ge))
                go(lambda: E.scalar_tensor_tensor(prw[:, 10:18], prw[:, 10:18],
                                                  -1e30, prw[:, 0:8],
                                                  ALU.mult, ALU.add))
                go(lambda: E.reduce_max(prw[:, 8:9], prw[:, 10:18], AX.X))
                go(lambda: E.tensor_scalar(prw[:, 10:18], prw[:, 0:8], prw[:, 8:9],
                                           None, ALU.is_ge))
                go(lambda: E.tensor_tensor(prw[:, 0:8], prw[:, 0:8], prw[:, 10:18],
                                           ALU.mult))
                go(lambda: E.reduce_sum(prw[:, 8:9], prw[:, 0:8], AX.X))
                go(lambda: E.tensor_scalar_add(prw[:, 8:9], prw[:, 8:9], 1e-9))
                go(lambda: E.reciprocal(prw[:, 9:10], prw[:, 8:9]))
                i = go(lambda: E.tensor_scalar_mul(P["probs_res"][:, ct, :],
                                                   prw[:, 0:8], prw[:, 9:10]),
                       inc=False)
                i.then_inc(S["sPR"].h, 1)
            eb.q["vector"].append(probs_b)
            S["sVC"].n += 12
            S["sPR"].n += 1
            eb.op("tensor", lambda E, ct=ct: E.transpose(
                psm[0:8, 16:144], P["probs_res"][:, ct, :], P["sidF"][:]),
                waits=[(S["sPR"], base["sPR"] + 2), (S["sPFC"], max(0, ct))],
                incs=[(S["sPTR"], 1)])
            eb.op("vector", lambda E, ct=ct: E.tensor_copy(
                P["pfm_st"][:], psm[0:8, 16:144]),
                waits=[(S["sPTR"], ct + 1), (S["sPFM"], max(0, 16 * ct))],
                incs=[(S["sPFC"], 1)])
            eb.op("scalar", lambda E, ct=ct: E.dma_start(
                tens["pfm"][:, ct, :], P["pfm_st"][:]),
                waits=[(S["sPFC"], ct + 1)], incs=[(S["sPFM"], 16)])

            qv_marks = []
            for mi, (Wb, kind) in enumerate(((WqTb, "q"), (WkTb, "k"), (WvTb, "v"))):
                for hf in range(2):
                    idx = 6 * ct + 2 * mi + hf
                    lidx = 2 * mi + hf
                    bank = idx % 2
                    for dc in range(8):
                        wl = []
                        if dc == 0:
                            wl = [(S["sXC"], ct + 1), (S["sAW"], CASTW)]
                            li2 = lidx - 2
                            if li2 >= -2 and idx >= 2:
                                if li2 >= 0:
                                    if li2 < 4:
                                        wl.append((S["sQEq"], 4 * ct + li2 + 1))
                                    else:
                                        wl.append((S["sQEv"], 2 * ct + li2 - 3))
                                else:
                                    wl.append((S["sQEv"], 2 * (ct - 1) + li2 + 3))
                        eb.op("tensor", lambda E, sl=sl, dc=dc, Wb=Wb, hf=hf,
                              bank=bank: E.matmul(
                                  pqk[bank][:],
                                  xnstage[:, dc, sl * 128:(sl + 1) * 128],
                                  Wb[:, dc, hf * 512:(hf + 1) * 512],
                                  start=(dc == 0), stop=(dc == 7)),
                              waits=wl, incs=([(S["sQM"], 1)] if dc == 7 else []))
                    mmn = S["sQM"].n
                    if kind == "v":
                        spv = S["sSPV0"] if b == 0 else S["sSPV1"]
                        eb.op("scalar", lambda E, bank=bank, hf=hf, b=b:
                              E.activation(
                                  vtb[b][:, hf * 512:(hf + 1) * 512],
                                  pqk[bank][:], AF.Identity),
                              waits=[(S["sQM"], mmn), (spv, 16 * (ct // 2))],
                              incs=[(S["sQEv"], 1)])
                    else:
                        dsttb = qtb if kind == "q" else ktb
                        sqb = sqh[lidx % 2]
                        sqw = ([(S["sST"], ct + 1)] if lidx < 2
                               else [(S["sQV"], qv_marks[lidx - 2])])
                        eb.op("scalar", lambda E, bank=bank, sqb=sqb: E.activation(
                            sqb[:], pqk[bank][:], AF.Square),
                            waits=[(S["sQM"], mmn)] + sqw, incs=[(S["sQA"], 1)])
                        qa1 = S["sQA"].n

                        vqs = S["sVC"].n

                        def qk_ss(E, sqb=sqb, qa1=qa1, vqs=vqs):
                            E.wait_ge(S["sQA"].h, qa1)
                            go = small_chain(E, S["sVC"], vqs)
                            go(lambda: E.reduce_sum(
                                qks[:, 0:8].unsqueeze(2),
                                sqb[:].rearrange("p (h d) -> p h d", h=8), AX.X))
                            i = go(lambda: E.tensor_scalar(qks[:, 8:16], qks[:, 0:8],
                                                           1e-24, None, ALU.max),
                                   inc=False)
                            i.then_inc(S["sQV"].h, 1)
                        eb.q["vector"].append(qk_ss)
                        S["sVC"].n += 1
                        S["sQV"].n += 1
                        qv_marks.append(S["sQV"].n)
                        eb.op("scalar", lambda E: E.activation(
                            qks[:, 16:24], qks[:, 8:16], AF.Ln),
                            waits=[(S["sQV"], S["sQV"].n)], incs=[(S["cACT"], 1)])
                        eb.op("scalar", lambda E: E.activation(
                            qks[:, 16:24], qks[:, 16:24], AF.Exp, scale=-0.5),
                            waits=[(S["cACT"], S["cACT"].n)], incs=[(S["sQA"], 1)])
                        qa2 = S["sQA"].n
                        eb.op("vector", lambda E, bank=bank, hf=hf, dsttb=dsttb:
                              E.tensor_tensor(
                                  dsttb[:, hf * 512:(hf + 1) * 512].rearrange(
                                      "p (h d) -> p h d", h=8),
                                  pqk[bank][:].rearrange("p (h d) -> p h d", h=8),
                                  qks[:, 16:24].unsqueeze(2).broadcast_to(
                                      [128, 8, 64]),
                                  ALU.mult),
                              waits=[(S["sQA"], qa2)], incs=[(S["sQEq"], 1)])
                if kind == "v":
                    pass

            qe_now = S["sQEq"].n
            rmap16 = lambda j: (j % 2) * 8 + j // 2
            for ti, (srctb, stage) in enumerate(((qtb, qstage), (ktb, kstage))):
                for dc in range(8):
                    j = ti * 8 + dc
                    wl = [(S["sQEq"], qe_now)] if j == 0 else []
                    if j >= 2:
                        wl.append((S["sQTE"], ct * 16 + j - 1))
                    eb.op("tensor", lambda E, srctb=srctb, dc=dc, r=rmap16(ti * 8 + dc):
                          E.transpose(pqb[:, r, :],
                                      srctb[:, dc * 128:(dc + 1) * 128], P["sidB"][:]),
                          waits=wl, incs=[(S["sQTP"], 1)])
                    wl2 = [(S["sQTP"], ct * 16 + j + 1)]
                    if dc == 0 and sl == 0:
                        wl2.append(((S["sSPq"] if ti == 0 else S["sSPk"]),
                                    spill_after.get(tile - 1, 0)))
                    eb.op("scalar", lambda E, stage=stage, dc=dc, sl=sl,
                          r=rmap16(ti * 8 + dc):
                          E.activation(stage[:, dc, sl * 128:(sl + 1) * 128],
                                       pqb[:, r, :], AF.Identity),
                          waits=wl2, incs=[(S["sQTE"], 1)])
            spv = S["sSPV0"] if b == 0 else S["sSPV1"]
            eb.op("scalar", lambda E, ct=ct, b=b: E.dma_start(
                vt[ct * 128:(ct + 1) * 128, :], vtb[b][:]),
                waits=[(S["sQEv"], 2 * (ct + 1))], incs=[(spv, 16)])

            if sl == 7:
                te_now = S["sQTE"].n
                xc_now = S["sXC"].n
                for stage, dst, ssem in ((xnstage, xnf, S["sSPx"]),
                                         (qstage, qf, S["sSPq"]),
                                         (kstage, kf, S["sSPk"])):
                    wl = ([(S["sXC"], xc_now)] if stage is xnstage
                          else [(S["sQTE"], te_now)])
                    eb.op("scalar", lambda E, stage=stage, dst=dst, tile=tile:
                          E.dma_start(
                              dst.rearrange("c p t -> p c t")[
                                  :, :, tile * 1024:(tile + 1) * 1024],
                              stage[:]),
                          waits=wl, incs=[(ssem, 16)])
                spill_after[tile] = 16 * (tile + 1)

            eb.op("vector", lambda E: E.tensor_copy(qks[:, 24:25], qks[:, 0:1]),
                  incs=[(S["sEC0"], 1)])
            eb.op("scalar", lambda E: E.copy(qks[:, 25:26], qks[:, 0:1]),
                  incs=[(S["sEC1"], 1)])
            eb.op("tensor", lambda E: E.transpose(
                pqb[:, 15, :], ktb[:, 896:1024], P["sidB"][:]),
                waits=[(S["sQTE"], ct * 16 + 16)], incs=[(S["sEC2"], 1)])

            d = {}
            for k2 in S:
                if S[k2].n != base[k2]:
                    key = k2.rstrip("01") if k2[-1] in "01" else k2
                    d[key] = d.get(key, 0) + S[k2].n - base[k2]
            if deltas0 is None and sl not in (0, 7):
                deltas0 = d
            elif sl not in (0, 7) and deltas0 is not None:
                assert d == deltas0, f"chunk {ct} deltas {d} != {deltas0}"

        meta = dict(SPx=S["sSPx"].n, SPq=S["sSPq"].n, SPk=S["sSPk"].n,
                    SPV0=S["sSPV0"].n, SPV1=S["sSPV1"].n, CASTW=CASTW)

        if debug:
            eb.op("scalar", lambda E: E.dma_start(dbgt["d_probs"], P["probs_res"][:]),
                  waits=[(S["sPR"], S["sPR"].n)], incs=[(S["dC"], 16)])
        if phases == 1:
            dcw = S["dC"].n
            fins = [(S["sSPx"], S["sSPx"].n), (S["sSPq"], S["sSPq"].n),
                    (S["sSPk"], S["sSPk"].n), (S["sSPV0"], S["sSPV0"].n),
                    (S["sSPV1"], S["sSPV1"].n), (S["sFW"], S["sFW"].n),
                    (S["dC"], dcw), (S["sPFM"], S["sPFM"].n)]

            def fin2(E, fins=fins):
                for s, v in fins:
                    if v:
                        E.wait_ge(s.h, v)
            eb.q["sync"].append(fin2)
        eb.emit(nc)
    return meta


def build_phase23(nc, S, P, tens, meta, dbgt, debug, phases):
    x, out, sS = tens["x"], tens["out"], tens["sS"]
    qf, kf, vt = tens["qf"], tens["kf"], tens["vt"]
    xnf, gwb, uwb, dwb = tens["xnf"], tens["gwb"], tens["uwb"], tens["dwb"]
    P1B = [("sSPx", meta["SPx"]), ("sSPq", meta["SPq"]), ("sSPk", meta["SPk"]),
           ("sSPV0", meta["SPV0"]), ("sSPV1", meta["SPV1"])]
    with ExitStack() as c2:
        def sb2(name, shape, dt=F32):
            return c2.enter_context(nc.sbuf_tensor(name, shape, dt))

        def ps2(name, shape, dt=F32):
            return c2.enter_context(nc.psum_tensor(name, shape, dt))

        qtl = [sb2("qtl0", [128, 8, 512], BF16)] * 2
        ksp = [sb2(f"ksp{i}", [128, 8, 256], BF16) for i in range(2)]
        vsp = [sb2(f"vsp{i}", [128, 2, 1024], BF16) for i in range(2)]
        st_s = [sb2(f"st_s{i}", [128, 2, 512], BF16) for i in range(4)]
        st_z = [sb2(f"st_z{i}", [128, 2, 512], BF16) for i in range(4)]
        stp2 = [sb2(f"st_p{i}", [128, 1024], BF16) for i in range(2)]
        st_q = [sb2(f"st_q{i}", [128, 1024], BF16) for i in range(2)]
        st_f = [sb2(f"st_f{i}", [128, 2, 512], BF16) for i in range(3)]
        ao_hq = sb2("ao_hq", [128, 4, 4, 64])
        ao_fm = sb2("ao_fm", [128, 8, 512], BF16)
        scr1 = sb2("scr1", [128, 1024])
        est = sb2("est", [128, 64])
        xtile = [sb2(f"xtile{i}", [128, 8, 512], BF16) for i in range(2)]
        gch = [sb2(f"gch{i}", [128, 8, 256], BF16) for i in range(2)]
        uch = [sb2(f"uch{i}", [128, 8, 256], BF16) for i in range(2)]
        Dsb = sb2("Dsb", [128, 16, 1024], BF16)
        hsb = sb2("hsb", [128, 16, 512], BF16)
        y_acc = sb2("y_acc", [128, 4, 1024])
        hs_scr = [sb2(f"hs_scr{i}", [128, 512]) for i in range(2)]
        pfm_sb = sb2("pfm_sb", [8, 512], BF16)
        xts = [sb2(f"xts{i}", [128, 512]) for i in range(2)]
        sts = [sb2(f"sts{i}", [128, 512]) for i in range(2)]

        psim = [ps2(f"psim{i}", [128, 512]) for i in range(2)]
        pav = ps2("pav", [128, 1024])
        pg = ps2("pg", [128, 512])
        pu = ps2("pu", [128, 512])
        py = [ps2(f"py{i}", [128, 512]) for i in range(2)]

        eb = EB()
        A = S

        mk = dict(sim_slice={}, woe_slice={}, sp_slice={}, kv={}, te=0, tp=0,
                  epa=0, epv=0, epi=0, wo_slice={}, ffn_mm_tt={}, ye_tt={},
                  ct_tt={}, gt_tt={}, fout_tt={})
        phi_ctr = [0]
        kv_ctr = [0]
        pend_av = []

        def co(h, j):
            return P["A_bc"][:, h * 9 + j:h * 9 + j + 1]

        def coB(h, j):
            return P["A_bcb"][:, h * 9 + j:h * 9 + j + 1].broadcast_to([128, 1024])

        def emit_av(pidx, k, qt, sp, hh, h, sb_, kvb, first_q, nsp):
            sb_ = pidx % 3
            wl = [(A["aPHI"], pidx + 1)]
            if first_q:
                wl.append((A["aTE"], mk["te"]))
                wl.append((A["aEPI"], mk["epi"]))
            valid = [(kcw, ts) for kcw in range(2) for ts in range(4)
                     if 2 * sp + kcw <= 4 * qt + ts]
            for n_i, (kcw, ts) in enumerate(valid):
                st = (sp == 0 and kcw == 0 and ts == 0 and hh in (0, 2))
                stp = (2 * sp + kcw == 4 * qt + ts)
                eb.op("tensor", lambda E, kcw=kcw, ts=ts, hh=hh, h=h, sb_=sb_,
                      kvb=kvb, st=st, stp=stp: E.matmul(
                          pav[:, hh * 256 + ts * 64:hh * 256 + (ts + 1) * 64],
                          st_f[sb_][:, kcw, ts * 128:(ts + 1) * 128],
                          vsp[kvb][:, kcw, h * 64:(h + 1) * 64],
                          start=st, stop=stp, skip_group_check=True),
                      waits=(wl if n_i == 0 else []),
                      incs=([(A["aAV"], 1)] if n_i == len(valid) - 1 else []))

        def attn_slice(k):
            seq, qt = divmod(k, 4)
            nsp = 2 * (qt + 1)
            t0 = k * 512
            qb = k % 2
            wl = [(S[n], v) for n, v in P1B]
            if k >= 1:
                wl.append((A["aSIM"], mk["sim_slice"][k - 1]))
            eb.op("sync", lambda E, qb=qb, t0=t0: E.dma_start(
                qtl[qb][:], qf.rearrange("c p t -> p c t")[:, :, t0:t0 + 512]),
                waits=wl, incs=[(A["aQ"], 16)])
            aQ_k = A["aQ"].n
            yield 1.0
            def kv_load(sp):
                kvi = kv_ctr[0]
                kv_ctr[0] += 1
                kvb = kvi % 2
                tk0 = seq * 2048 + sp * 256
                sK = A["aK0"] if kvb == 0 else A["aK1"]
                sV = A["aV0"] if kvb == 0 else A["aV1"]
                wlk = [(S[n], v) for n, v in P1B]
                wlv = []
                if kvi >= 2:
                    simm, avm = mk["kv"][kvi - 2]
                    wlk.append((A["aSIM"], simm))
                    wlv.append((A["aAV"], avm))
                    while A["aAV"].n < avm and pend_av:
                        pend_av.pop(0)()
                eb.op("sync", lambda E, kvb=kvb, tk0=tk0: E.dma_start(
                    ksp[kvb][:], kf.rearrange("c p t -> p c t")[:, :, tk0:tk0 + 256]),
                    waits=wlk, incs=[(sK, 16)])
                eb.op("sync", lambda E, kvb=kvb, tk0=tk0: E.dma_start(
                    vsp[kvb][:],
                    vt.rearrange("(a w p) d -> a p w d", p=128, w=2)[tk0 // 256]),
                    waits=wlv, incs=[(sV, 16)])
                return kvi, kvb, (sK.n, sV.n, sK, sV)

            CP, CV, CA = S["cPOOL"], S["cDVE"], S["cACT"]

            def emit_prep(hq, sp, hpair, kvi, kvb, kvn, diag):
                CH = []
                for cc in range(2):
                    hh = hpair + cc
                    h = hq * 4 + hh
                    pidx = phi_ctr[0]
                    phi_ctr[0] += 1
                    CH.append(dict(hh=hh, h=h, pidx=pidx, sb=pidx % 4,
                                   c=h // 2, par=(h % 2) * 64))
                for ch in CH:
                    pidx, sb_ = ch["pidx"], ch["sb"]
                    for kcw in range(2):
                        ei = 2 * pidx + kcw
                        bank = ei % 2
                        wl = []
                        if kcw == 0:
                            wl = [(kvn[2], kvn[0]), (kvn[3], kvn[1]),
                                  (A["aQ"], aQ_k)]
                        if ei >= 2:
                            wl.append((A["aEV"], 2 * (ei - 2) + 2))
                        eb.op("tensor", lambda E, c=ch["c"], par=ch["par"],
                              kcw=kcw, bank=bank, kvb=kvb, qb=qb: E.matmul(
                                  psim[bank][:],
                                  ksp[kvb][par:par + 64, c,
                                           kcw * 128:(kcw + 1) * 128],
                                  qtl[qb][par:par + 64, c, :],
                                  start=True, stop=True),
                              waits=wl, incs=[(A["aSIM"], 1)])
                        smn = A["aSIM"].n
                        wl2 = [(A["aSIM"], smn)]
                        if pidx >= 4:
                            wl2.append((A["aPHI"], 2 * (pidx // 2) - 2))
                        eb.op("scalar", lambda E, bank=bank, sb_=sb_,
                              kcw=kcw: E.activation(
                                  st_s[sb_][:, kcw, :], psim[bank][:],
                                  AF.Identity),
                              waits=wl2, incs=[(A["aEV"], 2)])
                    ch["evn"] = A["aEV"].n
                for ch in CH:
                    sb_, pidx = ch["sb"], ch["pidx"]
                    zc = st_z[sb_][:].rearrange("p w t -> p (w t)")
                    sc = st_s[sb_][:].rearrange("p w t -> p (w t)")
                    ch["zc"], ch["sc"] = zc, sc
                    ch["qs"] = st_q[pidx % 2][:]
                    eb.op("gpsimd", lambda E, zc=zc, sc=sc: E.tensor_tensor(
                        zc, sc, sc, ALU.mult),
                        waits=[(A["aEV"], ch["evn"]),
                               (A["aPHI"], max(0, 2 * (pidx // 2) - 2)),
                               (CP, CP.n)],
                        incs=[(CP, 1)])
                    ch["zn"] = CP.n
                return dict(CH=CH, kvi=kvi, kvb=kvb, diag=diag, sp=sp,
                            simn=A["aSIM"].n)

            def emit_chain(g, foq, k, qt, nsp):
                CH, kvb, diag, sp = g["CH"], g["kvb"], g["diag"], g["sp"]
                while len(pend_av) > 1:
                    pend_av.pop(0)()
                for ch in CH:
                    h, pidx = ch["h"], ch["pidx"]
                    pst = stp2[pidx % 2][:]
                    ch["pst"] = pst
                    eb.op("scalar", lambda E, zc=ch["zc"], h=h, pst=pst:
                          E.activation(pst, zc, AF.Identity,
                                       bias=co(h, 6), scale=co(h, 8)),
                          waits=[(CP, ch["zn"]),
                                 (A["aPHI"], max(0, pidx - 1)),
                                 (CA, max(0, CA.n - 1))],
                          incs=[(CA, 1)])
                    eb.op("scalar", lambda E, zc=ch["zc"], h=h,
                          qs=ch["qs"]: E.activation(
                              qs, zc, AF.Identity, bias=co(h, 5),
                              scale=co(h, 7)),
                          waits=[(CP, ch["zn"]),
                                 (A["aPHI"], max(0, pidx - 1)),
                                 (CA, max(0, CA.n - 1))],
                          incs=[(CA, 1)])
                    ch["initn"] = CA.n

                def dve(ch, f, waits=()):
                    eb.op("vector", f,
                          waits=list(waits) + [(CV, max(0, CV.n - 1))],
                          incs=[(CV, 1)])
                    return CV.n

                def act(ch, f, waits=()):
                    eb.op("scalar", f,
                          waits=list(waits) + [(CA, max(0, CA.n - 1))],
                          incs=[(CA, 1)])
                    return CA.n
                for ch in CH:
                    ch["m"] = dve(ch, lambda E, p=ch["pst"], zc=ch["zc"]:
                                  E.tensor_tensor(p, p, zc, ALU.mult),
                                  [(CA, ch["initn"])])
                for ch in CH:
                    bc4 = coB(ch["h"], 4)
                    eb.op("gpsimd", lambda E, p=ch["pst"], b=bc4:
                          E.tensor_tensor(p, p, b, ALU.add),
                          waits=[(CV, ch["m"]), (CP, max(0, CP.n - 1))],
                          incs=[(CP, 1)])
                    ch["mp"] = CP.n
                for ch in CH:
                    ch["mq"] = dve(ch, lambda E, q=ch["qs"], zc=ch["zc"]:
                                   E.tensor_tensor(q, q, zc, ALU.mult))
                for ch in CH:
                    ch["m"] = dve(ch, lambda E, p=ch["pst"], zc=ch["zc"]:
                                  E.tensor_tensor(p, p, zc, ALU.mult),
                                  [(CP, ch["mp"])])
                for ch in CH:
                    ch["aq"] = act(ch, lambda E, q=ch["qs"], h=ch["h"]:
                                   E.activation(q, q, AF.Identity,
                                                bias=co(h, 3)),
                                   [(CV, ch["mq"])])
                for ch in CH:
                    ch["ap"] = act(ch, lambda E, p=ch["pst"], h=ch["h"]:
                                   E.activation(p, p, AF.Identity,
                                                bias=co(h, 2)),
                                   [(CV, ch["m"])])
                for ch in CH:
                    ch["mq"] = dve(ch, lambda E, q=ch["qs"], zc=ch["zc"]:
                                   E.tensor_tensor(q, q, zc, ALU.mult),
                                   [(CA, ch["aq"])])
                for ch in CH:
                    ch["m"] = dve(ch, lambda E, p=ch["pst"], zc=ch["zc"]:
                                  E.tensor_tensor(p, p, zc, ALU.mult),
                                  [(CA, ch["ap"])])
                for ch in CH:
                    bc1 = coB(ch["h"], 1)
                    eb.op("gpsimd", lambda E, q=ch["qs"], b=bc1:
                          E.tensor_tensor(q, q, b, ALU.add),
                          waits=[(CV, ch["mq"]), (CP, max(0, CP.n - 1))],
                          incs=[(CP, 1)])
                    ch["mp"] = CP.n
                for ch in CH:
                    bc0 = coB(ch["h"], 0)
                    ch["m"] = dve(ch, lambda E, p=ch["pst"], b=bc0:
                                  E.tensor_tensor(p, p, b, ALU.add))
                for ch in CH:
                    ch["mq"] = dve(ch, lambda E, q=ch["qs"], sc=ch["sc"]:
                                   E.tensor_tensor(q, q, sc, ALU.mult),
                                   [(CP, ch["mp"])])
                for ch in CH:
                    pidx = ch["pidx"]
                    fcs = st_f[pidx % 3][:].rearrange("p w t -> p (w t)")
                    wl = [(CV, max(0, CV.n - 1))]
                    if pidx >= 3:
                        wl.append((A["aAV"], pidx - 2))
                    if diag:
                        eb.op("vector", lambda E, fcs=fcs, p=ch["pst"],
                              q=ch["qs"]: E.tensor_tensor(
                                  fcs, p, q, ALU.add),
                              waits=wl, incs=[(CV, 1)])
                        dmo = 0 if sp == nsp - 2 else 1024
                        eb.op("vector", lambda E, fcs=fcs, dmo=dmo:
                              E.tensor_tensor(
                                  fcs, fcs, P["dms"][:, dmo:dmo + 1024],
                                  ALU.mult),
                              waits=[(CV, CV.n)],
                              incs=[(A["aPHI"], 1)])
                    else:
                        eb.op("vector", lambda E, fcs=fcs, p=ch["pst"],
                              q=ch["qs"]: E.tensor_tensor(
                                  fcs, p, q, ALU.add),
                              waits=wl, incs=[(A["aPHI"], 1)])
                for ch in CH:
                    pend_av.append(
                        lambda pidx=ch["pidx"], k=k, qt=qt, sp=sp,
                        hh=ch["hh"], h=ch["h"], sb_=ch["sb"], kvb=kvb,
                        fq=foq[0], nsp=nsp:
                        emit_av(pidx, k, qt, sp, hh, h, sb_, kvb, fq,
                                nsp))
                    foq[0] = False
                mk["kv"][g["kvi"]] = (g["simn"],
                                      A["aAV"].n + len(pend_av))

            for hq in range(4):
                foq = [True]
                pend_kv = kv_load(0)
                pend_chain = None
                for sp in range(nsp):
                    kvi, kvb, kvn = pend_kv
                    diag = sp >= nsp - 2
                    for hpair in (0, 2):
                        g = emit_prep(hq, sp, hpair, kvi, kvb, kvn, diag)
                        if pend_chain is not None:
                            pend_chain()
                        pend_chain = (lambda g=g, foq=foq, k=k, qt=qt,
                                      nsp=nsp: emit_chain(g, foq, k, qt, nsp))
                        if hpair == 0:
                            yield 11.0
                    if sp + 1 < nsp:
                        pend_kv = kv_load(sp + 1)
                    yield 11.0
                if pend_chain is not None:
                    pend_chain()
                while pend_av:
                    pend_av.pop(0)()
                avq = A["aAV"].n
                epi_wl = [(A["aAV"], avq), (A["aEPA"], mk["epa"]),
                          (A["aTP"], mk["tp"])]
                for hi in range(4):
                    pv = pav[:, hi * 256:(hi + 1) * 256].rearrange(
                        "p (ts d) -> p ts d", ts=4)
                    dvb = P["dvls"][:, 4 * qt:4 * qt + 4].unsqueeze(2).broadcast_to(
                        [128, 4, 64])
                    eb.op("vector", lambda E, hi=hi, pv=pv, dvb=dvb: E.tensor_tensor(
                        ao_hq[:, :, hi, :], pv, dvb, ALU.mult),
                        waits=(epi_wl if hi == 0 else []),
                        incs=([(A["aEPI"], 1)] if hi == 3 else []))
                epin = A["aEPI"].n
                mk["epi"] = epin
                eb.op("scalar", lambda E: E.activation(
                    scr1[:], ao_hq[:].rearrange("p a b c -> p (a b c)"), AF.Square),
                    waits=[(A["aEPI"], epin), (A["aEPV"], mk["epv"]),
                           (A["aSPD"], mk["sp_slice"].get(k - 1, 0))],
                    incs=[(A["aEPA"], 1)])
                mk["epa"] = A["aEPA"].n
                vcm = S["sVC"].n

                def epi_dve(E, epan=mk["epa"], vcm=vcm):
                    E.wait_ge(A["aEPA"].h, epan)
                    go = small_chain(E, S["sVC"], vcm)
                    go(lambda: E.reduce_sum(
                        est[:, 0:16].unsqueeze(2),
                        scr1[:].rearrange("p (g d) -> p g d", d=64), AX.X))
                    go(lambda: E.tensor_scalar(est[:, 16:32], est[:, 0:16],
                                               1.0 / 64, 1e-6, ALU.mult, ALU.add))
                    I32 = mybir.dt.int32
                    t1 = est[:, 32:48]
                    t2 = est[:, 48:64]
                    oi = t1.bitcast(I32)
                    go(lambda: E.tensor_scalar(oi, est[:, 16:32].bitcast(I32), 23,
                                               None, ALU.arith_shift_right))
                    go(lambda: E.tensor_scalar(oi, oi, -1, None, ALU.bitwise_xor))
                    go(lambda: E.tensor_scalar(oi, oi, 382, None, ALU.add))
                    go(lambda: E.tensor_scalar(oi, oi, 1, None, ALU.arith_shift_right))
                    go(lambda: E.tensor_scalar(oi, oi, 23, None, ALU.arith_shift_left))
                    for _ in range(5):
                        go(lambda: E.tensor_tensor(t2, t1, t1, ALU.mult))
                        go(lambda: E.tensor_tensor(t2, t2, est[:, 16:32], ALU.mult))
                        go(lambda: E.tensor_scalar(t2, t2, -0.5, 1.5, ALU.mult,
                                                   ALU.add))
                        go(lambda: E.tensor_tensor(t1, t1, t2, ALU.mult))
                    i = go(lambda: E.tensor_tensor(
                        ao_hq[:].rearrange("p ts h d -> p (ts h) d"),
                        ao_hq[:].rearrange("p ts h d -> p (ts h) d"),
                        est[:, 32:48].rearrange("p (ts h) -> p ts h", ts=4)
                        .rearrange("p ts h -> p (ts h)").unsqueeze(2)
                        .broadcast_to([128, 16, 64]), ALU.mult), inc=False)
                    i.then_inc(A["aEPV"].h, 1)
                eb.q["vector"].append(epi_dve)
                S["sVC"].n += 27
                A["aEPV"].n += 1
                mk["epv"] = A["aEPV"].n
                rmap8 = lambda j: (j % 2) * 4 + j // 2
                tpw0 = [(A["aEPV"], mk["epv"]), (A["aWO"], mk["wo_slice"].get(k - 1, 0))]
                for j in range(8):
                    ts, jj = divmod(j, 2)
                    jc = hq * 2 + jj
                    r = rmap8(j)
                    wl = list(tpw0) if j == 0 else []
                    if j >= 2:
                        wl.append((A["aTE"], mk["te"] + j - 1))
                    eb.op("tensor", lambda E, ts=ts, jj=jj, r=r: E.transpose(
                        pav[:, r * 128:(r + 1) * 128],
                        ao_hq[:, ts, jj * 2:jj * 2 + 2, :].rearrange(
                            "p h d -> p (h d)"),
                        P["sidF"][:]),
                        waits=wl, incs=[(A["aTP"], 1)])
                    eb.op("scalar", lambda E, ts=ts, jc=jc, r=r: E.activation(
                        ao_fm[:, jc, ts * 128:(ts + 1) * 128],
                        pav[:, r * 128:(r + 1) * 128], AF.Identity),
                        waits=[(A["aTP"], A["aTP"].n)], incs=[(A["aTE"], 1)])
                mk["tp"] = A["aTP"].n
                mk["te"] = A["aTE"].n
                yield 8.0
            evall = A["aEV"].n
            wo0 = A["aWO"].n
            for g in range(8):
                ts, ih = divmod(g, 2)
                bank = g % 2
                wl = []
                if g == 0:
                    wl = [(A["aEV"], evall), (A["aTE"], mk["te"]),
                          (A["aEPV"], mk["epv"])]
                if g >= 2:
                    wl.append((A["aWOE"], wo0 // 1 + g - 1 if False else
                               A["aWOE"].n - 1))
                for jc in range(8):
                    eb.op("tensor", lambda E, jc=jc, ts=ts, ih=ih, bank=bank:
                          E.matmul(psim[bank][:],
                                   ao_fm[:, jc, ts * 128:(ts + 1) * 128],
                                   P["WoTb"][:, jc, ih * 512:(ih + 1) * 512],
                                   start=(jc == 0), stop=(jc == 7)),
                          waits=(wl if jc == 0 else []),
                          incs=([(A["aWO"], 1)] if jc == 7 else []))
                won = A["aWO"].n
                spp = A["aSP0"] if g % 2 == 0 else A["aSP1"]
                wle = [(A["aWO"], won)]
                if g >= 2:
                    wle.append((spp, spp.n))
                eb.op("scalar", lambda E, bank=bank, g=g: E.activation(
                    scr1[:, (g % 2) * 512:(g % 2) * 512 + 512], psim[bank][:],
                    AF.Identity),
                    waits=wle, incs=[(A["aWOE"], 1)])
                woen = A["aWOE"].n
                r0 = t0 + ts * 128
                eb.op("scalar", lambda E, r0=r0, ih=ih, g=g: E.dma_start(
                    sS[r0:r0 + 128, ih * 512:(ih + 1) * 512],
                    scr1[:, (g % 2) * 512:(g % 2) * 512 + 512]),
                    waits=[(A["aWOE"], woen)], incs=[(spp, 16)])
            sp0n, sp1n = A["aSP0"].n, A["aSP1"].n

            def s_echo(E, sp0n=sp0n, sp1n=sp1n):
                E.wait_ge(A["aSP0"].h, sp0n)
                E.wait_ge(A["aSP1"].h, sp1n)
                i = E.copy(est[:, 0:1], est[:, 0:1])
                i.then_inc(A["aSPD"].h, 1)
            eb.q["scalar"].append(s_echo)
            A["aSPD"].n += 1
            mk["wo_slice"][k] = A["aWO"].n
            mk["woe_slice"][k] = A["aWOE"].n
            mk["sim_slice"][k] = A["aSIM"].n
            mk["sp_slice"][k] = A["aSPD"].n
            yield 6.0

        FWE = 48

        def ffn_tt(tt):
            xb = tt % 2
            fxp = A["fXN0"] if xb == 0 else A["fXN1"]
            wl = [(S[n], v) for n, v in P1B]
            if tt >= 2:
                wl.append((A["fPG"], mk["ffn_mm_tt"][tt - 2][0]))
                wl.append((A["fPU"], mk["ffn_mm_tt"][tt - 2][1]))
            eb.op("sync", lambda E, xb=xb, tt=tt: E.dma_start(
                xtile[xb][:],
                xnf.rearrange("c p t -> p c t")[:, :, tt * 512:(tt + 1) * 512]),
                waits=wl, incs=[(fxp, 16)])
            wlp = [(S["sPFM"], 16 * NCH_L)]
            if tt >= 1:
                wlp.append((A["fBI"], A["fBI"].n))
            eb.op("sync", lambda E, tt=tt: E.dma_start(
                pfm_sb[:].rearrange("e (a t) -> e a t", a=4),
                tens["pfm"][:, tt * 4:tt * 4 + 4, :]),
                waits=wlp, incs=[(A["fPF"], 16)])
            yield 2.0
            for e in range(NE):
                wl = [(S["sFW"], S["sFW"].n)]
                if not (tt == 0 and e == 0):
                    wl.append((A["fPY"], A["fPY"].n))
                eb.op("sync", lambda E, e=e: E.dma_start(
                    Dsb[:], dwb[e].rearrange("(f p) d -> p f d", p=128)),
                    waits=wl, incs=[(A["fD"], 16)])
                fdn = A["fD"].n
                gu_marks = {}
                FWALL = S["sFW"].n
                for fcp in range(2):
                    gs = A["fG0"] if fcp % 2 == 0 else A["fG1"]
                    us = A["fU0"] if fcp % 2 == 0 else A["fU1"]
                    wl = [(S["sFW"], FWALL)]
                    wlu = []
                    if not (tt == 0 and e == 0 and fcp < 2):
                        wl.append((A["fPG"], A["fPG"].n))
                        wlu.append((A["fPU"], A["fPU"].n))
                    eb.op("sync", lambda E, e=e, fcp=fcp: E.dma_start(
                        gch[fcp % 2][:],
                        gwb[e].rearrange("(c p) f -> p c f", p=128)[
                            :, :, fcp * 256:(fcp + 1) * 256]),
                        waits=wl, incs=[(gs, 16)])
                    eb.op("sync", lambda E, e=e, fcp=fcp: E.dma_start(
                        uch[fcp % 2][:],
                        uwb[e].rearrange("(c p) f -> p c f", p=128)[
                            :, :, fcp * 256:(fcp + 1) * 256]),
                        waits=wlu, incs=[(us, 16)])
                    gu_marks[fcp] = (gs, gs.n, us, us.n)
                for fc in range(16):
                    fcp = fc // 2
                    if fc % 2 == 0 and fc >= 2 and fcp + 1 < 8:
                        gs = A["fG0"] if (fcp + 1) % 2 == 0 else A["fG1"]
                        us = A["fU0"] if (fcp + 1) % 2 == 0 else A["fU1"]
                        eb.op("sync", lambda E, e=e, fcp=fcp: E.dma_start(
                            gch[(fcp + 1) % 2][:],
                            gwb[e].rearrange("(c p) f -> p c f", p=128)[
                                :, :, (fcp + 1) * 256:(fcp + 2) * 256]),
                            waits=[(A["fPG"], A["fPG"].n)], incs=[(gs, 16)])
                        eb.op("sync", lambda E, e=e, fcp=fcp: E.dma_start(
                            uch[(fcp + 1) % 2][:],
                            uwb[e].rearrange("(c p) f -> p c f", p=128)[
                                :, :, (fcp + 1) * 256:(fcp + 2) * 256]),
                            waits=[(A["fPU"], A["fPU"].n)], incs=[(us, 16)])
                        gu_marks[fcp + 1] = (gs, gs.n, us, us.n)
                    pgb = pg if fc % 2 == 0 else py[0]
                    pub = pu if fc % 2 == 0 else py[1]
                    for dc in range(8):
                        wl = []
                        if dc == 0:
                            wl = [(gu_marks[fcp][0], gu_marks[fcp][1]),
                                  (fxp, 16 * (tt // 2 + 1)),
                                  (A["fSIL"], max(0, A["fSIL"].n - 1))]
                            if fc == 1:
                                wl.append((A["fYE"], A["fYE"].n))
                        eb.op("tensor", lambda E, dc=dc, fc=fc, fcp=fcp, xb=xb,
                              pgb=pgb: E.matmul(pgb[:],
                                       gch[fcp % 2][:, dc, (fc % 2) * 128:(fc % 2) * 128 + 128],
                                       xtile[xb][:, dc, :],
                                       start=(dc == 0), stop=(dc == 7)),
                              waits=wl, incs=([(A["fPG"], 1)] if dc == 7 else []))
                    pgn = A["fPG"].n
                    wl = [(A["fPG"], pgn), (A["fH"], max(0, A["fH"].n - 1))]
                    eb.op("scalar", lambda E, e=e, fc=fc, pgb=pgb: E.activation(
                        hs_scr[fc % 2][:], pgb[:], AF.Silu,
                        bias=P["gbs"][:, e * 16 + fc:e * 16 + fc + 1]),
                        waits=wl, incs=[(A["fSIL"], 1)])
                    siln = A["fSIL"].n
                    for dc in range(8):
                        wl = []
                        if dc == 0:
                            wl = [(gu_marks[fcp][2], gu_marks[fcp][3]),
                                  (A["fH"], max(0, A["fH"].n - 1))]
                        eb.op("tensor", lambda E, dc=dc, fc=fc, fcp=fcp, xb=xb,
                              pub=pub: E.matmul(pub[:],
                                       uch[fcp % 2][:, dc, (fc % 2) * 128:(fc % 2) * 128 + 128],
                                       xtile[xb][:, dc, :],
                                       start=(dc == 0), stop=(dc == 7)),
                              waits=wl, incs=([(A["fPU"], 1)] if dc == 7 else []))
                    pun = A["fPU"].n
                    wl = [(A["fPU"], pun), (A["fSIL"], siln)]
                    if tt > 0 or e > 0:
                        wl.append((A["fPY"], A["fPY"].n))
                    eb.op("vector", lambda E, e=e, fc=fc, pub=pub:
                          E.scalar_tensor_tensor(
                              hsb[:, fc, :], pub[:],
                              P["ubs"][:, e * 16 + fc:e * 16 + fc + 1],
                              hs_scr[fc % 2][:], ALU.add, ALU.mult),
                          waits=wl, incs=[(A["fH"], 1)])
                    if fc % 2 == 1:
                        yield 3.5
                fhn = A["fH"].n
                if e == 0:
                    for g8 in range(8):
                        t128, dh = divmod(g8, 2)
                        ct = tt * 4 + t128
                        bank = g8 % 2
                        wl = []
                        if g8 == 0:
                            wl = [(A["fYE"], mk["ye_tt"].get(tt - 1, 0)),
                                  (A["fPF"], A["fPF"].n),
                                  (A["fH"], fhn)]
                        if g8 >= 2:
                            wl.append((A["fBIC"], A["fBIC"].n - 1))
                        eb.op("tensor", lambda E, t128=t128, dh=dh, bank=bank: E.matmul(
                            py[bank][:], pfm_sb[0:8, t128 * 128:(t128 + 1) * 128],
                            P["dbs"][0:8, dh * 512:(dh + 1) * 512],
                            start=True, stop=True),
                            waits=wl, incs=[(A["fBI"], 1)])
                        bin_ = A["fBI"].n
                        wl2 = [(A["fBI"], bin_)]
                        if g8 == 0:
                            wl2.append((A["fGT"], mk["gt_tt"].get(tt - 1, 0)))
                        eb.op("vector", lambda E, t128=t128, dh=dh, bank=bank: E.tensor_copy(
                            y_acc[:, t128, dh * 512:(dh + 1) * 512], py[bank][:]),
                            waits=wl2, incs=[(A["fBIC"], 1)])
                    bic_all = A["fBIC"].n
                for g8 in range(8):
                    t128, dh = divmod(g8, 2)
                    ct = tt * 4 + t128
                    bank = g8 % 2
                    for fc in range(16):
                        wl = []
                        if fc == 0:
                            wl = [(A["fH"], fhn), (A["fD"], fdn),
                                  (A["fYE"], max(0, A["fYE"].n - 1))]
                            if e == 0:
                                wl.append((A["fBIC"], bic_all))
                        eb.op("tensor", lambda E, fc=fc, t128=t128, dh=dh,
                              bank=bank: E.matmul(
                                  py[bank][:],
                                  hsb[:, fc, t128 * 128:(t128 + 1) * 128],
                                  Dsb[:, fc, dh * 512:(dh + 1) * 512],
                                  start=(fc == 0), stop=(fc == 15)),
                              waits=wl, incs=([(A["fPY"], 1)] if fc == 15 else []))
                    pyn = A["fPY"].n
                    eb.op("vector", lambda E, t128=t128, dh=dh, bank=bank, ct=ct,
                          e=e: E.scalar_tensor_tensor(
                              y_acc[:, t128, dh * 512:(dh + 1) * 512],
                              py[bank][:], P["probs_res"][:, ct, e:e + 1],
                              y_acc[:, t128, dh * 512:(dh + 1) * 512],
                              ALU.mult, ALU.add),
                          waits=[(A["fPY"], pyn)], incs=[(A["fYE"], 1)])
                    yield 3.5
            mk["ffn_mm_tt"][tt] = (A["fPG"].n, A["fPU"].n)
            mk["ye_tt"][tt] = A["fYE"].n

        def ffn_tail(tt):
            yeall = mk["ye_tt"][tt]
            fhn = A["fH"].n
            assert tt in mk["sp_slice"], f"attn slice {tt} must precede ffn tail"
            spm = mk["sp_slice"][tt]
            for i in range(8):
                t128, i2 = divmod(i, 2)
                r0 = tt * 512 + t128 * 128
                cs = i2 * 512
                wl = [(A["fYE"], yeall), (A["fCT"], max(0, A["fCT"].n - 1))]
                if i == 0:
                    wl.append((A["fH"], fhn))
                eb.op("scalar", lambda E, t128=t128, cs=cs, i=i: E.activation(
                    hs_scr[i % 2][:], y_acc[:, t128, cs:cs + 512], AF.Silu),
                    waits=wl, incs=[(A["fGT"], 1)])
                gtn = A["fGT"].n
                fxx = A["fXSx0"] if i % 2 == 0 else A["fXSx1"]
                fxss = A["fXSs0"] if i % 2 == 0 else A["fXSs1"]
                wl = [(A["fCT"], max(0, A["fCT"].n - 1))]
                eb.op("sync", lambda E, r0=r0, cs=cs, i=i: E.dma_start(
                    xts[i % 2][:], x[r0:r0 + 128, cs:cs + 512]),
                    waits=wl, incs=[(fxx, 16)])
                fop = A["fOUT0"] if i % 2 == 0 else A["fOUT1"]
                eb.op("sync", lambda E, r0=r0, cs=cs, i=i: E.dma_start(
                    sts[i % 2][:], sS[r0:r0 + 128, cs:cs + 512]),
                    waits=[(A["aSPD"], spm), (A["fCT"], max(0, A["fCT"].n - 1)),
                           (fop, fop.n)],
                    incs=[(fxss, 16)])
                xsn = (fxx, fxx.n, fxss, fxss.n)
                vcc = S["sVC"].n

                def comb(E, i=i, gtn=gtn, xsn=xsn, vcc=vcc):
                    E.wait_ge(A["fGT"].h, gtn)
                    E.wait_ge(xsn[0].h, xsn[1])
                    E.wait_ge(xsn[2].h, xsn[3])
                    go = small_chain(E, S["sVC"], vcc)
                    go(lambda: E.tensor_tensor(sts[i % 2][:], sts[i % 2][:],
                                               hs_scr[i % 2][:], ALU.mult))
                    ii = go(lambda: E.tensor_tensor(sts[i % 2][:], sts[i % 2][:],
                                                    xts[i % 2][:], ALU.add),
                            inc=False)
                    ii.then_inc(A["fCT"].h, 1)
                eb.q["vector"].append(comb)
                S["sVC"].n += 1
                A["fCT"].n += 1
                ctn = A["fCT"].n
                eb.op("scalar", lambda E, r0=r0, cs=cs, i=i: E.dma_start(
                    out[r0:r0 + 128, cs:cs + 512], sts[i % 2][:]),
                    waits=[(A["fCT"], ctn)], incs=[(fop, 16)])
                yield 2.0
            mk["ct_tt"][tt] = A["fCT"].n
            mk["gt_tt"][tt] = A["fGT"].n

        only_attn = (phases == 2)
        for k in (SLICES if SLICES is not None else range(NSLICE)):
            a = attn_slice(k)
            f = None if only_attn else ffn_tt(k)
            a_done = False
            f_done = only_attn
            ta = tf = 0.0
            while not (a_done and f_done):
                if not a_done and (f_done or ta <= tf):
                    try:
                        ta += next(a)
                    except StopIteration:
                        a_done = True
                else:
                    try:
                        tf += next(f)
                    except StopIteration:
                        f_done = True
            if not only_attn:
                for _ in ffn_tail(k):
                    pass

        fins = [(A["fOUT0"], A["fOUT0"].n), (A["fOUT1"], A["fOUT1"].n),
                (A["aSP0"], A["aSP0"].n), (A["aSP1"], A["aSP1"].n)]

        def fin(E, fins=fins):
            for s, v in fins:
                if v:
                    E.wait_ge(s.h, v)
        eb.q["sync"].append(fin)
        eb.emit(nc)


_BF = ml_dtypes.bfloat16


def host_inputs(inputs):
    f32 = np.float32
    W = {}
    W["wqT"] = np.ascontiguousarray(inputs["L0_attn_Wq"].T.astype(f32, copy=False))
    W["wkT"] = np.ascontiguousarray(inputs["L0_attn_Wk"].T.astype(f32, copy=False))
    W["wvT"] = np.ascontiguousarray(inputs["L0_attn_Wv"].T.astype(f32, copy=False))
    W["woT"] = np.ascontiguousarray(inputs["L0_attn_Wo"].T.astype(f32, copy=False))
    W["rwT"] = np.ascontiguousarray(inputs["L0_ffn_router_w"].T.astype(f32, copy=False))
    W["polyT"] = np.ascontiguousarray(
        inputs["L0_attn_poly_weights"].T.astype(f32, copy=False))
    W["hdrow"] = np.ascontiguousarray(
        inputs["L0_attn_head_degrees"].reshape(1, 16).astype(f32, copy=False))
    nw = inputs["L0_attn_norm_w"].astype(f32, copy=False)
    W["nw2"] = np.ascontiguousarray(np.concatenate([nw, nw]).reshape(128, 1))
    W["gw"] = np.ascontiguousarray(inputs["L0_ffn_gate_w"].astype(f32, copy=False))
    W["uw"] = np.ascontiguousarray(inputs["L0_ffn_up_w"].astype(f32, copy=False))
    W["dw"] = np.ascontiguousarray(inputs["L0_ffn_down_w"].astype(f32, copy=False))
    W["gbl"] = np.ascontiguousarray(
        inputs["L0_ffn_gate_b"].reshape(NE, 16, 128).transpose(2, 0, 1)
        .reshape(128, 128).astype(f32, copy=False))
    W["ubl"] = np.ascontiguousarray(
        inputs["L0_ffn_up_b"].reshape(NE, 16, 128).transpose(2, 0, 1)
        .reshape(128, 128).astype(f32, copy=False))
    W["db"] = np.ascontiguousarray(inputs["L0_ffn_down_b"].astype(f32, copy=False))
    W["identF"] = np.eye(128, dtype=f32)
    W["identB"] = np.eye(128, dtype=_BF)
    W["rmat"] = gegen_monomials()
    W["kcol"] = np.arange(9, dtype=f32).reshape(9, 1)
    p = np.arange(128, dtype=np.float64)
    j = np.arange(16, dtype=np.float64)
    W["dvl"] = (1.0 / (j[None, :] * 128 + p[:, None] + 1.0)).astype(f32)
    dm = np.zeros((128, 2, 2, 512), np.float32)
    for v in range(2):
        for kcw in range(2):
            thr = (2 * v + kcw) * 128 + p[:, None]
            dm[:, v, kcw, :] = (np.arange(512)[None, :] >= thr)
    W["dmc"] = np.ascontiguousarray(dm.reshape(128, 2048).astype(_BF))
    return W


def kernel(**inputs):
    nc, _ = build_nc(debug=DEBUG, phases=BUILD_PHASES)
    W = host_inputs(inputs)
    xf = inputs["L0_attn_x"].astype(np.float32, copy=False)
    in_maps = []
    for c in range(NCORE):
        m = dict(W)
        m["x"] = np.ascontiguousarray(xf[c * 4:(c + 1) * 4].reshape(TOK, DD))
        in_maps.append(m)
    res = run_bass_kernel_spmd(nc, in_maps, core_ids=list(range(NCORE)))
    outs = [np.asarray(r["out"]) for r in res.results]
    return np.concatenate(outs, axis=0).reshape(32, 2048, 1024)
